# version 04
# baseline (speedup 1.0000x reference)
import numpy as np
import ml_dtypes
from contextlib import ExitStack

import concourse.bass as bass
import concourse.mybir as mybir
from concourse.bass_utils import run_bass_kernel_spmd

F32 = mybir.dt.float32
BF16 = mybir.dt.bfloat16
AF = mybir.ActivationFunctionType
ALU = mybir.AluOpType
AX = mybir.AxisListType
BF = ml_dtypes.bfloat16

T, D = 2048, 1024
TCH = T // 128
DCH = D // 128
NCORE = 8
GAMMA = 0.96875

GROUPS = [[0, 1, 2, 3, 4, 5, 6, 7]]


def _host_consts():
    out = {}
    out["ident"] = np.eye(128, dtype=BF)
    A = np.zeros((T, T), np.float64)
    A[0, 1] = 1.0
    for t in range(1, T):
        j = np.arange(1, t + 1)
        A[t, j] = GAMMA ** (t - j)
    A_cat = np.concatenate([A[0::2].T, A[1::2].T], axis=1)
    out["a_cat"] = np.ascontiguousarray(A_cat.astype(BF))
    out["triu"] = np.ascontiguousarray(np.triu(np.ones((128, 128))).astype(BF))
    return out


def _prep_weights(inp):
    f = lambda a: np.ascontiguousarray(np.asarray(a, np.float32))
    tr = lambda a: np.ascontiguousarray(np.asarray(a, np.float32).T)
    p = {}
    p["l0a_wq"] = f(inp["L0_attn_a_Wq"])
    p["l0a_wk"] = f(inp["L0_attn_a_Wk"])
    p["l0a_wv"] = f(inp["L0_attn_a_Wv"])
    gw = np.asarray(inp["L0_ffn_gate_w"], np.float32)
    p["l0f_gwt"] = np.ascontiguousarray(gw.T.reshape(DCH, 128, 12).transpose(1, 0, 2))
    p["l0f_eb"] = np.ascontiguousarray(np.asarray(inp["L0_ffn_expert_bias"], np.float32)[None, :])
    p["l0f_w1t"] = np.ascontiguousarray(np.asarray(inp["L0_ffn_w1"], np.float32).transpose(0, 2, 1))
    p["l0f_w2t"] = np.ascontiguousarray(np.asarray(inp["L0_ffn_w2"], np.float32).transpose(0, 2, 1))
    p["l0b_wqkv"] = f(inp["L0_attn_b_Wqkv"])
    bqkv = np.asarray(inp["L0_attn_b_bqkv"], np.float32)
    p["l0b_bqkv"] = np.ascontiguousarray(bqkv[None, :])
    p["l0b_wout"] = f(inp["L0_attn_b_Wout"])
    p["l0b_bout"] = np.ascontiguousarray(np.asarray(inp["L0_attn_b_bout"], np.float32)[None, :])
    p["l0b_bq_cols"] = np.ascontiguousarray(bqkv[:D].reshape(DCH, 128).T)
    p["l1a_wqkvt"] = tr(inp["L1_attn_a_Wqkv"])
    p["l1a_bqkv"] = np.ascontiguousarray(np.asarray(inp["L1_attn_a_bqkv"], np.float32)[None, :])
    p["l1a_wprojt"] = tr(inp["L1_attn_a_Wproj"])
    p["l1a_bproj"] = np.ascontiguousarray(np.asarray(inp["L1_attn_a_bproj"], np.float32)[None, :])
    p["l1a_scale_col"] = np.ascontiguousarray(
        np.broadcast_to(np.asarray(inp["L1_attn_a_scale"], np.float32).reshape(1, 1), (128, 1)))
    wr = np.asarray(inp["L1_ffn_Wr"], np.float32)
    p["l1f_wrt"] = np.ascontiguousarray(wr.reshape(DCH, 128, 8).transpose(1, 0, 2))
    p["l1f_br"] = np.ascontiguousarray(np.asarray(inp["L1_ffn_br"], np.float32)[None, :])
    b1 = np.asarray(inp["L1_ffn_b1"], np.float32)
    p["l1f_b1_cols"] = np.ascontiguousarray(
        b1.reshape(8, 32, 128).transpose(2, 0, 1).reshape(128, 8 * 32))
    p["l1f_b2"] = np.ascontiguousarray(np.asarray(inp["L1_ffn_b2"], np.float32))
    p["l1f_w1"] = np.ascontiguousarray(np.asarray(inp["L1_ffn_W1"], np.float32))
    p["l1f_w2"] = np.ascontiguousarray(np.asarray(inp["L1_ffn_W2"], np.float32))
    p["l1b_wqt"] = tr(inp["L1_attn_b_Wq"])
    p["l1b_wkt"] = tr(inp["L1_attn_b_Wk"])
    p["l1b_wvt"] = tr(inp["L1_attn_b_Wv"])
    p["l1b_wot"] = tr(inp["L1_attn_b_Wo"])
    p["l1b_bq_cols"] = np.ascontiguousarray(
        np.asarray(inp["L1_attn_b_bq"], np.float32).reshape(DCH, 128).T)
    p["l1b_bk_cols"] = np.ascontiguousarray(
        np.asarray(inp["L1_attn_b_bk"], np.float32).reshape(DCH, 128).T)
    p["l1b_bv"] = np.ascontiguousarray(np.asarray(inp["L1_attn_b_bv"], np.float32)[None, :])
    p["l1b_bo"] = np.ascontiguousarray(np.asarray(inp["L1_attn_b_bo"], np.float32)[None, :])
    p["l1b_ascale_col"] = np.ascontiguousarray(
        np.broadcast_to(np.asarray(inp["L1_attn_b_attn_scale"], np.float32).reshape(1, 1), (128, 1)))
    pgrid = np.arange(128)[:, None]
    cgrid = np.arange(512)[None, :]
    m = np.zeros((128, 4, 512), np.float32)
    for o in range(4):
        ktv = 128 * o + pgrid
        keep = ~((ktv > cgrid) & ((ktv // 64) == (cgrid // 64)))
        m[:, o, :] = keep.astype(np.float32)
    p["l1b_masks"] = m.astype(BF)
    hi = np.zeros((128, DCH, 16), np.float32)
    for dc in range(DCH):
        for pp in range(128):
            hi[pp, dc, dc * 2 + pp // 64] = 1.0
    p["l1b_hind"] = hi.astype(BF)
    return p


class Sem:
    def __init__(self, h):
        self.h = h
        self.v = 0

    def inc(self, inst, n=1):
        inst.then_inc(self.h, n)
        self.v += n
        return self.v


class Ctx:

    def __init__(self, nc, ex):
        self.nc = nc
        self.ex = ex
        self.sb = lambda name, shape, dt: ex.enter_context(nc.sbuf_tensor(name, shape, dt))
        self.psum = lambda ex_, name, shape, dt=F32: ex_.enter_context(nc.psum_tensor(name, shape, dt))
        mksem = lambda name: Sem(ex.enter_context(nc.semaphore(name)))
        self.sW = mksem("sW")
        self.sX = mksem("sX")
        self.sMM = mksem("sMM")
        self.sV = mksem("sV")
        self.sS = mksem("sS")
        self.sG = mksem("sG")
        self.sCC = mksem("sCC")

        self.xres = self.sb("xres", [128, TCH, D], F32)
        self.xlnT = self.sb("xlnT", [128, DCH, T], BF16)
        self.ones = self.sb("ones", [128, 128], BF16)
        self.ident = self.sb("ident", [128, 128], BF16)
        self.triu = self.sb("triu", [128, 128], BF16)
        self.mu = self.sb("mu", [128, TCH], F32)
        self.rstd = self.sb("rstd", [128, TCH], F32)
        self.acc = self.sb("acc", [128, TCH], F32)
        self.s1 = self.sb("s1", [128, TCH], F32)
        self.var = self.sb("var", [128, TCH], F32)
        self.sq_scr = self.sb("sq_scr", [128, D], BF16)
        self.xln_rot = [self.sb(f"xln_rot{i}", [128, D], BF16) for i in range(2)]

        self.stream_deps = []
        self.fence_deps = []
        self.ln_evac = []

    def barrier_tokens(self):
        return [(s, s.v) for s in (self.sMM, self.sV, self.sS, self.sX, self.sW, self.sG)]


def _wait(eng, deps):
    for (sem, val) in deps:
        if val > 0:
            eng.wait_ge(sem.h, val)


def emit_ln(c, tag):
    nc = c.nc
    deps = list(c.stream_deps)
    tok_acc = c.sS.v + 1

    with nc.Block(f"ln{tag}_stats", no_gpsimd_drain=True) as b:
        @b.scalar
        def _(se):
            _wait(se, deps)
            ins = None
            for i in range(TCH):
                ins = se.activation(c.sq_scr[:], c.xres[:, i, :], AF.Square,
                                    accum_out=c.acc[:, i:i + 1])
            assert c.sS.inc(ins) == tok_acc

        @b.vector
        def _(ve):
            _wait(ve, deps)
            for i in range(TCH):
                ve.reduce_sum(c.s1[:, i:i + 1], c.xres[:, i, :], AX.X)
            ve.drain()
            ve.wait_ge(c.sS.h, tok_acc)
            ve.tensor_scalar_mul(c.mu[:], c.s1[:], 1.0 / D)
            ve.drain()
            ve.scalar_tensor_tensor(c.var[:], c.mu[:], -1.0, c.mu[:], ALU.mult, ALU.mult)
            ve.drain()
            ve.scalar_tensor_tensor(c.var[:], c.acc[:], 1.0 / D, c.var[:], ALU.mult, ALU.add)
            ve.drain()
            ve.tensor_scalar_add(c.var[:], c.var[:], 1e-5)
            ve.drain()
            ins = ve.reciprocal(c.var[:], c.var[:])
            c.sV.inc(ins)

    tok_var = c.sV.v
    with nc.Block(f"ln{tag}_rstd", no_gpsimd_drain=True) as b:
        @b.scalar
        def _(se):
            se.wait_ge(c.sV.h, tok_var)
            se.drain()
            ins = se.activation(c.rstd[:], c.var[:], AF.Sqrt)
            c.sS.inc(ins)

    tok_rstd = c.sS.v
    c.tok_stats = (c.sS, tok_rstd)

    ex = ExitStack()
    ps_tr = [c.psum(ex, f"ps_tr_{tag}_{i}", [128, D], BF16) for i in range(2)]
    sV0, sMM0, sS0 = c.sV.v, c.sMM.v, c.sS.v
    xln_tok = [sV0 + i + 1 for i in range(TCH)]
    tr_tok = [sMM0 + i + 1 for i in range(TCH)]
    evac_tok = [sS0 + i + 1 for i in range(TCH)]

    with nc.Block(f"ln{tag}_tr", no_gpsimd_drain=True) as b:
        @b.vector
        def _(ve):
            ve.wait_ge(c.sS.h, tok_rstd)
            for i in range(TCH):
                if i >= 2:
                    ve.wait_ge(c.sMM.h, tr_tok[i - 2])
                ins = ve.tensor_scalar(c.xln_rot[i % 2][:], c.xres[:, i, :],
                                       c.mu[:, i:i + 1], c.rstd[:, i:i + 1],
                                       ALU.subtract, ALU.mult)
                assert c.sV.inc(ins) == xln_tok[i]

        @b.tensor
        def _(te):
            for i in range(TCH):
                te.wait_ge(c.sV.h, xln_tok[i])
                if i >= 2:
                    te.wait_ge(c.sS.h, evac_tok[i - 2])
                ins = None
                for cc_ in range(DCH):
                    ins = te.transpose(ps_tr[i % 2][:, cc_ * 128:(cc_ + 1) * 128],
                                       c.xln_rot[i % 2][:, cc_ * 128:(cc_ + 1) * 128],
                                       c.ident[:])
                assert c.sMM.inc(ins) == tr_tok[i]

        @b.scalar
        def _(se):
            for i in range(TCH):
                se.wait_ge(c.sMM.h, tr_tok[i])
                src = ps_tr[i % 2][:].rearrange("p (c q) -> p c q", c=DCH)
                ins = se.copy(out=c.xlnT[:, :, i * 128:(i + 1) * 128], in_=src)
                assert c.sS.inc(ins) == evac_tok[i]

    ex.close()
    c.ln_evac = evac_tok
    c.tok_xlnT = (c.sS, c.sS.v)


L0A_PHASES = 4

def emit_l0a(c, w, dram):
    nc = c.nc
    emit_ln(c, "s0")
    if L0A_PHASES < 1:
        return
    fence = list(c.fence_deps)
    ex = ExitStack()
    sb = lambda name, shape, dt: ex.enter_context(nc.sbuf_tensor(name, shape, dt))
    q_tm = sb("l0a_q", [128, TCH, D], BF16)
    exw = ExitStack()
    sbw = lambda name, shape, dt: exw.enter_context(nc.sbuf_tensor(name, shape, dt))
    wq_sb = sbw("l0a_wq", [128, DCH, D], BF16)
    wk_sb = sbw("l0a_wk", [128, DCH, D], BF16)
    wv_sb = sbw("l0a_wv", [128, DCH, D], BF16)
    k_rot = [sbw(f"l0a_k{i}", [128, D], BF16) for i in range(2)]
    v_rot = [sbw(f"l0a_v{i}", [128, D], BF16) for i in range(2)]
    m_rot = [sbw(f"l0a_m{i}", [128, D], F32) for i in range(2)]
    vs = sbw("l0a_vs", [128, TCH], F32)

    wtok = {}
    with nc.Block("l0a_wload", no_gpsimd_drain=True) as b:
        @b.gpsimd
        def _(ge):
            _wait(ge, fence)
            for name, dst in (("l0a_wk", wk_sb), ("l0a_wv", wv_sb), ("l0a_wq", wq_sb)):
                src = w[name].rearrange("(kc p) e -> p kc e", p=128)
                for kc in range(DCH):
                    ins = ge.dma_start(out=dst[:, kc, :], in_=src[:, kc, :])
                    c.sW.inc(ins, 16)
                wtok[name] = c.sW.v

    if L0A_PHASES < 2:
        exw.close(); ex.close()
        return
    exq = ExitStack()
    ps = [c.psum(exq, f"l0a_ps{g}", [128, 512]) for g in range(6)]
    sMM0, sS0, sV0, sX0 = c.sMM.v, c.sS.v, c.sV.v, c.sX.v
    mm_tok = lambda i, g: sMM0 + i * 6 + g + 1
    ev_tok = lambda i, g: sS0 + i * 6 + g + 1
    v_tok = lambda i: sV0 + 2 * i + 2
    x_tok = lambda i: sX0 + (i + 1) * 16

    tok_xlnT_all = c.tok_xlnT[1]
    with nc.Block("l0a_proj", no_gpsimd_drain=True) as b:
        @b.tensor
        def _(te):
            te.wait_ge(c.sW.h, wtok["l0a_wq"])
            te.wait_ge(c.sS.h, tok_xlnT_all)
            for i in range(TCH):
                te.wait_ge(c.sS.h, c.ln_evac[i])
                for g, (wsb, e0) in enumerate(((wk_sb, 0), (wk_sb, 512), (wv_sb, 0),
                                               (wv_sb, 512), (wq_sb, 0), (wq_sb, 512))):
                    if i >= 1:
                        te.wait_ge(c.sS.h, ev_tok(i - 1, g))
                    ins = None
                    for kc in range(DCH):
                        ins = te.matmul(ps[g][:], c.xlnT[:, kc, i * 128:(i + 1) * 128],
                                        wsb[:, kc, e0:e0 + 512],
                                        start=(kc == 0), stop=(kc == DCH - 1))
                    assert c.sMM.inc(ins) == mm_tok(i, g)

        @b.scalar
        def _(se):
            for i in range(TCH):
                for g in range(6):
                    se.wait_ge(c.sMM.h, mm_tok(i, g))
                    if g < 2:
                        if i >= 2:
                            se.wait_ge(c.sV.h, v_tok(i - 2))
                        dst = k_rot[i % 2][:, g * 512:(g + 1) * 512]
                    elif g < 4:
                        if i >= 2:
                            se.wait_ge(c.sV.h, v_tok(i - 2))
                        dst = v_rot[i % 2][:, (g - 2) * 512:(g - 1) * 512]
                    else:
                        dst = q_tm[:, i, (g - 4) * 512:(g - 3) * 512]
                    ins = se.copy(out=dst, in_=ps[g][:])
                    assert c.sS.inc(ins) == ev_tok(i, g)

        @b.vector
        def _(ve):
            for i in range(TCH):
                ve.wait_ge(c.sS.h, ev_tok(i, 3))
                ins = ve.reduce_sum(vs[:, i:i + 1], v_rot[i % 2][:], AX.X)
                ve.drain()
                assert c.sV.inc(ins) == v_tok(i) - 1
                ve.wait_ge(c.sS.h, ev_tok(i, 1))
                if i >= 2:
                    ve.wait_ge(c.sX.h, x_tok(i - 2))
                ins = ve.tensor_scalar_mul(m_rot[i % 2][:], k_rot[i % 2][:], vs[:, i:i + 1])
                assert c.sV.inc(ins) == v_tok(i)

        @b.sync
        def _(sy):
            for i in range(TCH):
                sy.wait_ge(c.sV.h, v_tok(i))
                ins = sy.dma_start(out=dram["cc_in"][i * 128:(i + 1) * 128, :],
                                   in_=m_rot[i % 2][:])
                assert c.sX.inc(ins, 16) == x_tok(i)

    exq.close()
    tok_proj_mm = c.sMM.v
    exw.close()

    if L0A_PHASES < 3:
        c.fence_deps = c.barrier_tokens()
        ex.close()
        return
    m_bf = sb("l0a_mbf", [128, TCH, D], BF16)
    tok_mloc = c.sX.v
    with nc.Block("l0a_cc") as b:
        @b.gpsimd
        def _(ge):
            ge.wait_ge(c.sX.h, tok_mloc)
            ge.wait_ge(c.sMM.h, tok_proj_mm)
            ge.collective_compute(
                "AllReduce", ALU.add, replica_groups=[list(range(NCORE))],
                ins=[dram["cc_in"]], outs=[dram["cc_out"]],
            ).then_inc(c.sCC.h)
            ge.wait_ge(c.sCC.h, 1)
            src = dram["cc_out"].rearrange("(i p) r -> p i r", p=128)
            ins = None
            for i0 in range(0, TCH, 4):
                ins = ge.dma_start(out=m_bf[:, i0:i0 + 4, :], in_=src[:, i0:i0 + 4, :])
                c.sW.inc(ins, 16)
    tok_mbf = c.sW.v

    if L0A_PHASES < 4:
        c.fence_deps = c.barrier_tokens()
        ex.close()
        return
    a_rot = [sb(f"l0a_a{i}", [128, TCH, 512], BF16) for i in range(2)]
    exd = ExitStack()
    psd = [c.psum(exd, f"l0a_psd{i}", [128, 512]) for i in range(2)]
    t1 = [sb(f"l0a_t1_{i}", [128, 512], F32) for i in range(2)]

    sX0, sMM0, sV0 = c.sX.v, c.sMM.v, c.sV.v
    tok_proj_evac = c.sS.v
    a_tok = lambda blk: sX0 + (blk + 1) * 16 * TCH
    d_mm_tok = lambda blk, rs: sMM0 + blk * 8 + rs + 1
    d_v_tok = lambda blk, rs: sV0 + (blk * 8 + rs + 1) * 2

    with nc.Block("l0a_sprime", no_gpsimd_drain=True) as b:
        @b.sync
        def _(sy):
            for blk in range(4):
                if blk >= 2:
                    sy.wait_ge(c.sMM.h, d_mm_tok(blk - 2, 7))
                col0 = blk * 512
                src = dram["a_cat"]
                ins = None
                for jc in range(TCH):
                    ins = sy.dma_start(out=a_rot[blk % 2][:, jc, :],
                                       in_=src[jc * 128:(jc + 1) * 128, col0:col0 + 512])
                    c.sX.inc(ins, 16)
                assert c.sX.v == a_tok(blk)

        @b.tensor
        def _(te):
            te.wait_ge(c.sW.h, tok_mbf)
            te.wait_ge(c.sS.h, tok_proj_evac)
            for blk in range(4):
                te.wait_ge(c.sX.h, a_tok(blk))
                for rs in range(8):
                    if blk * 8 + rs >= 2:
                        prev = blk * 8 + rs - 2
                        te.wait_ge(c.sV.h, d_v_tok(prev // 8, prev % 8))
                    ins = None
                    for jc in range(TCH):
                        ins = te.matmul(psd[rs % 2][:], m_bf[:, jc, rs * 128:(rs + 1) * 128],
                                        a_rot[blk % 2][:, jc, :],
                                        start=(jc == 0), stop=(jc == TCH - 1))
                    assert c.sMM.inc(ins) == d_mm_tok(blk, rs)

        @b.vector
        def _(ve):
            ve.wait_ge(c.sS.h, tok_proj_evac)
            for blk in range(4):
                sel, cs = blk // 2, blk % 2
                for rs in range(8):
                    ct = sel * 8 + rs
                    ve.wait_ge(c.sMM.h, d_mm_tok(blk, rs))
                    ve.tensor_tensor(t1[rs % 2][:], psd[rs % 2][:],
                                     q_tm[:, ct, cs * 512:(cs + 1) * 512], ALU.mult)
                    ins = ve.tensor_add(c.xres[:, ct, cs * 512:(cs + 1) * 512],
                                        c.xres[:, ct, cs * 512:(cs + 1) * 512], t1[rs % 2][:])
                    c.sV.inc(ins, 2)
                    assert c.sV.v == d_v_tok(blk, rs)

    exd.close()
    c.stream_deps = [(c.sV, c.sV.v)]
    c.fence_deps = c.barrier_tokens()
    ex.close()


def emit_l0ffn(c, w, dram, tag):
    nc = c.nc
    emit_ln(c, tag)
    fence = list(c.fence_deps)
    ex = ExitStack()
    sb = lambda name, shape, dt: ex.enter_context(nc.sbuf_tensor(f"{tag}_{name}", shape, dt))
    NE, NL = 8, 12
    FF = 2048
    FC = FF // 128

    gwt_sb = sb("gwt", [128, DCH, NL], BF16)
    eb_sb = sb("eb", [1, NL], BF16)
    lg = sb("lg", [128, TCH, NL], F32)
    eq1 = sb("eq1", [128, TCH, NL], F32)
    eq2 = sb("eq2", [128, TCH, NL], F32)
    m1 = sb("m1", [128, TCH], F32)
    m2 = sb("m2", [128, TCH], F32)
    p1 = sb("p1", [128, TCH], F32)
    p2 = sb("p2", [128, TCH], F32)
    wz = sb("wz", [128, TCH], F32)
    ab = sb("ab", [128, 2, TCH], F32)
    h_sb = sb("h", [128, FC, T], BF16)
    w1rot = [sb(f"w1r{i}", [128, DCH, 128], BF16) for i in range(2)]
    w2_sb = sb("w2", [128, FC, D], BF16)

    exp = ExitStack()
    ps_lg = [c.psum(exp, f"{tag}_pslg{i}", [128, NL]) for i in range(2)]
    ps_h = [c.psum(exp, f"{tag}_psh{i}", [128, 512]) for i in range(2)]
    ps_y = [c.psum(exp, f"{tag}_psy{i}", [128, 512]) for i in range(4)]

    with nc.Block(f"{tag}_smload", no_gpsimd_drain=True) as b:
        @b.gpsimd
        def _(ge):
            _wait(ge, fence)
            c.sW.inc(ge.dma_start(out=gwt_sb[:], in_=w["l0f_gwt"]), 16)
            c.sW.inc(ge.dma_start(out=eb_sb[:], in_=w["l0f_eb"]), 16)
    tok_sm = c.sW.v

    sMM0, sV0 = c.sMM.v, c.sV.v
    lgmm_tok = [sMM0 + i + 1 for i in range(TCH)]
    lgev_tok = [sV0 + i + 1 for i in range(TCH)]
    with nc.Block(f"{tag}_gates", no_gpsimd_drain=True) as b:
        @b.tensor
        def _(te):
            _wait(te, fence)
            te.wait_ge(c.sW.h, tok_sm)
            for i in range(TCH):
                te.wait_ge(c.sS.h, c.ln_evac[i])
                if i >= 2:
                    te.wait_ge(c.sV.h, lgev_tok[i - 2])
                for kc in range(DCH):
                    te.matmul(ps_lg[i % 2][:], c.xlnT[:, kc, i * 128:(i + 1) * 128],
                              gwt_sb[:, kc, :], start=(kc == 0), stop=False)
                ins = te.matmul(ps_lg[i % 2][:], c.ones[0:1, :], eb_sb[:],
                                start=False, stop=True)
                assert c.sMM.inc(ins) == lgmm_tok[i]

        @b.vector
        def _(ve):
            for i in range(TCH):
                ve.wait_ge(c.sMM.h, lgmm_tok[i])
                ins = ve.tensor_copy(out=lg[:, i, :], in_=ps_lg[i % 2][:])
                assert c.sV.inc(ins) == lgev_tok[i]
            ve.drain()
            ve.reduce_max(m1[:, :, None], lg[:], AX.X)
            ve.drain()
            ve.tensor_tensor(eq1[:], lg[:], m1[:, :, None].broadcast_to([128, TCH, NL]), ALU.is_equal)
            ve.scalar_tensor_tensor(eq2[:], eq1[:], -1e30, lg[:], ALU.mult, ALU.add)
            ve.drain()
            ve.reduce_max(m2[:, :, None], eq2[:], AX.X)
            ve.drain()
            ve.tensor_tensor(eq2[:], lg[:], m2[:, :, None].broadcast_to([128, TCH, NL]), ALU.is_equal)
            ins = ve.tensor_sub(p2[:], m2[:], m1[:])
            c.sV.inc(ins)

    tok_diff = c.sV.v
    with nc.Block(f"{tag}_sig", no_gpsimd_drain=True) as b:
        @b.scalar
        def _(se):
            se.wait_ge(c.sV.h, tok_diff)
            ins = se.activation(p1[:], p2[:], AF.Sigmoid, scale=-1.0)
            c.sS.inc(ins)
    tok_p1 = c.sS.v

    with nc.Block(f"{tag}_gmath", no_gpsimd_drain=True) as b:
        @b.vector
        def _(ve):
            ve.wait_ge(c.sS.h, tok_p1)
            ve.tensor_scalar(p2[:], p1[:], -1.0, 1.0, ALU.mult, ALU.add)
            ve.drain()
            ve.tensor_tensor(lg[:], eq1[:], p1[:, :, None].broadcast_to([128, TCH, NL]), ALU.mult)
            ve.tensor_tensor(eq2[:], eq2[:], p2[:, :, None].broadcast_to([128, TCH, NL]), ALU.mult)
            ve.tensor_add(lg[:], lg[:], eq2[:])
            ve.drain()
            ve.reduce_sum(wz[:, :, None], lg[:, :, NE:NL], AX.X)
            ve.drain()
            ve.tensor_tensor(ab[:, 0, :], wz[:], c.rstd[:], ALU.mult)
            ve.drain()
            ve.tensor_tensor(ab[:, 1, :], ab[:, 0, :], c.mu[:], ALU.mult)
            ve.tensor_scalar_add(ab[:, 0, :], ab[:, 0, :], 1.0)
            ve.drain()
            ins = None
            for i in range(TCH):
                ins = ve.tensor_scalar(c.xres[:, i, :], c.xres[:, i, :],
                                       ab[:, 0, i:i + 1], ab[:, 1, i:i + 1],
                                       ALU.mult, ALU.subtract)
            c.sV.inc(ins)
    tok_gates = c.sV.v

    w1t = w["l0f_w1t"]
    w2t = w["l0f_w2t"]
    sW0, sMM1, sS1, sV1 = c.sW.v, c.sMM.v, c.sS.v, c.sV.v
    w1tok = lambda e, fc: sW0 + e * 32 * 16 + (fc + 1) * 16
    w2tok = lambda e: sW0 + e * 32 * 16 + 32 * 16
    hmm = lambda e, g: sMM1 + e * 96 + g + 1
    ymm = lambda e, yi: sMM1 + e * 96 + 64 + yi + 1
    hev = lambda e, g: sS1 + e * 64 + g + 1
    ytok = lambda e, yi: sV1 + e * 32 + yi + 1

    with nc.Block(f"{tag}_experts", no_gpsimd_drain=True) as b:
        @b.gpsimd
        def _(ge):
            for e in range(NE):
                src1 = w1t[e].rearrange("(dc p) f -> p dc f", p=128)
                for fc in range(FC):
                    slot = fc % 2
                    if fc >= 2:
                        ge.wait_ge(c.sMM.h, hmm(e, (fc - 2) * 4 + 3))
                    elif e > 0:
                        ge.wait_ge(c.sMM.h, hmm(e - 1, (fc + 14) * 4 + 3))
                    ins = ge.dma_start(out=w1rot[slot][:],
                                       in_=src1[:, :, fc * 128:(fc + 1) * 128])
                    assert c.sW.inc(ins, 16) == w1tok(e, fc)
                if e > 0:
                    ge.wait_ge(c.sV.h, ytok(e - 1, 31))
                src2 = w2t[e].rearrange("(fc p) d -> p fc d", p=128)
                for fc in range(FC):
                    ins = ge.dma_start(out=w2_sb[:, fc, :], in_=src2[:, fc, :])
                    c.sW.inc(ins, 16)
                assert c.sW.v == w2tok(e)

        @b.tensor
        def _(te):
            for e in range(NE):
                for fc in range(FC):
                    te.wait_ge(c.sW.h, w1tok(e, fc))
                    for tt in range(4):
                        g = fc * 4 + tt
                        if g >= 2:
                            te.wait_ge(c.sS.h, hev(e, g - 2))
                        elif e > 0:
                            te.wait_ge(c.sS.h, hev(e - 1, 62 + g))
                        ins = None
                        for dc in range(DCH):
                            ins = te.matmul(ps_h[g % 2][:],
                                            w1rot[fc % 2][:, dc, :],
                                            c.xlnT[:, dc, tt * 512:(tt + 1) * 512],
                                            start=(dc == 0), stop=(dc == DCH - 1))
                        assert c.sMM.inc(ins) == hmm(e, g)
                te.wait_ge(c.sW.h, w2tok(e))
                te.wait_ge(c.sS.h, hev(e, 63))
                for tcol in range(TCH):
                    for dd in range(2):
                        yi = tcol * 2 + dd
                        if yi >= 4:
                            te.wait_ge(c.sV.h, ytok(e, yi - 4))
                        elif e > 0:
                            te.wait_ge(c.sV.h, ytok(e - 1, 28 + yi))
                        ins = None
                        for fc in range(FC):
                            ins = te.matmul(ps_y[yi % 4][:],
                                            h_sb[:, fc, tcol * 128:(tcol + 1) * 128],
                                            w2_sb[:, fc, dd * 512:(dd + 1) * 512],
                                            start=(fc == 0), stop=(fc == FC - 1))
                        assert c.sMM.inc(ins) == ymm(e, yi)

        @b.scalar
        def _(se):
            for e in range(NE):
                if e > 0:
                    se.wait_ge(c.sMM.h, ymm(e - 1, 31))
                for g in range(64):
                    fc, tt = g // 4, g % 4
                    se.wait_ge(c.sMM.h, hmm(e, g))
                    ins = se.activation(h_sb[:, fc, tt * 512:(tt + 1) * 512],
                                        ps_h[g % 2][:], AF.Silu)
                    assert c.sS.inc(ins) == hev(e, g)

        @b.vector
        def _(ve):
            for e in range(NE):
                for yi in range(32):
                    tcol, dd = yi // 2, yi % 2
                    ve.wait_ge(c.sMM.h, ymm(e, yi))
                    ins = ve.scalar_tensor_tensor(
                        c.xres[:, tcol, dd * 512:(dd + 1) * 512],
                        ps_y[yi % 4][:], lg[:, tcol, e:e + 1],
                        c.xres[:, tcol, dd * 512:(dd + 1) * 512],
                        ALU.mult, ALU.add)
                    assert c.sV.inc(ins) == ytok(e, yi)

    exp.close()
    c.stream_deps = [(c.sV, c.sV.v)]
    c.fence_deps = c.barrier_tokens()
    ex.close()


L0B_PHASES = 9

def emit_l0b(c, w, dram, tag="s2"):
    nc = c.nc
    emit_ln(c, tag)
    fence = list(c.fence_deps)
    ex = ExitStack()
    sb = lambda name, shape, dt: ex.enter_context(nc.sbuf_tensor(f"{tag}_{name}", shape, dt))
    q_fm = sb("qfm", [128, DCH, T], BF16)
    bq_cols = sb("bqc", [128, DCH], F32)
    bkv_row = sb("bkvr", [1, 2048], BF16)
    bout_row = sb("boutr", [1, D], BF16)
    nk2 = sb("nk2", [128, TCH], F32)
    rk = sb("rk", [128, TCH], F32)
    k_rot = [sb(f"k{i}", [128, D], BF16) for i in range(2)]
    v_rot = [sb(f"v{i}", [128, D], BF16) for i in range(2)]
    pr_rot = [sb(f"pr{i}", [128, D], BF16) for i in range(2)]
    sq2 = sb("sq2", [128, T], BF16)
    row_bf = sb("rowbf", [1, T], BF16)
    rq_col = sb("rqc", [128, TCH], F32)
    kvT_col = sb("kvT", [128, DCH], F32)
    bb_sb = sb("bb", [128, D], BF16)
    t2 = [sb(f"t2_{i}", [128, 512], F32) for i in range(2)]

    ex2 = ExitStack()
    wqkv_sb = ex2.enter_context(nc.sbuf_tensor(f"{tag}_wqkv", [128, DCH, 2 * D], BF16))

    with nc.Block(f"{tag}_load", no_gpsimd_drain=True) as b:
        @b.gpsimd
        def _(ge):
            _wait(ge, fence)
            src = w["l0b_wqkv"].rearrange("(dc p) e -> p dc e", p=128)
            for dc in range(DCH):
                c.sW.inc(ge.dma_start(out=wqkv_sb[:, dc, 0:D], in_=src[:, dc, 0:D]), 16)
            c.sW.inc(ge.dma_start(out=bkv_row[:], in_=w["l0b_bqkv"][:, D:3 * D]), 16)
            c.sW.inc(ge.dma_start(out=bout_row[:], in_=w["l0b_bout"]), 16)

        @b.sync
        def _(sy):
            _wait(sy, fence)
            c.sX.inc(sy.dma_start(out=bq_cols[:], in_=w["l0b_bq_cols"]), 16)
    tok_w = c.sW.v
    tok_bq = c.sX.v

    ex_row = ExitStack()
    ps_row = [c.psum(ex_row, f"{tag}_psr{i}", [1, 512]) for i in range(2)]
    exp1 = ExitStack()
    ps_q = [c.psum(exp1, f"{tag}_psq{i}", [128, 512]) for i in range(2)]
    ps_k = [c.psum(exp1, f"{tag}_psk{i}", [128, 512]) for i in range(2)]
    ps_v = [c.psum(exp1, f"{tag}_psv{i}", [128, 512]) for i in range(2)]

    sMM0, sV0q = c.sMM.v, c.sV.v
    qmm = lambda g: sMM0 + g + 1
    qev = lambda g: sV0q + g + 1
    with nc.Block(f"{tag}_p1", no_gpsimd_drain=True) as b:
        @b.tensor
        def _(te):
            _wait(te, fence)
            te.wait_ge(c.sW.h, tok_w)
            te.wait_ge(c.sS.h, c.tok_xlnT[1])
            for ec in range(DCH):
                for tt in range(4):
                    g = ec * 4 + tt
                    if g >= 2:
                        te.wait_ge(c.sV.h, qev(g - 2))
                    ins = None
                    for dc in range(DCH):
                        ins = te.matmul(ps_q[g % 2][:], wqkv_sb[:, dc, ec * 128:(ec + 1) * 128],
                                        c.xlnT[:, dc, tt * 512:(tt + 1) * 512],
                                        start=(dc == 0), stop=(dc == DCH - 1))
                    assert c.sMM.inc(ins) == qmm(g)

        @b.vector
        def _(ve):
            ve.wait_ge(c.sX.h, tok_bq)
            for g in range(32):
                ec, tt = g // 4, g % 4
                ve.wait_ge(c.sMM.h, qmm(g))
                ins = ve.tensor_scalar_add(q_fm[:, ec, tt * 512:(tt + 1) * 512],
                                           ps_q[g % 2][:], bq_cols[:, ec:ec + 1])
                assert c.sV.inc(ins) == qev(g)

    if L0B_PHASES < 2:
        c.fence_deps = c.barrier_tokens()
        exp1.close(); ex_row.close(); ex2.close(); ex.close()
        return
    tok_p1_mm = c.sMM.v
    with nc.Block(f"{tag}_loadB", no_gpsimd_drain=True) as b:
        @b.gpsimd
        def _(ge):
            ge.wait_ge(c.sMM.h, tok_p1_mm)
            src = w["l0b_wqkv"].rearrange("(dc p) e -> p dc e", p=128)
            for dc in range(DCH):
                c.sW.inc(ge.dma_start(out=wqkv_sb[:, dc, :], in_=src[:, dc, D:3 * D]), 16)
    tok_wB = c.sW.v

    sMM0, sS0, sV0 = c.sMM.v, c.sS.v, c.sV.v
    kvmm = lambda i, g: sMM0 + i * 5 + g + 1
    rowmm = lambda i: sMM0 + i * 5 + 5
    sev = lambda i, j: sS0 + i * 6 + j + 1
    vtk = lambda i, j: sV0 + i * 2 + j + 1
    with nc.Block(f"{tag}_p2", no_gpsimd_drain=True) as b:
        @b.tensor
        def _(te):
            te.wait_ge(c.sW.h, tok_wB)
            for i in range(TCH):
                for g in range(4):
                    src_e = g * 512
                    pp = (ps_k if g < 2 else ps_v)[g % 2]
                    if i >= 1:
                        te.wait_ge(c.sS.h, sev(i - 1, g))
                    for dc in range(DCH):
                        te.matmul(pp[:], c.xlnT[:, dc, i * 128:(i + 1) * 128],
                                  wqkv_sb[:, dc, src_e:src_e + 512],
                                  start=(dc == 0), stop=False)
                    ins = te.matmul(pp[:], c.ones[0:1, :],
                                    bkv_row[:, src_e:src_e + 512],
                                    start=False, stop=True)
                    assert c.sMM.inc(ins) == kvmm(i, g)
                te.wait_ge(c.sV.h, vtk(i, 1))
                ins = None
                for half in range(2):
                    ins = te.matmul(ps_row[half][:], c.ones[:, 0:1],
                                    pr_rot[i % 2][:, half * 512:(half + 1) * 512],
                                    start=(i == 0), stop=(i == TCH - 1))
                c.sMM.inc(ins)
                assert c.sMM.v == rowmm(i)

        @b.scalar
        def _(se):
            for i in range(TCH):
                for g in range(4):
                    se.wait_ge(c.sMM.h, kvmm(i, g))
                    if i >= 2:
                        se.wait_ge(c.sV.h, vtk(i - 2, 1))
                    dst = (k_rot if g < 2 else v_rot)[i % 2][:, (g % 2) * 512:(g % 2) * 512 + 512]
                    ins = se.copy(out=dst, in_=(ps_k if g < 2 else ps_v)[g % 2][:])
                    assert c.sS.inc(ins) == sev(i, g)
                ins = se.activation(c.sq_scr[:], k_rot[i % 2][:], AF.Square,
                                    accum_out=nk2[:, i:i + 1])
                assert c.sS.inc(ins) == sev(i, 4)
                se.wait_ge(c.sV.h, vtk(i, 0))
                ins = se.activation(rk[:, i:i + 1], rk[:, i:i + 1], AF.Sqrt)
                assert c.sS.inc(ins) == sev(i, 5)

        @b.vector
        def _(ve):
            for i in range(TCH):
                ve.wait_ge(c.sS.h, sev(i, 4))
                ins = ve.reciprocal(rk[:, i:i + 1], nk2[:, i:i + 1])
                ve.drain()
                assert c.sV.inc(ins) == vtk(i, 0)
                ve.wait_ge(c.sS.h, sev(i, 5))
                if i >= 2:
                    ve.wait_ge(c.sMM.h, rowmm(i - 2))
                ins = ve.scalar_tensor_tensor(pr_rot[i % 2][:], k_rot[i % 2][:],
                                              rk[:, i:i + 1], v_rot[i % 2][:],
                                              ALU.mult, ALU.mult)
                assert c.sV.inc(ins) == vtk(i, 1)

    tok_p2_mm = c.sMM.v
    exp1.close()
    ex2.close()
    if L0B_PHASES < 3:
        c.fence_deps = c.barrier_tokens()
        ex_row.close(); ex.close()
        return

    exn1 = ExitStack()
    ps_nq = [c.psum(exn1, f"{tag}_psnq{i}", [1, 512]) for i in range(4)]
    ps_tr2 = c.psum(exn1, f"{tag}_pstr2", [128, TCH], F32)
    sMM0, sS0 = c.sMM.v, c.sS.v
    sqe = lambda ec: sS0 + ec + 1
    nqmm = lambda ec: sMM0 + ec + 1
    with nc.Block(f"{tag}_qn", no_gpsimd_drain=True) as b:
        @b.scalar
        def _(se):
            for ec in range(DCH):
                if ec >= 1:
                    se.wait_ge(c.sMM.h, nqmm(ec - 1))
                ins = se.activation(sq2[:], q_fm[:, ec, :], AF.Square)
                assert c.sS.inc(ins) == sqe(ec)

        @b.tensor
        def _(te):
            for ec in range(DCH):
                te.wait_ge(c.sS.h, sqe(ec))
                ins = None
                for tt in range(4):
                    ins = te.matmul(ps_nq[tt][:], c.ones[:, 0:1],
                                    sq2[:, tt * 512:(tt + 1) * 512],
                                    start=(ec == 0), stop=(ec == DCH - 1))
                assert c.sMM.inc(ins) == nqmm(ec)

    if L0B_PHASES < 2.7:
        c.fence_deps = c.barrier_tokens()
        exn1.close(); ex_row.close(); ex.close()
        return
    tok_nq = c.sMM.v
    sV0 = c.sV.v
    with nc.Block(f"{tag}_qn2", no_gpsimd_drain=True) as b:
        @b.vector
        def _(ve):
            ve.wait_ge(c.sMM.h, tok_nq)
            ins = None
            for tt in range(4):
                ins = ve.tensor_copy(out=row_bf[:, tt * 512:(tt + 1) * 512], in_=ps_nq[tt][:])
            c.sV.inc(ins)
    tok_nqrow = c.sV.v

    if L0B_PHASES < 2.9:
        c.fence_deps = c.barrier_tokens()
        exn1.close(); ex_row.close(); ex.close()
        return
    sMM0, sV0, sS0 = c.sMM.v, c.sV.v, c.sS.v
    stage = ex.enter_context(nc.sbuf_tensor(f"{tag}_stage", [128, 128], BF16))
    ps_trs = exn1.enter_context(nc.psum_tensor(f"{tag}_pstrs", [128, 128], BF16))
    stg_tok = [sV0 + 1 + i for i in range(TCH)]
    trp_tok = [sMM0 + 1 + i for i in range(TCH)]
    col_tok = [sS0 + 1 + i for i in range(TCH)]
    with nc.Block(f"{tag}_qn3", no_gpsimd_drain=True) as b:
        @b.vector
        def _(ve):
            ve.memset(stage[:], 0.0)
            for i in range(TCH):
                if i >= 1:
                    ve.wait_ge(c.sMM.h, trp_tok[i - 1])
                ins = ve.tensor_copy(out=stage[0:1, :], in_=row_bf[:, i * 128:(i + 1) * 128])
                assert c.sV.inc(ins) == stg_tok[i]

        @b.tensor
        def _(te):
            for i in range(TCH):
                te.wait_ge(c.sV.h, stg_tok[i])
                if i >= 1:
                    te.wait_ge(c.sS.h, col_tok[i - 1])
                ins = te.transpose(ps_trs[:], stage[:], c.ident[:])
                assert c.sMM.inc(ins) == trp_tok[i]

        @b.scalar
        def _(se):
            for i in range(TCH):
                se.wait_ge(c.sMM.h, trp_tok[i])
                ins = se.copy(out=rq_col[:, i:i + 1], in_=ps_trs[:, 0:1])
                assert c.sS.inc(ins) == col_tok[i]
    tok_qn3 = c.sS.v
    sV0, sS0 = c.sV.v, c.sS.v
    with nc.Block(f"{tag}_qn4", no_gpsimd_drain=True) as b:
        @b.vector
        def _(ve):
            ve.wait_ge(c.sS.h, tok_qn3)
            ins = ve.reciprocal(rq_col[:], rq_col[:])
            ve.drain()
            c.sV.inc(ins)

        @b.scalar
        def _(se):
            se.wait_ge(c.sV.h, sV0 + 1)
            ins = se.activation(rq_col[:], rq_col[:], AF.Sqrt)
            c.sS.inc(ins)
    tok_rq = (c.sS, c.sS.v)
    tok_rowbf_free = trp_tok[-1]
    exn1.close()
    if L0B_PHASES < 4:
        c.fence_deps = c.barrier_tokens()
        ex_row.close(); ex.close()
        return

    ex3 = ExitStack()
    wout_sb = ex3.enter_context(nc.sbuf_tensor(f"{tag}_wout", [128, DCH, D], BF16))
    ws_sb = ex3.enter_context(nc.sbuf_tensor(f"{tag}_ws", [128, DCH, D], BF16))
    exn2 = ExitStack()
    ps_trk = c.psum(exn2, f"{tag}_pstrk", [128, DCH], F32)
    ps_bb = [c.psum(exn2, f"{tag}_psbb{i}", [128, 512]) for i in range(2)]

    with nc.Block(f"{tag}_wload2", no_gpsimd_drain=True) as b:
        @b.gpsimd
        def _(ge):
            ge.wait_ge(c.sMM.h, tok_p2_mm)
            src = w["l0b_wout"].rearrange("(dc p) e -> p dc e", p=128)
            for dc in range(DCH):
                c.sW.inc(ge.dma_start(out=wout_sb[:, dc, :], in_=src[:, dc, :]), 16)
    tok_wout = c.sW.v

    sV0, sMM0 = c.sV.v, c.sMM.v
    with nc.Block(f"{tag}_ws", no_gpsimd_drain=True) as b:
        @b.vector
        def _(ve):
            ve.wait_ge(c.sMM.h, tok_p2_mm)
            ve.wait_ge(c.sMM.h, tok_rowbf_free)
            ve.tensor_copy(out=row_bf[:, 0:512], in_=ps_row[0][:])
            ins = ve.tensor_copy(out=row_bf[:, 512:1024], in_=ps_row[1][:])
            assert c.sV.inc(ins) == sV0 + 1

        @b.tensor
        def _(te):
            te.wait_ge(c.sV.h, sV0 + 1)
            ins = None
            for dc in range(DCH):
                ins = te.matmul(ps_trk[:, dc:dc + 1], row_bf[:, dc * 128:(dc + 1) * 128],
                                c.ones[0:1, 0:1], start=True, stop=True)
            assert c.sMM.inc(ins) == sMM0 + 1
            te.wait_ge(c.sW.h, tok_w)
            for half in range(2):
                ins = te.matmul(ps_bb[half][:], c.ones[0:1, :],
                                bout_row[:, half * 512:(half + 1) * 512],
                                start=True, stop=True)
            assert c.sMM.inc(ins) == sMM0 + 2

        @b.vector
        def _(ve):
            ve.wait_ge(c.sMM.h, sMM0 + 1)
            ve.tensor_copy(out=kvT_col[:], in_=ps_trk[:])
            ve.drain()
            ve.wait_ge(c.sW.h, tok_wout)
            for dc in range(DCH):
                ve.tensor_scalar_mul(ws_sb[:, dc, :], wout_sb[:, dc, :], kvT_col[:, dc:dc + 1])
            ve.wait_ge(c.sMM.h, sMM0 + 2)
            ve.tensor_copy(out=bb_sb[:, 0:512], in_=ps_bb[0][:])
            ins = ve.tensor_copy(out=bb_sb[:, 512:1024], in_=ps_bb[1][:])
            assert c.sV.inc(ins) == sV0 + 2
    tok_ws = c.sV.v
    exn2.close()
    ex_row.close()
    if L0B_PHASES < 5:
        c.fence_deps = c.barrier_tokens()
        ex3.close(); ex.close()
        return

    exf = ExitStack()
    ps_f = [c.psum(exf, f"{tag}_psf{i}", [128, 512]) for i in range(2)]
    sMM0, sV0 = c.sMM.v, c.sV.v
    fmm = lambda j: sMM0 + j + 1
    fvt = lambda j: sV0 + 2 * (j + 1)
    with nc.Block(f"{tag}_fin", no_gpsimd_drain=True) as b:
        @b.tensor
        def _(te):
            te.wait_ge(c.sV.h, tok_ws)
            for j in range(32):
                i, dd = j // 2, j % 2
                if j >= 2:
                    te.wait_ge(c.sV.h, fvt(j - 2))
                ins = None
                for dc in range(DCH):
                    ins = te.matmul(ps_f[j % 2][:], q_fm[:, dc, i * 128:(i + 1) * 128],
                                    ws_sb[:, dc, dd * 512:(dd + 1) * 512],
                                    start=(dc == 0), stop=(dc == DCH - 1))
                assert c.sMM.inc(ins) == fmm(j)

        @b.vector
        def _(ve):
            ve.wait_ge(tok_rq[0].h, tok_rq[1])
            for j in range(32):
                i, dd = j // 2, j % 2
                ve.wait_ge(c.sMM.h, fmm(j))
                ve.scalar_tensor_tensor(t2[j % 2][:], ps_f[j % 2][:], rq_col[:, i:i + 1],
                                        bb_sb[:, dd * 512:(dd + 1) * 512], ALU.mult, ALU.add)
                ins = ve.tensor_add(c.xres[:, i, dd * 512:(dd + 1) * 512],
                                    c.xres[:, i, dd * 512:(dd + 1) * 512], t2[j % 2][:])
                c.sV.inc(ins, 2)
                assert c.sV.v == fvt(j)

    exf.close()
    ex3.close()
    c.stream_deps = [(c.sV, c.sV.v)]
    c.fence_deps = c.barrier_tokens()
    ex.close()


def emit_l1a(c, w, dram, tag="s4"):
    nc = c.nc
    emit_ln(c, tag)
    fence = list(c.fence_deps)
    EPS = 1e-3
    ex = ExitStack()
    sb = lambda name, shape, dt: ex.enter_context(nc.sbuf_tensor(f"{tag}_{name}", shape, dt))
    wqkv_sb = sb("wqkv", [128, DCH, 3 * D], BF16)
    wproj_sb = sb("wproj", [128, DCH, D], BF16)
    bqkv_row = sb("bqkvr", [1, 3 * D], BF16)
    bproj_row = sb("bprojr", [1, D], BF16)
    scale_col = sb("scol", [128, 1], F32)
    carry = sb("carry", [1, 2 * D], F32)
    carry_bf = sb("carrybf", [1, 2 * D], BF16)
    ea = sb("ea", [128, D], BF16)
    eb = sb("ebuf", [128, D], BF16)
    q_t = sb("qt", [128, D], BF16)
    k_t = sb("kt", [128, D], BF16)
    v_t = sb("vt", [128, D], BF16)
    kv_t = sb("kvt", [128, D], BF16)
    kc_t = sb("kct", [128, D], BF16)
    kvc_t = sb("kvct", [128, D], BF16)
    y_t = sb("yt", [128, D], BF16)
    ylT = sb("ylT", [128, DCH, 128], BF16)
    den_ch = sb("den", [128, 16], F32)
    rden_ch = sb("rden", [128, 16], F32)

    exp = ExitStack()
    ps = [c.psum(exp, f"{tag}_ps{i}", [128, 512]) for i in range(2)]
    ps_cum = [c.psum(exp, f"{tag}_pscum{i}", [128, 512]) for i in range(2)]
    ps_sum = c.psum(exp, f"{tag}_pssum", [1, 512])
    ps_ytr = c.psum(exp, f"{tag}_psytr", [128, D], BF16)
    ps_pr = [c.psum(exp, f"{tag}_pspr{i}", [128, 512]) for i in range(2)]

    with nc.Block(f"{tag}_load", no_gpsimd_drain=True) as b:
        @b.gpsimd
        def _(ge):
            _wait(ge, fence)
            src = w["l1a_wqkvt"].rearrange("(dc p) e -> p dc e", p=128)
            for dc in range(DCH):
                c.sW.inc(ge.dma_start(out=wqkv_sb[:, dc, :], in_=src[:, dc, :]), 16)
            src = w["l1a_wprojt"].rearrange("(dc p) e -> p dc e", p=128)
            for dc in range(DCH):
                c.sW.inc(ge.dma_start(out=wproj_sb[:, dc, :], in_=src[:, dc, :]), 16)
            c.sW.inc(ge.dma_start(out=bqkv_row[:], in_=w["l1a_bqkv"]), 16)
            c.sW.inc(ge.dma_start(out=bproj_row[:], in_=w["l1a_bproj"]), 16)

        @b.sync
        def _(sy):
            _wait(sy, fence)
            c.sX.inc(sy.dma_start(out=scale_col[:], in_=w["l1a_scale_col"]), 16)
    tok_w = c.sW.v
    tok_sc = c.sX.v

    sMM0, sS0, sV0 = c.sMM.v, c.sS.v, c.sV.v
    NT, NS, NV = 17, 9, 24
    tM = lambda i, j: sMM0 + i * NT + j
    tS = lambda i, j: sS0 + i * NS + j
    tV = lambda i, j: sV0 + 1 + i * NV + j
    cons = {0: ("V", 3), 1: ("V", 4), 2: ("V", 8), 3: ("V", 9), 4: ("S", 3), 5: ("S", 4)}

    with nc.Block(f"{tag}_main", no_gpsimd_drain=True) as b:
        @b.tensor
        def _(te):
            te.wait_ge(c.sW.h, tok_w)
            te.wait_ge(c.sS.h, c.tok_xlnT[1])
            for i in range(TCH):
                for g in range(6):
                    G = i * 6 + g
                    if G >= 2:
                        pi, pg = divmod(G - 2, 6)
                        eng, j = cons[pg]
                        if eng == "V":
                            te.wait_ge(c.sV.h, tV(pi, j))
                        else:
                            te.wait_ge(c.sS.h, tS(pi, j))
                    for dc in range(DCH):
                        te.matmul(ps[g % 2][:], c.xlnT[:, dc, i * 128:(i + 1) * 128],
                                  wqkv_sb[:, dc, g * 512:(g + 1) * 512],
                                  start=(dc == 0), stop=False)
                    ins = te.matmul(ps[g % 2][:], c.ones[0:1, :],
                                    bqkv_row[:, g * 512:(g + 1) * 512],
                                    start=False, stop=True)
                    assert c.sMM.inc(ins) == tM(i, g + 1)
                te.wait_ge(c.sV.h, tV(i, 10))
                te.wait_ge(c.sV.h, tV(i - 1, 16) if i >= 1 else sV0 + 1)
                for sl in range(2):
                    if i >= 1:
                        te.wait_ge(c.sS.h, tS(i - 1, 7 + sl))
                    te.matmul(ps_cum[sl][:], c.triu[:], k_t[:, sl * 512:(sl + 1) * 512],
                              start=True, stop=False)
                    ins = te.matmul(ps_cum[sl][:], c.ones[0:1, :],
                                    carry_bf[:, sl * 512:(sl + 1) * 512],
                                    start=False, stop=True)
                    assert c.sMM.inc(ins) == tM(i, 7 + sl)
                te.wait_ge(c.sV.h, tV(i, 11))
                for sl in range(2):
                    te.wait_ge(c.sS.h, tS(i, 5 + sl))
                    te.matmul(ps_cum[sl][:], c.triu[:], kv_t[:, sl * 512:(sl + 1) * 512],
                              start=True, stop=False)
                    ins = te.matmul(ps_cum[sl][:], c.ones[0:1, :],
                                    carry_bf[:, D + sl * 512:D + (sl + 1) * 512],
                                    start=False, stop=True)
                    assert c.sMM.inc(ins) == tM(i, 9 + sl)
                for j in range(4):
                    if j == 0:
                        if i >= 1:
                            te.wait_ge(c.sV.h, tV(i - 1, 15))
                    else:
                        te.wait_ge(c.sV.h, tV(i, 11 + j))
                    src = (k_t if j < 2 else kv_t)[:, (j % 2) * 512:(j % 2) * 512 + 512]
                    ins = te.matmul(ps_sum[:], c.ones[:, 0:1], src, start=True, stop=True)
                    assert c.sMM.inc(ins) == tM(i, 11 + j)
                te.wait_ge(c.sV.h, tV(i, 22))
                if i >= 1:
                    te.wait_ge(c.sS.h, tS(i - 1, 9))
                ins = None
                for cc_ in range(DCH):
                    ins = te.transpose(ps_ytr[:, cc_ * 128:(cc_ + 1) * 128],
                                       y_t[:, cc_ * 128:(cc_ + 1) * 128], c.ident[:])
                assert c.sMM.inc(ins) == tM(i, 15)
                te.wait_ge(c.sS.h, tS(i, 9))
                for cs in range(2):
                    if i >= 1:
                        te.wait_ge(c.sV.h, tV(i - 1, 23 + cs))
                    for ec in range(DCH):
                        te.matmul(ps_pr[cs][:], ylT[:, ec, :],
                                  wproj_sb[:, ec, cs * 512:(cs + 1) * 512],
                                  start=(ec == 0), stop=False)
                    ins = te.matmul(ps_pr[cs][:], c.ones[0:1, :],
                                    bproj_row[:, cs * 512:(cs + 1) * 512],
                                    start=False, stop=True)
                    assert c.sMM.inc(ins) == tM(i, 16 + cs)

        @b.scalar
        def _(se):
            for i in range(TCH):
                se.wait_ge(c.sV.h, tV(i, 2))
                ins = se.activation(ea[:], ea[:], AF.Exp)
                assert c.sS.inc(ins) == tS(i, 1)
                se.wait_ge(c.sV.h, tV(i, 7))
                ins = se.activation(ea[:], ea[:], AF.Exp)
                assert c.sS.inc(ins) == tS(i, 2)
                for sl in range(2):
                    se.wait_ge(c.sMM.h, tM(i, 5 + sl))
                    if i >= 1:
                        se.wait_ge(c.sV.h, tV(i - 1, 11))
                    ins = se.copy(out=v_t[:, sl * 512:(sl + 1) * 512],
                                  in_=ps[sl][:])
                    assert c.sS.inc(ins) == tS(i, 3 + sl)
                for sl in range(2):
                    se.wait_ge(c.sMM.h, tM(i, 7 + sl))
                    if i >= 1:
                        se.wait_ge(c.sV.h, tV(i - 1, 17))
                    ins = se.copy(out=kc_t[:, sl * 512:(sl + 1) * 512], in_=ps_cum[sl][:])
                    assert c.sS.inc(ins) == tS(i, 5 + sl)
                for sl in range(2):
                    se.wait_ge(c.sMM.h, tM(i, 9 + sl))
                    if i >= 1:
                        se.wait_ge(c.sV.h, tV(i - 1, 21))
                    ins = se.copy(out=kvc_t[:, sl * 512:(sl + 1) * 512], in_=ps_cum[sl][:])
                    assert c.sS.inc(ins) == tS(i, 7 + sl)
                se.wait_ge(c.sMM.h, tM(i, 15))
                if i >= 1:
                    se.wait_ge(c.sMM.h, tM(i - 1, 17))
                ins = se.copy(out=ylT[:],
                              in_=ps_ytr[:].rearrange("p (cc q) -> p cc q", cc=DCH))
                assert c.sS.inc(ins) == tS(i, 9)

        @b.vector
        def _(ve):
            ve.memset(carry[:], 0.0)
            ins = ve.memset(carry_bf[:], 0.0)
            assert c.sV.inc(ins) == sV0 + 1
            ve.wait_ge(c.sX.h, tok_sc)
            for i in range(TCH):
                bank_q = [0, 1]
                for sl in range(2):
                    ve.wait_ge(c.sMM.h, tM(i, 1 + sl))
                    ins = ve.tensor_scalar(ea[:, sl * 512:(sl + 1) * 512],
                                           ps[bank_q[sl]][:], scale_col[:], 0.0,
                                           ALU.mult, ALU.min)
                    assert c.sV.inc(ins) == tV(i, 1 + sl)
                for sl in range(2):
                    ins = ve.tensor_scalar(eb[:, sl * 512:(sl + 1) * 512],
                                           ps[bank_q[sl]][:], scale_col[:], 0.0,
                                           ALU.mult, ALU.max)
                    assert c.sV.inc(ins) == tV(i, 3 + sl)
                ve.wait_ge(c.sS.h, tS(i, 1))
                ins = ve.tensor_add(q_t[:], ea[:], eb[:])
                assert c.sV.inc(ins) == tV(i, 5)
                bank_k = [0, 1]
                for sl in range(2):
                    ve.wait_ge(c.sMM.h, tM(i, 3 + sl))
                    ins = ve.tensor_scalar(ea[:, sl * 512:(sl + 1) * 512],
                                           ps[bank_k[sl]][:], scale_col[:], 0.0,
                                           ALU.mult, ALU.min)
                    assert c.sV.inc(ins) == tV(i, 6 + sl)
                for sl in range(2):
                    ins = ve.tensor_scalar(eb[:, sl * 512:(sl + 1) * 512],
                                           ps[bank_k[sl]][:], scale_col[:], 0.0,
                                           ALU.mult, ALU.max)
                    assert c.sV.inc(ins) == tV(i, 8 + sl)
                ve.wait_ge(c.sS.h, tS(i, 2))
                ins = ve.tensor_add(k_t[:], ea[:], eb[:])
                assert c.sV.inc(ins) == tV(i, 10)
                ve.wait_ge(c.sS.h, tS(i, 4))
                ins = ve.tensor_mul(kv_t[:], k_t[:], v_t[:])
                assert c.sV.inc(ins) == tV(i, 11)
                for j in range(4):
                    ve.wait_ge(c.sMM.h, tM(i, 11 + j))
                    off = (0 if j < 2 else D) + (j % 2) * 512
                    ins = ve.tensor_add(carry[:, off:off + 512], carry[:, off:off + 512],
                                        ps_sum[:])
                    assert c.sV.inc(ins) == tV(i, 12 + j)
                ve.drain()
                ve.wait_ge(c.sMM.h, tM(i, 10))
                ins = ve.tensor_copy(out=carry_bf[:], in_=carry[:])
                assert c.sV.inc(ins) == tV(i, 16)
                ve.wait_ge(c.sS.h, tS(i, 6))
                ins = ve.tensor_mul(ea[:], q_t[:], kc_t[:])
                assert c.sV.inc(ins) == tV(i, 17)
                ve.drain()
                ins = ve.reduce_sum(den_ch[:, :, None],
                                    ea[:].rearrange("p (h d) -> p h d", h=16), AX.X)
                assert c.sV.inc(ins) == tV(i, 18)
                ve.drain()
                ins = ve.tensor_scalar_max(den_ch[:], den_ch[:], EPS)
                assert c.sV.inc(ins) == tV(i, 19)
                ve.drain()
                ins = ve.reciprocal(rden_ch[:], den_ch[:])
                assert c.sV.inc(ins) == tV(i, 20)
                ve.drain()
                ve.wait_ge(c.sS.h, tS(i, 8))
                ins = ve.tensor_mul(ea[:], q_t[:], kvc_t[:])
                assert c.sV.inc(ins) == tV(i, 21)
                if i >= 1:
                    ve.wait_ge(c.sMM.h, tM(i - 1, 15))
                ins = ve.tensor_tensor(y_t[:].rearrange("p (h d) -> p h d", h=16),
                                       ea[:].rearrange("p (h d) -> p h d", h=16),
                                       rden_ch[:, :, None].broadcast_to([128, 16, 64]),
                                       ALU.mult)
                assert c.sV.inc(ins) == tV(i, 22)
                for cs in range(2):
                    ve.wait_ge(c.sMM.h, tM(i, 16 + cs))
                    ins = ve.tensor_add(c.xres[:, i, cs * 512:(cs + 1) * 512],
                                        c.xres[:, i, cs * 512:(cs + 1) * 512],
                                        ps_pr[cs][:])
                    assert c.sV.inc(ins) == tV(i, 23 + cs)

    exp.close()
    c.stream_deps = [(c.sV, c.sV.v)]
    c.fence_deps = c.barrier_tokens()
    ex.close()


def emit_l1ffn(c, w, dram, tag):
    nc = c.nc
    emit_ln(c, tag)
    fence = list(c.fence_deps)
    NE, FF = 8, 4096
    FC = FF // 128
    NSPAN = 2
    ex = ExitStack()
    sb = lambda name, shape, dt: ex.enter_context(nc.sbuf_tensor(f"{tag}_{name}", shape, dt))
    wr_sb = sb("wr", [128, DCH, NE], BF16)
    br_row = sb("br", [1, NE], BF16)
    b1_cols = sb("b1c", [128, NE * FC], F32)
    b2_sb = sb("b2", [8, D], BF16)
    lgr = sb("lgr", [128, TCH, NE], F32)
    Xs = sb("Xs", [128, TCH, NE], F32)
    diff = sb("diff", [128, TCH, NE], F32)
    sq = sb("sq", [128, TCH, NE], F32)
    gates = sb("gates", [128, TCH, NE], F32)
    gates_bf = sb("gatesb", [128, TCH, NE], BF16)
    tau_lo = sb("taulo", [128, TCH], F32)
    tau_m = sb("taum", [128, TCH], F32)
    f_lo = sb("flo", [128, TCH], F32)
    fm = sb("fm", [128, TCH], F32)
    prod = sb("prod", [128, TCH], F32)
    maskb = sb("maskb", [128, TCH], mybir.dt.uint8)
    gT_sb = sb("gT", [8, TCH, 128], BF16)
    h_sb = sb("h", [128, FC, 1024], BF16)
    w1rot = [sb(f"w1r{i}", [128, DCH, 256], BF16) for i in range(3)]
    w2rot = [sb(f"w2r{i}", [128, 2, 512], BF16) for i in range(3)]

    exp_r = ExitStack()
    ps_r = [c.psum(exp_r, f"{tag}_psr{i}", [128, NE]) for i in range(2)]
    ps_gt = c.psum(exp_r, f"{tag}_psgt", [8, 128], BF16)

    with nc.Block(f"{tag}_load", no_gpsimd_drain=True) as b:
        @b.gpsimd
        def _(ge):
            _wait(ge, fence)
            c.sW.inc(ge.dma_start(out=wr_sb[:], in_=w["l1f_wrt"]), 16)
            c.sW.inc(ge.dma_start(out=br_row[:], in_=w["l1f_br"]), 16)
            c.sW.inc(ge.dma_start(out=b2_sb[:], in_=w["l1f_b2"]), 16)

        @b.sync
        def _(sy):
            _wait(sy, fence)
            c.sX.inc(sy.dma_start(out=b1_cols[:], in_=w["l1f_b1_cols"]), 16)
    tok_sm = c.sW.v
    tok_b1 = c.sX.v

    sMM0, sV0, sS0 = c.sMM.v, c.sV.v, c.sS.v
    rmm = lambda i: sMM0 + i + 1
    rev = lambda i: sV0 + i + 1
    with nc.Block(f"{tag}_router", no_gpsimd_drain=True) as b:
        @b.tensor
        def _(te):
            _wait(te, fence)
            te.wait_ge(c.sW.h, tok_sm)
            te.wait_ge(c.sS.h, c.tok_xlnT[1])
            for i in range(TCH):
                if i >= 2:
                    te.wait_ge(c.sV.h, rev(i - 2))
                for dc in range(DCH):
                    te.matmul(ps_r[i % 2][:], c.xlnT[:, dc, i * 128:(i + 1) * 128],
                              wr_sb[:, dc, :], start=(dc == 0), stop=False)
                ins = te.matmul(ps_r[i % 2][:], c.ones[0:1, :], br_row[:],
                                start=False, stop=True)
                assert c.sMM.inc(ins) == rmm(i)

        @b.vector
        def _(ve):
            for i in range(TCH):
                ve.wait_ge(c.sMM.h, rmm(i))
                ins = ve.tensor_copy(out=lgr[:, i, :], in_=ps_r[i % 2][:])
                assert c.sV.inc(ins) == rev(i)
            bcast = lambda t: t[:, :, None].broadcast_to([128, TCH, NE])
            ve.drain()
            ve.tensor_scalar_mul(Xs[:], lgr[:], 0.5)
            ve.drain()
            ve.reduce_max(tau_lo[:, :, None], Xs[:], AX.X)
            ve.drain()
            ve.tensor_scalar_add(tau_lo[:], tau_lo[:], -1.0)
            ve.drain()
            ve.tensor_tensor(diff[:], Xs[:], bcast(tau_lo), ALU.subtract)
            ve.scalar_tensor_tensor(sq[:], diff[:], 0.0, diff[:], ALU.max, ALU.mult)
            ve.drain()
            ve.reduce_sum(f_lo[:, :, None], sq[:], AX.X)
            ve.drain()
            ve.tensor_scalar_add(f_lo[:], f_lo[:], -1.0)
            ve.drain()
            dmc = float(1.0 - (1.0 / NE) ** 0.5)
            for it in range(25):
                dmi = dmc * (0.5 ** (it + 1))
                ve.tensor_scalar_add(tau_m[:], tau_lo[:], dmi)
                ve.drain()
                ve.tensor_tensor(diff[:], Xs[:], bcast(tau_m), ALU.subtract)
                ve.scalar_tensor_tensor(sq[:], diff[:], 0.0, diff[:], ALU.max, ALU.mult)
                ve.drain()
                ve.reduce_sum(fm[:, :, None], sq[:], AX.X)
                ve.drain()
                ve.tensor_scalar_add(fm[:], fm[:], -1.0)
                ve.drain()
                ve.tensor_mul(prod[:], fm[:], f_lo[:])
                ve.drain()
                ve.tensor_scalar(maskb[:], prod[:], 0.0, None, ALU.is_ge)
                ve.drain()
                ve.copy_predicated(tau_lo[:], maskb[:], tau_m[:])
                ve.drain()
            ve.tensor_tensor(diff[:], Xs[:], bcast(tau_m), ALU.subtract)
            ve.scalar_tensor_tensor(sq[:], diff[:], 0.0, diff[:], ALU.max, ALU.mult)
            ve.drain()
            ve.reduce_sum(fm[:, :, None], sq[:], AX.X)
            ve.drain()
            ve.reciprocal(fm[:], fm[:])
            ve.drain()
            ve.tensor_tensor(gates[:], sq[:], bcast(fm), ALU.mult)
            ve.drain()
            ve.tensor_scalar(diff[:], gates[:], 1e-9, None, ALU.is_gt)
            ve.drain()
            ve.tensor_mul(gates[:], gates[:], diff[:])
            ve.drain()
            ins = ve.tensor_copy(out=gates_bf[:], in_=gates[:])
            c.sV.inc(ins)
    tok_gates = c.sV.v

    sMM0, sS0 = c.sMM.v, c.sS.v
    with nc.Block(f"{tag}_gt", no_gpsimd_drain=True) as b:
        @b.tensor
        def _(te):
            te.wait_ge(c.sV.h, tok_gates)
            for i in range(TCH):
                if i >= 1:
                    te.wait_ge(c.sS.h, sS0 + i)
                ins = te.transpose(ps_gt[:], gates_bf[:, i, :], c.ident[:])
                assert c.sMM.inc(ins) == sMM0 + i + 1

        @b.scalar
        def _(se):
            for i in range(TCH):
                se.wait_ge(c.sMM.h, sMM0 + i + 1)
                ins = se.copy(out=gT_sb[:, i, :], in_=ps_gt[:])
                assert c.sS.inc(ins) == sS0 + i + 1
    tok_gT = c.sS.v
    exp_r.close()
    exp = ExitStack()
    ps_hy = [c.psum(exp, f"{tag}_pshy{i}", [128, 512]) for i in range(8)]

    w1 = w["l1f_w1"]
    w2 = w["l1f_w2"]
    sMM0, sS0, sV0, sW0 = c.sMM.v, c.sS.v, c.sV.v, c.sW.v
    NT_e, NS_e, NV_e, NW_e = 144, 64, 16, (16 + 32) * 16
    tMb = lambda spe: sMM0 + spe * NT_e
    hmmt = lambda spe, fc, tt: tMb(spe) + fc * 2 + tt + 1
    yfct = lambda spe, dd, fc: tMb(spe) + 64 + dd * 40 + fc + 1
    ybct = lambda spe, dd, tcol: tMb(spe) + 64 + dd * 40 + 32 + tcol + 1
    hevt = lambda spe, g: sS0 + spe * NS_e + g + 1
    yvt = lambda spe, dd, tcol: sV0 + spe * NV_e + dd * 8 + tcol + 1
    w1t = lambda spe, fc: sW0 + spe * NW_e + (fc // 2 + 1) * 16
    w2t = lambda spe, dd, fc: sW0 + spe * NW_e + 16 * 16 + ((dd * 32 + fc) // 2 + 1) * 16

    with nc.Block(f"{tag}_experts", no_gpsimd_drain=True) as b:
        @b.gpsimd
        def _(ge):
            for spe in range(NSPAN * NE):
                sp, e = divmod(spe, NE)
                src1 = w1[e].rearrange("(dc p) f -> p dc f", p=128)
                for fp in range(FC // 2):
                    fc = 2 * fp
                    if fp >= 3:
                        ge.wait_ge(c.sMM.h, hmmt(spe, (fc - 6) * 2 // 2 * 2 + 2 * 2 - 1)
                                   ) if False else ge.wait_ge(c.sMM.h, hmmt(spe, (fp - 3) * 2 + 1, 1))
                    elif spe > 0:
                        ge.wait_ge(c.sMM.h, hmmt(spe - 1, (fp + 13) * 2 + 1, 1))
                    ins = ge.dma_start(out=w1rot[fp % 3][:],
                                       in_=src1[:, :, fc * 128:(fc + 2) * 128])
                    assert c.sW.inc(ins, 16) == w1t(spe, fc)
                src2 = w2[e].rearrange("(fc p) d -> p fc d", p=128)
                for dd in range(2):
                    for fp in range(FC // 2):
                        fc = 2 * fp
                        k = dd * 16 + fp
                        if k >= 3:
                            pd, pf = divmod(k - 3, 16)
                            ge.wait_ge(c.sMM.h, yfct(spe, pd, 2 * pf + 1))
                        elif spe > 0:
                            pd, pf = divmod(k + 29, 16)
                            ge.wait_ge(c.sMM.h, yfct(spe - 1, pd, 2 * pf + 1))
                        ins = ge.dma_start(out=w2rot[k % 3][:],
                                           in_=src2[:, fc:fc + 2, dd * 512:(dd + 1) * 512])
                        assert c.sW.inc(ins, 16) == w2t(spe, dd, fc)

        @b.tensor
        def _(te):
            for spe in range(NSPAN * NE):
                sp, e = divmod(spe, NE)
                t0 = sp * 8
                if spe == 0:
                    te.wait_ge(c.sS.h, tok_gT)
                    te.wait_ge(c.sW.h, tok_sm)
                elif True:
                    te.wait_ge(c.sV.h, yvt(spe - 1, 1, 7))
                for fc in range(FC):
                    te.wait_ge(c.sW.h, w1t(spe, fc))
                    for tt in range(2):
                        g = fc * 2 + tt
                        if g >= 2:
                            te.wait_ge(c.sS.h, hevt(spe, g - 2))
                        elif spe > 0:
                            te.wait_ge(c.sS.h, hevt(spe - 1, 62 + g))
                        ins = None
                        for dc in range(DCH):
                            ins = te.matmul(ps_hy[g % 2][:],
                                            w1rot[(fc // 2) % 3][:, dc, (fc % 2) * 128:(fc % 2) * 128 + 128],
                                            c.xlnT[:, dc, sp * 1024 + tt * 512:sp * 1024 + (tt + 1) * 512],
                                            start=(dc == 0), stop=(dc == DCH - 1))
                        assert c.sMM.inc(ins) == hmmt(spe, fc, tt)
                te.wait_ge(c.sS.h, hevt(spe, 63))
                for dd in range(2):
                    for fc in range(FC):
                        te.wait_ge(c.sW.h, w2t(spe, dd, fc))
                        if dd == 1 and fc == 0:
                            for tcol in range(8):
                                te.wait_ge(c.sV.h, yvt(spe, 0, tcol))
                        ins = None
                        for tcol in range(8):
                            ins = te.matmul(ps_hy[tcol][:],
                                            h_sb[:, fc, tcol * 128:(tcol + 1) * 128],
                                            w2rot[((dd * 32 + fc) // 2) % 3][:, fc % 2, :],
                                            start=(fc == 0), stop=False)
                        assert c.sMM.inc(ins) == yfct(spe, dd, fc)
                    for tcol in range(8):
                        ins = te.matmul(ps_hy[tcol][:], gT_sb[:, t0 + tcol, :],
                                        b2_sb[:, dd * 512:(dd + 1) * 512],
                                        start=False, stop=True)
                        assert c.sMM.inc(ins) == ybct(spe, dd, tcol)

        @b.scalar
        def _(se):
            se.wait_ge(c.sX.h, tok_b1)
            for spe in range(NSPAN * NE):
                sp, e = divmod(spe, NE)
                if spe > 0:
                    se.wait_ge(c.sMM.h, yfct(spe - 1, 1, 31))
                for g in range(64):
                    fc, tt = g // 2, g % 2
                    se.wait_ge(c.sMM.h, hmmt(spe, fc, tt))
                    ins = se.activation(h_sb[:, fc, tt * 512:(tt + 1) * 512],
                                        ps_hy[g % 2][:], AF.Relu,
                                        bias=b1_cols[:, e * FC + fc:e * FC + fc + 1])
                    assert c.sS.inc(ins) == hevt(spe, g)

        @b.vector
        def _(ve):
            for spe in range(NSPAN * NE):
                sp, e = divmod(spe, NE)
                t0 = sp * 8
                for dd in range(2):
                    for tcol in range(8):
                        ve.wait_ge(c.sMM.h, ybct(spe, dd, tcol))
                        ins = ve.scalar_tensor_tensor(
                            c.xres[:, t0 + tcol, dd * 512:(dd + 1) * 512],
                            ps_hy[tcol][:], gates[:, t0 + tcol, e:e + 1],
                            c.xres[:, t0 + tcol, dd * 512:(dd + 1) * 512],
                            ALU.mult, ALU.add)
                        assert c.sV.inc(ins) == yvt(spe, dd, tcol)

    exp.close()
    c.stream_deps = [(c.sV, c.sV.v)]
    c.fence_deps = c.barrier_tokens()
    ex.close()


def emit_l1b(c, w, dram, tag="s6"):
    nc = c.nc
    emit_ln(c, tag)
    fence = list(c.fence_deps)
    NH = 16
    ex = ExitStack()
    sb = lambda name, shape, dt: ex.enter_context(nc.sbuf_tensor(f"{tag}_{name}", shape, dt))
    K_fm = sb("kfm", [128, DCH, T], BF16)
    V_sb = sb("vsb", [128, TCH, NH, 65], BF16)
    hind = sb("hind", [128, DCH, NH], BF16)
    bq_cols = sb("bqc", [128, DCH], F32)
    bk_cols = sb("bkc", [128, DCH], F32)
    bqs_cols = sb("bqsc", [128, DCH], F32)
    bv_row = sb("bvr", [1, D], BF16)
    bo_row = sb("bor", [1, D], BF16)
    ascale = sb("asc", [128, 1], F32)
    sq1 = sb("sq1", [128, 1024], BF16)
    nk2m = sb("nk2m", [16, 1], F32)
    sk = sb("sk", [16, 1], F32)
    cq2 = sb("cq2", [16, 1], F32)
    cq = sb("cq", [16, 1], F32)
    ch_bf = sb("chbf", [16, 1], BF16)
    chrow = sb("chrow", [1, 16], BF16)
    cneg = sb("cneg", [128, 16], F32)
    rd_row = sb("rdrow", [1, 512], BF16)
    rb = sb("rbb", [64, 512], BF16)

    ex2 = ExitStack()
    wkv_sb = ex2.enter_context(nc.sbuf_tensor(f"{tag}_wkv", [128, DCH, D], BF16))

    with nc.Block(f"{tag}_load", no_gpsimd_drain=True) as b:
        @b.gpsimd
        def _(ge):
            _wait(ge, fence)
            src = w["l1b_wkt"].rearrange("(dc p) e -> p dc e", p=128)
            for dc in range(DCH):
                c.sW.inc(ge.dma_start(out=wkv_sb[:, dc, :], in_=src[:, dc, :]), 16)
            c.sW.inc(ge.dma_start(out=bv_row[:], in_=w["l1b_bv"]), 16)
            c.sW.inc(ge.dma_start(out=bo_row[:], in_=w["l1b_bo"]), 16)
            c.sW.inc(ge.dma_start(out=hind[:], in_=w["l1b_hind"]), 16)

        @b.sync
        def _(sy):
            _wait(sy, fence)
            c.sX.inc(sy.dma_start(out=bq_cols[:], in_=w["l1b_bq_cols"]), 16)
            c.sX.inc(sy.dma_start(out=bk_cols[:], in_=w["l1b_bk_cols"]), 16)
            c.sX.inc(sy.dma_start(out=ascale[:], in_=w["l1b_ascale_col"]), 16)
    tok_w1 = c.sW.v
    tok_cols = c.sX.v

    exp1 = ExitStack()
    ps_kv = [c.psum(exp1, f"{tag}_pskv{i}", [128, 512]) for i in range(4)]
    ps_nk = [c.psum(exp1, f"{tag}_psnk{i}", [16, 512]) for i in range(4)]

    sMM0, sS0, sV0 = c.sMM.v, c.sS.v, c.sV.v
    kmm = lambda g: sMM0 + g + 1
    kev = lambda g: sV0 + 2 + g + 1
    vmm = lambda g: sMM0 + 32 + g + 1
    vev = lambda g: sS0 + g + 1
    sqe = lambda ec, hf: sS0 + 32 + ec * 2 + hf + 1
    nkmm = lambda ec, hf: sMM0 + 64 + ec * 2 + hf + 1

    tok_wv = c.sW.v + 8 * 16
    with nc.Block(f"{tag}_p1", no_gpsimd_drain=True) as b:
        @b.gpsimd
        def _(ge):
            ge.wait_ge(c.sMM.h, kmm(31))
            src = w["l1b_wvt"].rearrange("(dc p) e -> p dc e", p=128)
            for dc in range(DCH):
                c.sW.inc(ge.dma_start(out=wkv_sb[:, dc, :], in_=src[:, dc, :]), 16)
            assert c.sW.v == tok_wv

        @b.tensor
        def _(te):
            _wait(te, fence)
            te.wait_ge(c.sW.h, tok_w1)
            te.wait_ge(c.sS.h, c.tok_xlnT[1])
            for g in range(32):
                ec, tt = g // 4, g % 4
                if g >= 2:
                    te.wait_ge(c.sV.h, kev(g - 2))
                ins = None
                for dc in range(DCH):
                    ins = te.matmul(ps_kv[g % 2][:], wkv_sb[:, dc, ec * 128:(ec + 1) * 128],
                                    c.xlnT[:, dc, tt * 512:(tt + 1) * 512],
                                    start=(dc == 0), stop=(dc == DCH - 1))
                assert c.sMM.inc(ins) == kmm(g)
            te.wait_ge(c.sW.h, tok_wv)
            for g in range(32):
                i, ee = g // 2, g % 2
                if g >= 2:
                    te.wait_ge(c.sS.h, vev(g - 2))
                for dc in range(DCH):
                    te.matmul(ps_kv[2 + g % 2][:], c.xlnT[:, dc, i * 128:(i + 1) * 128],
                              wkv_sb[:, dc, ee * 512:(ee + 1) * 512],
                              start=(dc == 0), stop=False)
                ins = te.matmul(ps_kv[2 + g % 2][:], c.ones[0:1, :],
                                bv_row[:, ee * 512:(ee + 1) * 512], start=False, stop=True)
                assert c.sMM.inc(ins) == vmm(g)
            for ec in range(DCH):
                for hf in range(2):
                    te.wait_ge(c.sS.h, sqe(ec, hf))
                    ins = None
                    for tt in range(2):
                        ins = te.matmul(ps_nk[hf * 2 + tt][:], hind[:, ec, :],
                                        sq1[:, tt * 512:(tt + 1) * 512],
                                        start=(ec == 0), stop=(ec == DCH - 1))
                    assert c.sMM.inc(ins) == nkmm(ec, hf)

        @b.vector
        def _(ve):
            ve.wait_ge(c.sX.h, tok_cols)
            ins = ve.tensor_mul(bqs_cols[:], bq_cols[:], ascale[:].broadcast_to([128, DCH]))
            assert c.sV.inc(ins) == sV0 + 1
            ins = ve.memset(V_sb[:, :, :, 64:65], 1.0)
            assert c.sV.inc(ins) == sV0 + 2
            for g in range(32):
                ec, tt = g // 4, g % 4
                ve.wait_ge(c.sMM.h, kmm(g))
                ins = ve.tensor_scalar_add(K_fm[:, ec, tt * 512:(tt + 1) * 512],
                                           ps_kv[g % 2][:], bk_cols[:, ec:ec + 1])
                assert c.sV.inc(ins) == kev(g)
            ve.wait_ge(c.sMM.h, nkmm(DCH - 1, 1))
            for tt in range(4):
                if tt == 0:
                    ve.reduce_max(nk2m[:], ps_nk[tt][:], AX.X)
                else:
                    ve.reduce_max(cq2[:], ps_nk[tt][:], AX.X)
                    ve.drain()
                    ve.tensor_max(nk2m[:], nk2m[:], cq2[:])
                ve.drain()
            ins = ve.tensor_copy(out=sk[:], in_=nk2m[:])
            c.sV.inc(ins)

        @b.scalar
        def _(se):
            for g in range(32):
                i, ee = g // 2, g % 2
                se.wait_ge(c.sMM.h, vmm(g))
                dst = V_sb[:, i, ee * 8:(ee + 1) * 8, 0:64]
                ins = se.copy(out=dst,
                              in_=ps_kv[2 + g % 2][:].rearrange("p (h d) -> p h d", h=8))
                assert c.sS.inc(ins) == vev(g)
            for ec in range(DCH):
                se.wait_ge(c.sV.h, kev(ec * 4 + 3))
                for hf in range(2):
                    if ec * 2 + hf >= 2:
                        pe, ph = divmod(ec * 2 + hf - 2, 2)
                        se.wait_ge(c.sMM.h, nkmm(pe, ph))
                    ins = se.activation(sq1[:], K_fm[:, ec, hf * 1024:(hf + 1) * 1024],
                                        AF.Square)
                    assert c.sS.inc(ins) == sqe(ec, hf)
    tok_p1v = c.sV.v

    with nc.Block(f"{tag}_sk", no_gpsimd_drain=True) as b:
        @b.scalar
        def _(se):
            se.wait_ge(c.sV.h, tok_p1v)
            se.drain()
            ins = se.activation(sk[:], sk[:], AF.Sqrt)
            c.sS.inc(ins)
    tok_sk = c.sS.v
    exp1.close()
    ex2.close()

    ex3 = ExitStack()
    sb3 = lambda name, shape, dt: ex3.enter_context(nc.sbuf_tensor(f"{tag}_{name}", shape, dt))
    mask_sb = sb3("masks", [128, 4, 512], BF16)
    wo_half = sb3("woh", [128, DCH, 512], BF16)
    Q_ch = sb3("qch", [128, DCH, 512], BF16)
    attn = sb3("attn", [128, DCH, 512], BF16)
    expT2 = sb3("expT2", [128, 1024], BF16)
    expT = [expT2[:, 0:512], expT2[:, 512:1024]]
    wq_rot = [sb3(f"wqr{i}", [128, 128], BF16) for i in range(3)]

    exp2 = ExitStack()
    ps_s2 = c.psum(exp2, f"{tag}_pss", [128, 1024])
    ps_s = [ps_s2[:, 0:512], ps_s2[:, 512:1024]]
    ps_a = [c.psum(exp2, f"{tag}_psa{i}", [65, 512]) for i in range(2)]
    ps_o = [c.psum(exp2, f"{tag}_pso{i}", [128, 512]) for i in range(2)]
    ps_n = c.psum(exp2, f"{tag}_psn", [16, 512])
    ps_cn = c.psum(exp2, f"{tag}_pscn", [128, 16])

    with nc.Block(f"{tag}_mload", no_gpsimd_drain=True) as b:
        @b.gpsimd
        def _(ge):
            ge.wait_ge(c.sMM.h, c.sMM.v)
            c.sW.inc(ge.dma_start(out=mask_sb[:], in_=w["l1b_masks"]), 16)
    tok_mask = c.sW.v

    NT_j = 64 + 8 + 1 + NH * 33 + 8
    NS_j = 8 + 1 + NH * 8
    NV_j = 8 + 1 + 1 + 1 + NH * 7 + 8
    NW_j = 64 * 16 + 2 * 8 * 16
    NX_j = 32
    sMM1, sS1, sV1, sW1, sX1 = c.sMM.v, c.sS.v, c.sV.v, c.sW.v, c.sX.v
    qmk = lambda j, ec, dc: sMM1 + j * NT_j + ec * 8 + dc + 1
    qmm = lambda j, ec: sMM1 + j * NT_j + ec * 8 + 8
    nmm = lambda j, ec: sMM1 + j * NT_j + 64 + ec + 1
    cbm = lambda j: sMM1 + j * NT_j + 73
    _spos = lambda kt: 1 if kt == 0 else 2 * kt
    _apos = lambda kt: 3 + 2 * kt if kt <= 13 else 17 + kt
    smm = lambda j, h, kt: sMM1 + j * NT_j + 73 + h * 33 + _spos(kt)
    amk = lambda j, h, kt: sMM1 + j * NT_j + 73 + h * 33 + _apos(kt)
    bmm = lambda j, h: sMM1 + j * NT_j + 73 + h * 33 + 33
    omm = lambda j, g: sMM1 + j * NT_j + 73 + NH * 33 + g + 1
    sqq = lambda j, ec: sS1 + j * NS_j + ec + 1
    cqs = lambda j: sS1 + j * NS_j + 9
    eev = lambda j, h, p: sS1 + j * NS_j + 9 + h * 8 + p + 1
    qev = lambda j, ec: sV1 + j * NV_j + ec + 1
    nqv = lambda j: sV1 + j * NV_j + 9
    chv = lambda j: sV1 + j * NV_j + 10
    cnv = lambda j: sV1 + j * NV_j + 11
    mskv = lambda j, h, oi: sV1 + j * NV_j + 11 + h * 7 + oi + 1
    rcp = lambda j, h: sV1 + j * NV_j + 11 + h * 7 + 5
    rbc = lambda j, h: sV1 + j * NV_j + 11 + h * 7 + 6
    atv = lambda j, h: sV1 + j * NV_j + 11 + h * 7 + 7
    oev = lambda j, g: sV1 + j * NV_j + 11 + NH * 7 + g + 1
    wqt = lambda j, blk: sW1 + j * NW_j + (blk + 1) * 16
    wot = lambda j, dd: sW1 + j * NW_j + 64 * 16 + (dd + 1) * 8 * 16
    xrow = lambda j, n: sX1 + j * NX_j + n * 16

    with nc.Block(f"{tag}_qloop", no_gpsimd_drain=True) as b:
        @b.gpsimd
        def _(ge):
            for j in range(4):
                src = w["l1b_wqt"].rearrange("(dc p) e -> p dc e", p=128)
                for blk in range(64):
                    ec, dc = divmod(blk, 8)
                    if blk >= 3:
                        pe_, pd_ = divmod(blk - 3, 8)
                        ge.wait_ge(c.sMM.h, qmk(j, pe_, pd_))
                    elif j > 0:
                        pe_, pd_ = divmod(blk + 61, 8)
                        ge.wait_ge(c.sMM.h, qmk(j - 1, pe_, pd_))
                    ins = ge.dma_start(out=wq_rot[blk % 3][:],
                                       in_=src[:, dc, ec * 128:(ec + 1) * 128])
                    assert c.sW.inc(ins, 16) == wqt(j, blk)
                srco = w["l1b_wot"].rearrange("(dc p) e -> p dc e", p=128)
                for dd in range(2):
                    if j > 0 or dd > 0:
                        pj, pdd = (j, 0) if dd == 1 else (j - 1, 1)
                        ge.wait_ge(c.sMM.h, omm(pj, pdd * 4 + 3))
                    for dc in range(DCH):
                        c.sW.inc(ge.dma_start(out=wo_half[:, dc, :],
                                              in_=srco[:, dc, dd * 512:(dd + 1) * 512]), 16)
                    assert c.sW.v == wot(j, dd)

        @b.sync
        def _(sy):
            for j in range(4):
                sy.wait_ge(c.sV.h, chv(j))
                ins = sy.dma_start(out=dram["l1b_scr"][:], in_=ch_bf[:])
                assert c.sX.inc(ins, 16) == xrow(j, 1)
                sy.wait_ge(c.sX.h, xrow(j, 1))
                if j > 0:
                    sy.wait_ge(c.sMM.h, cbm(j - 1))
                ins = sy.dma_start(out=chrow[:], in_=dram["l1b_scr"].rearrange("s one -> one s"))
                assert c.sX.inc(ins, 16) == xrow(j, 2)

        @b.tensor
        def _(te):
            for j in range(4):
                if j == 0:
                    te.wait_ge(c.sS.h, tok_sk)
                    te.wait_ge(c.sW.h, tok_mask)
                for ec in range(DCH):
                    if ec >= 2:
                        te.wait_ge(c.sV.h, qev(j, ec - 2))
                    elif j > 0:
                        te.wait_ge(c.sS.h, eev(j - 1, 15, 7))
                    for dc in range(DCH):
                        blk = ec * 8 + dc
                        te.wait_ge(c.sW.h, wqt(j, blk))
                        ins = te.matmul(ps_s[ec % 2], wq_rot[blk % 3][:],
                                        c.xlnT[:, dc, j * 512:(j + 1) * 512],
                                        start=(dc == 0), stop=(dc == DCH - 1))
                        assert c.sMM.inc(ins) == qmk(j, ec, dc)
                for ec in range(DCH):
                    te.wait_ge(c.sS.h, sqq(j, ec))
                    if ec == 0 and j > 0:
                        te.wait_ge(c.sV.h, nqv(j - 1))
                    ins = te.matmul(ps_n[:], hind[:, ec, :],
                                    sq1[:, (ec % 2) * 512:(ec % 2) * 512 + 512],
                                    start=(ec == 0), stop=(ec == DCH - 1))
                    assert c.sMM.inc(ins) == nmm(j, ec)
                te.wait_ge(c.sX.h, xrow(j, 2))
                if j > 0:
                    te.wait_ge(c.sV.h, cnv(j - 1))
                ins = te.matmul(ps_cn[:], c.ones[0:1, :], chrow[:], start=True, stop=True)
                assert c.sMM.inc(ins) == cbm(j)
                for h in range(NH):
                    base = (h % 2) * 64
                    hc = h // 2

                    def _scores(kt, h=h, base=base, hc=hc, j=j):
                        if kt >= 2:
                            te.wait_ge(c.sS.h, eev(j, h, (kt - 2) // 2))
                        elif h > 0:
                            te.wait_ge(c.sS.h, eev(j, h - 1, 7))
                        else:
                            te.wait_ge(c.sV.h, qev(j, 7))
                        ins = te.matmul(ps_s[kt % 2],
                                        K_fm[base:base + 64, hc, kt * 128:(kt + 1) * 128],
                                        Q_ch[base:base + 64, hc, :],
                                        start=True, stop=True)
                        assert c.sMM.inc(ins) == smm(j, h, kt)

                    def _att(kt, h=h, j=j):
                        if kt == 0:
                            if h >= 2:
                                te.wait_ge(c.sV.h, atv(j, h - 2))
                            elif j > 0:
                                te.wait_ge(c.sV.h, atv(j - 1, 14 + h))
                        oi = kt - 4 * j
                        if 0 <= oi < 4:
                            te.wait_ge(c.sV.h, mskv(j, h, oi))
                        else:
                            te.wait_ge(c.sS.h, eev(j, h, kt // 2))
                        ins = te.matmul(ps_a[h % 2][:], V_sb[:, kt, h, :], expT[kt % 2],
                                        start=(kt == 0), stop=(kt == TCH - 1))
                        assert c.sMM.inc(ins) == amk(j, h, kt)

                    _scores(0)
                    _scores(1)
                    for kt in range(14):
                        _att(kt)
                        _scores(kt + 2)
                    _att(14)
                    _att(15)
                    te.wait_ge(c.sV.h, rcp(j, h))
                    if h == 0 and j > 0:
                        te.wait_ge(c.sV.h, oev(j - 1, 7))
                    ins = te.matmul(ps_o[0][0:64, :], c.ones[0:1, 0:64], rd_row[:],
                                    start=True, stop=True)
                    assert c.sMM.inc(ins) == bmm(j, h)
                for g in range(8):
                    dd, tcol = g // 4, g % 4
                    te.wait_ge(c.sW.h, wot(j, dd))
                    if g >= 2:
                        te.wait_ge(c.sV.h, oev(j, g - 2))
                    else:
                        te.wait_ge(c.sV.h, atv(j, 15))
                    for dc in range(DCH):
                        te.matmul(ps_o[g % 2][:], attn[:, dc, tcol * 128:(tcol + 1) * 128],
                                  wo_half[:, dc, :],
                                  start=(dc == 0), stop=False)
                    ins = te.matmul(ps_o[g % 2][:], c.ones[0:1, :],
                                    bo_row[:, dd * 512:(dd + 1) * 512], start=False, stop=True)
                    assert c.sMM.inc(ins) == omm(j, g)

        @b.scalar
        def _(se):
            for j in range(4):
                for ec in range(DCH):
                    se.wait_ge(c.sV.h, qev(j, ec))
                    if ec >= 2:
                        se.wait_ge(c.sMM.h, nmm(j, ec - 2))
                    elif j > 0:
                        se.wait_ge(c.sMM.h, nmm(j - 1, ec + 6))
                    else:
                        se.wait_ge(c.sMM.h, nkmm(DCH - 1, 1))
                    ins = se.activation(sq1[:, (ec % 2) * 512:(ec % 2) * 512 + 512],
                                        Q_ch[:, ec, :], AF.Square)
                    assert c.sS.inc(ins) == sqq(j, ec)
                se.wait_ge(c.sV.h, nqv(j))
                se.drain()
                ins = se.activation(cq[:], cq2[:], AF.Sqrt)
                assert c.sS.inc(ins) == cqs(j)
                for h in range(NH):
                    for p in range(8):
                        se.wait_ge(c.sMM.h, smm(j, h, 2 * p + 1))
                        if h == 0 and p == 0:
                            se.wait_ge(c.sV.h, cnv(j))
                        if p >= 1:
                            se.wait_ge(c.sMM.h, amk(j, h, 2 * p - 1))
                        elif h > 0:
                            se.wait_ge(c.sMM.h, amk(j, h - 1, 15))
                        elif j > 0:
                            se.wait_ge(c.sMM.h, amk(j - 1, 15, 15))
                        ins = se.activation(expT2[:], ps_s2[:], AF.Exp,
                                            bias=cneg[:, h:h + 1])
                        assert c.sS.inc(ins) == eev(j, h, p)

        @b.vector
        def _(ve):
            for j in range(4):
                for ec in range(DCH):
                    ve.wait_ge(c.sMM.h, qmm(j, ec))
                    if j > 0:
                        ve.wait_ge(c.sMM.h, smm(j - 1, 2 * ec + 1, 15))
                    ins = ve.tensor_scalar(Q_ch[:, ec, :], ps_s[ec % 2],
                                           ascale[:], bqs_cols[:, ec:ec + 1],
                                           ALU.mult, ALU.add)
                    assert c.sV.inc(ins) == qev(j, ec)
                ve.wait_ge(c.sMM.h, nmm(j, 7))
                ins = ve.reduce_max(cq2[:], ps_n[:], AX.X)
                assert c.sV.inc(ins) == nqv(j)
                ve.drain()
                ve.wait_ge(c.sS.h, cqs(j))
                if j > 0:
                    ve.wait_ge(c.sX.h, xrow(j - 1, 1))
                ins = ve.tensor_mul(ch_bf[:], cq[:], sk[:])
                assert c.sV.inc(ins) == chv(j)
                ve.wait_ge(c.sMM.h, cbm(j))
                if j > 0:
                    ve.wait_ge(c.sS.h, eev(j - 1, 15, 7))
                ins = ve.tensor_scalar_mul(cneg[:], ps_cn[:], -1.0)
                assert c.sV.inc(ins) == cnv(j)
                for h in range(NH):
                    for oi in range(4):
                        kt = 4 * j + oi
                        ve.wait_ge(c.sS.h, eev(j, h, kt // 2))
                        ins = ve.tensor_mul(expT[kt % 2], expT[kt % 2],
                                            mask_sb[:, oi, :])
                        assert c.sV.inc(ins) == mskv(j, h, oi)
                    ve.wait_ge(c.sMM.h, amk(j, h, 15))
                    if h > 0:
                        ve.wait_ge(c.sMM.h, bmm(j, h - 1))
                    elif j > 0:
                        ve.wait_ge(c.sMM.h, bmm(j - 1, 15))
                    with nc.allow_low_precision("denom recip feeds a bf16 matmul operand"):
                        ins = ve.reciprocal(rd_row[:], ps_a[h % 2][64:65, :])
                    assert c.sV.inc(ins) == rcp(j, h)
                    ve.drain()
                    ve.wait_ge(c.sMM.h, bmm(j, h))
                    ins = ve.tensor_copy(out=rb[:], in_=ps_o[0][0:64, :])
                    assert c.sV.inc(ins) == rbc(j, h)
                    if h == 0 and j > 0:
                        ve.wait_ge(c.sMM.h, omm(j - 1, 7))
                    ins = ve.tensor_tensor(attn[(h % 2) * 64:(h % 2) * 64 + 64, h // 2, :],
                                           ps_a[h % 2][0:64, :], rb[:], ALU.mult)
                    assert c.sV.inc(ins) == atv(j, h)
                for g in range(8):
                    dd, tcol = g // 4, g % 4
                    ve.wait_ge(c.sMM.h, omm(j, g))
                    ins = ve.tensor_add(c.xres[:, j * 4 + tcol, dd * 512:(dd + 1) * 512],
                                        c.xres[:, j * 4 + tcol, dd * 512:(dd + 1) * 512],
                                        ps_o[g % 2][:])
                    assert c.sV.inc(ins) == oev(j, g)

    exp2.close()
    ex3.close()
    c.stream_deps = [(c.sV, c.sV.v)]
    c.fence_deps = c.barrier_tokens()
    ex.close()


SUB_WEIGHTS = {
    0: ["l0a_wq", "l0a_wk", "l0a_wv"],
    1: ["l0f_gwt", "l0f_eb", "l0f_w1t", "l0f_w2t"],
    3: ["l0f_gwt", "l0f_eb", "l0f_w1t", "l0f_w2t"],
    2: ["l0b_wqkv", "l0b_bqkv", "l0b_wout", "l0b_bout", "l0b_bq_cols"],
    4: ["l1a_wqkvt", "l1a_bqkv", "l1a_wprojt", "l1a_bproj", "l1a_scale_col"],
    5: ["l1f_wrt", "l1f_br", "l1f_b1_cols", "l1f_b2", "l1f_w1", "l1f_w2"],
    7: ["l1f_wrt", "l1f_br", "l1f_b1_cols", "l1f_b2", "l1f_w1", "l1f_w2"],
    6: ["l1b_wqt", "l1b_wkt", "l1b_wvt", "l1b_wot", "l1b_bq_cols", "l1b_bk_cols",
        "l1b_bv", "l1b_bo", "l1b_ascale_col", "l1b_masks", "l1b_hind"],
}
SUB_CONSTS = {
    0: ["a_cat"],
}
EMITTERS = {}


def _emit_sub0(c, w, dram):
    emit_l0a(c, w, dram)


EMITTERS[0] = _emit_sub0
EMITTERS[1] = lambda c, w, dram: emit_l0ffn(c, w, dram, "s1")
EMITTERS[3] = lambda c, w, dram: emit_l0ffn(c, w, dram, "s3")
EMITTERS[2] = lambda c, w, dram: emit_l0b(c, w, dram, "s2")
EMITTERS[4] = lambda c, w, dram: emit_l1a(c, w, dram, "s4")
EMITTERS[5] = lambda c, w, dram: emit_l1ffn(c, w, dram, "s5")
EMITTERS[7] = lambda c, w, dram: emit_l1ffn(c, w, dram, "s7")
EMITTERS[6] = lambda c, w, dram: emit_l1b(c, w, dram, "s6")

_WSHAPES = {
    "l0a_wq": ([D, D], F32), "l0a_wk": ([D, D], F32), "l0a_wv": ([D, D], F32),
    "l0f_gwt": ([128, DCH, 12], F32), "l0f_eb": ([1, 12], F32),
    "l0f_w1t": ([8, D, 2048], F32), "l0f_w2t": ([8, 2048, D], F32),
    "l0b_wqkv": ([D, 3 * D], F32), "l0b_bqkv": ([1, 3 * D], F32),
    "l0b_wout": ([D, D], F32), "l0b_bout": ([1, D], F32),
    "l0b_bq_cols": ([128, DCH], F32),
    "l1a_wqkvt": ([D, 3 * D], F32), "l1a_bqkv": ([1, 3 * D], F32),
    "l1a_wprojt": ([D, D], F32), "l1a_bproj": ([1, D], F32),
    "l1a_scale_col": ([128, 1], F32),
    "l1f_wrt": ([128, DCH, 8], F32), "l1f_br": ([1, 8], F32),
    "l1f_b1_cols": ([128, 8 * 32], F32), "l1f_b2": ([8, D], F32),
    "l1f_w1": ([8, D, 4096], F32), "l1f_w2": ([8, 4096, D], F32),
    "l1b_wqt": ([D, D], F32), "l1b_wkt": ([D, D], F32),
    "l1b_wvt": ([D, D], F32), "l1b_wot": ([D, D], F32),
    "l1b_bq_cols": ([128, DCH], F32), "l1b_bk_cols": ([128, DCH], F32),
    "l1b_bv": ([1, D], F32), "l1b_bo": ([1, D], F32),
    "l1b_ascale_col": ([128, 1], F32),
    "l1b_masks": ([128, 4, 512], BF16), "l1b_hind": ([128, DCH, 16], BF16),
}
_CSHAPES = {
    "ident": ([128, 128], BF16),
    "a_cat": ([T, T], BF16),
    "triu": ([128, 128], BF16),
}


def _build_launch(subs):
    nc = bass.Bass("TRN2", target_bir_lowering=False)
    x_in = nc.dram_tensor("x_in", [T, D], F32, kind="ExternalInput").ap()
    out = nc.dram_tensor("out", [T, D], F32, kind="ExternalOutput").ap()

    w = {}
    for s in subs:
        for name in SUB_WEIGHTS[s]:
            if name not in w:
                shape, dt = _WSHAPES[name]
                w[name] = nc.dram_tensor("d_" + name, shape, dt, kind="ExternalInput").ap()
    need_consts = {"ident", "triu"}
    for s in subs:
        need_consts.update(SUB_CONSTS.get(s, []))
    for name in sorted(need_consts):
        shape, dt = _CSHAPES[name]
        w[name] = nc.dram_tensor("c_" + name, shape, dt, kind="ExternalInput").ap()

    dram = {"a_cat": w.get("a_cat")}
    if 6 in subs:
        dram["l1b_scr"] = nc.dram_tensor("l1b_scr", [16, 1], BF16).ap()
    if 0 in subs:
        dram["cc_in"] = nc.dram_tensor("cc_in", [T, D], F32).ap()
        dram["cc_out"] = nc.dram_tensor("cc_out", [T, D], F32, addr_space="Shared").ap()

    with ExitStack() as ex:
        c = Ctx(nc, ex)
        with nc.Block("init", no_gpsimd_drain=True) as b:
            @b.sync
            def _(sy):
                src = x_in.rearrange("(i p) d -> p i d", p=128)
                for i0 in range(0, TCH, 4):
                    ins = sy.dma_start(out=c.xres[:, i0:i0 + 4, :], in_=src[:, i0:i0 + 4, :])
                    c.sX.inc(ins, 16)
                c.sX.inc(sy.dma_start(out=c.ident[:], in_=w["ident"]), 16)
                c.sX.inc(sy.dma_start(out=c.triu[:], in_=w["triu"]), 16)

            @b.gpsimd
            def _(ge):
                ins = ge.memset(c.ones[:], 1.0)
                c.sG.inc(ins)
        tok_init = c.sX.v
        c.stream_deps = [(c.sX, tok_init)]

        for s in subs:
            EMITTERS[s](c, w, dram)

        tok_store = c.sX.v + 16 * 4
        with nc.Block("fin") as b:
            @b.sync
            def _(sy):
                _wait(sy, c.stream_deps)
                dst = out.rearrange("(i p) d -> p i d", p=128)
                for i0 in range(0, TCH, 4):
                    ins = sy.dma_start(out=dst[:, i0:i0 + 4, :], in_=c.xres[:, i0:i0 + 4, :])
                    c.sX.inc(ins, 16)
                assert c.sX.v == tok_store

            @b.gpsimd
            def _(ge):
                ge.wait_ge(c.sX.h, tok_store)
    return nc


_NC_CACHE = {}


def _get_launch(subs):
    key = tuple(subs)
    if key not in _NC_CACHE:
        _NC_CACHE[key] = _build_launch(list(subs))
    return _NC_CACHE[key]


def kernel(**inputs):
    consts = _host_consts()
    prep = _prep_weights(inputs)
    x = np.ascontiguousarray(np.asarray(inputs["L0_attn_a_x"], np.float32))
    stream = [np.ascontiguousarray(x[b]) for b in range(NCORE)]

    for subs in GROUPS:
        nc = _get_launch(subs)
        names = set()
        for s in subs:
            names.update(SUB_WEIGHTS[s])
        cnames = {"ident", "triu"}
        for s in subs:
            cnames.update(SUB_CONSTS.get(s, []))
        in_maps = []
        for b in range(NCORE):
            m = {"x_in": stream[b]}
            for n in names:
                m["d_" + n] = prep[n]
            for n in cnames:
                m["c_" + n] = consts[n]
            in_maps.append(m)
        res = run_bass_kernel_spmd(nc, in_maps, core_ids=list(range(NCORE)))
        stream = [np.asarray(res.results[b]["out"]) for b in range(NCORE)]

    return np.stack(stream).astype(np.float32)


# version 05
# speedup vs baseline: 1.0302x; 1.0302x over previous
import numpy as np
import ml_dtypes
from contextlib import ExitStack

import concourse.bass as bass
import concourse.mybir as mybir
from concourse.bass_utils import run_bass_kernel_spmd

F32 = mybir.dt.float32
BF16 = mybir.dt.bfloat16
AF = mybir.ActivationFunctionType
ALU = mybir.AluOpType
AX = mybir.AxisListType
BF = ml_dtypes.bfloat16

T, D = 2048, 1024
TCH = T // 128
DCH = D // 128
NCORE = 8
GAMMA = 0.96875

GROUPS = [[0, 1, 2, 3, 4, 5, 6, 7]]


def _host_consts():
    out = {}
    out["ident"] = np.eye(128, dtype=BF)
    A = np.zeros((T, T), np.float64)
    A[0, 1] = 1.0
    for t in range(1, T):
        j = np.arange(1, t + 1)
        A[t, j] = GAMMA ** (t - j)
    A_cat = np.concatenate([A[0::2].T, A[1::2].T], axis=1)
    out["a_cat"] = np.ascontiguousarray(A_cat.astype(BF))
    out["triu"] = np.ascontiguousarray(np.triu(np.ones((128, 128))).astype(BF))
    return out


def _prep_weights(inp):
    f = lambda a: np.ascontiguousarray(np.asarray(a, np.float32))
    tr = lambda a: np.ascontiguousarray(np.asarray(a, np.float32).T)
    p = {}
    p["l0a_wq"] = f(inp["L0_attn_a_Wq"])
    p["l0a_wk"] = f(inp["L0_attn_a_Wk"])
    p["l0a_wv"] = f(inp["L0_attn_a_Wv"])
    gw = np.asarray(inp["L0_ffn_gate_w"], np.float32)
    p["l0f_gwt"] = np.ascontiguousarray(gw.T.reshape(DCH, 128, 12).transpose(1, 0, 2))
    p["l0f_eb"] = np.ascontiguousarray(np.asarray(inp["L0_ffn_expert_bias"], np.float32)[None, :])
    p["l0f_w1t"] = np.ascontiguousarray(np.asarray(inp["L0_ffn_w1"], np.float32).transpose(0, 2, 1))
    p["l0f_w2t"] = np.ascontiguousarray(np.asarray(inp["L0_ffn_w2"], np.float32).transpose(0, 2, 1))
    p["l0b_wqkv"] = f(inp["L0_attn_b_Wqkv"])
    bqkv = np.asarray(inp["L0_attn_b_bqkv"], np.float32)
    p["l0b_bqkv"] = np.ascontiguousarray(bqkv[None, :])
    p["l0b_wout"] = f(inp["L0_attn_b_Wout"])
    p["l0b_bout"] = np.ascontiguousarray(np.asarray(inp["L0_attn_b_bout"], np.float32)[None, :])
    p["l0b_bq_cols"] = np.ascontiguousarray(bqkv[:D].reshape(DCH, 128).T)
    p["l1a_wqkvt"] = tr(inp["L1_attn_a_Wqkv"])
    p["l1a_bqkv"] = np.ascontiguousarray(np.asarray(inp["L1_attn_a_bqkv"], np.float32)[None, :])
    p["l1a_wprojt"] = tr(inp["L1_attn_a_Wproj"])
    p["l1a_bproj"] = np.ascontiguousarray(np.asarray(inp["L1_attn_a_bproj"], np.float32)[None, :])
    p["l1a_scale_col"] = np.ascontiguousarray(
        np.broadcast_to(np.asarray(inp["L1_attn_a_scale"], np.float32).reshape(1, 1), (128, 1)))
    wr = np.asarray(inp["L1_ffn_Wr"], np.float32)
    p["l1f_wrt"] = np.ascontiguousarray(wr.reshape(DCH, 128, 8).transpose(1, 0, 2))
    p["l1f_br"] = np.ascontiguousarray(np.asarray(inp["L1_ffn_br"], np.float32)[None, :])
    b1 = np.asarray(inp["L1_ffn_b1"], np.float32)
    p["l1f_b1_cols"] = np.ascontiguousarray(
        b1.reshape(8, 32, 128).transpose(2, 0, 1).reshape(128, 8 * 32))
    p["l1f_b2"] = np.ascontiguousarray(np.asarray(inp["L1_ffn_b2"], np.float32))
    p["l1f_w1"] = np.ascontiguousarray(np.asarray(inp["L1_ffn_W1"], np.float32))
    p["l1f_w2"] = np.ascontiguousarray(np.asarray(inp["L1_ffn_W2"], np.float32))
    p["l1b_wqt"] = tr(inp["L1_attn_b_Wq"])
    p["l1b_wkt"] = tr(inp["L1_attn_b_Wk"])
    p["l1b_wvt"] = tr(inp["L1_attn_b_Wv"])
    p["l1b_wot"] = tr(inp["L1_attn_b_Wo"])
    p["l1b_bq_cols"] = np.ascontiguousarray(
        np.asarray(inp["L1_attn_b_bq"], np.float32).reshape(DCH, 128).T)
    p["l1b_bk_cols"] = np.ascontiguousarray(
        np.asarray(inp["L1_attn_b_bk"], np.float32).reshape(DCH, 128).T)
    p["l1b_bv"] = np.ascontiguousarray(np.asarray(inp["L1_attn_b_bv"], np.float32)[None, :])
    p["l1b_bo"] = np.ascontiguousarray(np.asarray(inp["L1_attn_b_bo"], np.float32)[None, :])
    p["l1b_ascale_col"] = np.ascontiguousarray(
        np.broadcast_to(np.asarray(inp["L1_attn_b_attn_scale"], np.float32).reshape(1, 1), (128, 1)))
    pgrid = np.arange(128)[:, None]
    cgrid = np.arange(512)[None, :]
    m = np.zeros((128, 4, 512), np.float32)
    for o in range(4):
        ktv = 128 * o + pgrid
        keep = ~((ktv > cgrid) & ((ktv // 64) == (cgrid // 64)))
        m[:, o, :] = keep.astype(np.float32)
    p["l1b_masks"] = m.astype(BF)
    hi = np.zeros((128, DCH, 16), np.float32)
    for dc in range(DCH):
        for pp in range(128):
            hi[pp, dc, dc * 2 + pp // 64] = 1.0
    p["l1b_hind"] = hi.astype(BF)
    return p


class Sem:
    def __init__(self, h):
        self.h = h
        self.v = 0

    def inc(self, inst, n=1):
        inst.then_inc(self.h, n)
        self.v += n
        return self.v


class Ctx:

    def __init__(self, nc, ex):
        self.nc = nc
        self.ex = ex
        self.sb = lambda name, shape, dt: ex.enter_context(nc.sbuf_tensor(name, shape, dt))
        self.psum = lambda ex_, name, shape, dt=F32: ex_.enter_context(nc.psum_tensor(name, shape, dt))
        mksem = lambda name: Sem(ex.enter_context(nc.semaphore(name)))
        self.sW = mksem("sW")
        self.sX = mksem("sX")
        self.sMM = mksem("sMM")
        self.sV = mksem("sV")
        self.sS = mksem("sS")
        self.sG = mksem("sG")
        self.sCC = mksem("sCC")

        self.xres = self.sb("xres", [128, TCH, D], F32)
        self.xlnT = self.sb("xlnT", [128, DCH, T], BF16)
        self.ones = self.sb("ones", [128, 128], BF16)
        self.ident = self.sb("ident", [128, 128], BF16)
        self.triu = self.sb("triu", [128, 128], BF16)
        self.mu = self.sb("mu", [128, TCH], F32)
        self.rstd = self.sb("rstd", [128, TCH], F32)
        self.acc = self.sb("acc", [128, TCH], F32)
        self.s1 = self.sb("s1", [128, TCH], F32)
        self.var = self.sb("var", [128, TCH], F32)
        self.sq_scr = self.sb("sq_scr", [128, D], BF16)
        self.xln_rot = [self.sb(f"xln_rot{i}", [128, D], BF16) for i in range(2)]

        self.stream_deps = []
        self.fence_deps = []
        self.ln_evac = []

    def barrier_tokens(self):
        return [(s, s.v) for s in (self.sMM, self.sV, self.sS, self.sX, self.sW, self.sG)]


def _wait(eng, deps):
    for (sem, val) in deps:
        if val > 0:
            eng.wait_ge(sem.h, val)


def emit_ln(c, tag):
    nc = c.nc
    deps = list(c.stream_deps)
    tok_acc = c.sS.v + 1

    with nc.Block(f"ln{tag}_stats", no_gpsimd_drain=True) as b:
        @b.scalar
        def _(se):
            _wait(se, deps)
            ins = None
            for i in range(TCH):
                ins = se.activation(c.sq_scr[:], c.xres[:, i, :], AF.Square,
                                    accum_out=c.acc[:, i:i + 1])
            assert c.sS.inc(ins) == tok_acc

        @b.vector
        def _(ve):
            _wait(ve, deps)
            for i in range(TCH):
                ve.reduce_sum(c.s1[:, i:i + 1], c.xres[:, i, :], AX.X)
            ve.drain()
            ve.wait_ge(c.sS.h, tok_acc)
            ve.tensor_scalar_mul(c.mu[:], c.s1[:], 1.0 / D)
            ve.drain()
            ve.scalar_tensor_tensor(c.var[:], c.mu[:], -1.0, c.mu[:], ALU.mult, ALU.mult)
            ve.drain()
            ve.scalar_tensor_tensor(c.var[:], c.acc[:], 1.0 / D, c.var[:], ALU.mult, ALU.add)
            ve.drain()
            ve.tensor_scalar_add(c.var[:], c.var[:], 1e-5)
            ve.drain()
            ins = ve.reciprocal(c.var[:], c.var[:])
            c.sV.inc(ins)

    tok_var = c.sV.v
    with nc.Block(f"ln{tag}_rstd", no_gpsimd_drain=True) as b:
        @b.scalar
        def _(se):
            se.wait_ge(c.sV.h, tok_var)
            se.drain()
            ins = se.activation(c.rstd[:], c.var[:], AF.Sqrt)
            c.sS.inc(ins)

    tok_rstd = c.sS.v
    c.tok_stats = (c.sS, tok_rstd)

    ex = ExitStack()
    ps_tr = [c.psum(ex, f"ps_tr_{tag}_{i}", [128, D], BF16) for i in range(2)]
    sV0, sMM0, sS0 = c.sV.v, c.sMM.v, c.sS.v
    xln_tok = [sV0 + i + 1 for i in range(TCH)]
    tr_tok = [sMM0 + i + 1 for i in range(TCH)]
    evac_tok = [sS0 + i + 1 for i in range(TCH)]

    with nc.Block(f"ln{tag}_tr", no_gpsimd_drain=True) as b:
        @b.vector
        def _(ve):
            ve.wait_ge(c.sS.h, tok_rstd)
            for i in range(TCH):
                if i >= 2:
                    ve.wait_ge(c.sMM.h, tr_tok[i - 2])
                ins = ve.tensor_scalar(c.xln_rot[i % 2][:], c.xres[:, i, :],
                                       c.mu[:, i:i + 1], c.rstd[:, i:i + 1],
                                       ALU.subtract, ALU.mult)
                assert c.sV.inc(ins) == xln_tok[i]

        @b.tensor
        def _(te):
            for i in range(TCH):
                te.wait_ge(c.sV.h, xln_tok[i])
                if i >= 2:
                    te.wait_ge(c.sS.h, evac_tok[i - 2])
                ins = None
                for cc_ in range(DCH):
                    ins = te.transpose(ps_tr[i % 2][:, cc_ * 128:(cc_ + 1) * 128],
                                       c.xln_rot[i % 2][:, cc_ * 128:(cc_ + 1) * 128],
                                       c.ident[:])
                assert c.sMM.inc(ins) == tr_tok[i]

        @b.scalar
        def _(se):
            for i in range(TCH):
                se.wait_ge(c.sMM.h, tr_tok[i])
                src = ps_tr[i % 2][:].rearrange("p (c q) -> p c q", c=DCH)
                ins = se.copy(out=c.xlnT[:, :, i * 128:(i + 1) * 128], in_=src)
                assert c.sS.inc(ins) == evac_tok[i]

    ex.close()
    c.ln_evac = evac_tok
    c.tok_xlnT = (c.sS, c.sS.v)


L0A_PHASES = 4

def emit_l0a(c, w, dram):
    nc = c.nc
    emit_ln(c, "s0")
    if L0A_PHASES < 1:
        return
    fence = list(c.fence_deps)
    ex = ExitStack()
    sb = lambda name, shape, dt: ex.enter_context(nc.sbuf_tensor(name, shape, dt))
    q_tm = sb("l0a_q", [128, TCH, D], BF16)
    exw = ExitStack()
    sbw = lambda name, shape, dt: exw.enter_context(nc.sbuf_tensor(name, shape, dt))
    wq_sb = sbw("l0a_wq", [128, DCH, D], BF16)
    wk_sb = sbw("l0a_wk", [128, DCH, D], BF16)
    wv_sb = sbw("l0a_wv", [128, DCH, D], BF16)
    k_rot = [sbw(f"l0a_k{i}", [128, D], BF16) for i in range(2)]
    v_rot = [sbw(f"l0a_v{i}", [128, D], BF16) for i in range(2)]
    m_rot = [sbw(f"l0a_m{i}", [128, D], F32) for i in range(2)]
    vs = sbw("l0a_vs", [128, TCH], F32)

    wtok = {}
    with nc.Block("l0a_wload", no_gpsimd_drain=True) as b:
        @b.gpsimd
        def _(ge):
            _wait(ge, fence)
            for name, dst in (("l0a_wk", wk_sb), ("l0a_wv", wv_sb), ("l0a_wq", wq_sb)):
                src = w[name].rearrange("(kc p) e -> p kc e", p=128)
                for kc in range(DCH):
                    ins = ge.dma_start(out=dst[:, kc, :], in_=src[:, kc, :])
                    c.sW.inc(ins, 16)
                wtok[name] = c.sW.v

    if L0A_PHASES < 2:
        exw.close(); ex.close()
        return
    exq = ExitStack()
    ps = [c.psum(exq, f"l0a_ps{g}", [128, 512]) for g in range(6)]
    sMM0, sS0, sV0, sX0 = c.sMM.v, c.sS.v, c.sV.v, c.sX.v
    mm_tok = lambda i, g: sMM0 + i * 6 + g + 1
    ev_tok = lambda i, g: sS0 + i * 6 + g + 1
    v_tok = lambda i: sV0 + 2 * i + 2
    x_tok = lambda i: sX0 + (i + 1) * 16

    tok_xlnT_all = c.tok_xlnT[1]
    with nc.Block("l0a_proj", no_gpsimd_drain=True) as b:
        @b.tensor
        def _(te):
            te.wait_ge(c.sW.h, wtok["l0a_wq"])
            te.wait_ge(c.sS.h, tok_xlnT_all)
            for i in range(TCH):
                te.wait_ge(c.sS.h, c.ln_evac[i])
                for g, (wsb, e0) in enumerate(((wk_sb, 0), (wk_sb, 512), (wv_sb, 0),
                                               (wv_sb, 512), (wq_sb, 0), (wq_sb, 512))):
                    if i >= 1:
                        te.wait_ge(c.sS.h, ev_tok(i - 1, g))
                    ins = None
                    for kc in range(DCH):
                        ins = te.matmul(ps[g][:], c.xlnT[:, kc, i * 128:(i + 1) * 128],
                                        wsb[:, kc, e0:e0 + 512],
                                        start=(kc == 0), stop=(kc == DCH - 1))
                    assert c.sMM.inc(ins) == mm_tok(i, g)

        @b.scalar
        def _(se):
            for i in range(TCH):
                for g in range(6):
                    se.wait_ge(c.sMM.h, mm_tok(i, g))
                    if g < 2:
                        if i >= 2:
                            se.wait_ge(c.sV.h, v_tok(i - 2))
                        dst = k_rot[i % 2][:, g * 512:(g + 1) * 512]
                    elif g < 4:
                        if i >= 2:
                            se.wait_ge(c.sV.h, v_tok(i - 2))
                        dst = v_rot[i % 2][:, (g - 2) * 512:(g - 1) * 512]
                    else:
                        dst = q_tm[:, i, (g - 4) * 512:(g - 3) * 512]
                    ins = se.copy(out=dst, in_=ps[g][:])
                    assert c.sS.inc(ins) == ev_tok(i, g)

        @b.vector
        def _(ve):
            for i in range(TCH):
                ve.wait_ge(c.sS.h, ev_tok(i, 3))
                ins = ve.reduce_sum(vs[:, i:i + 1], v_rot[i % 2][:], AX.X)
                ve.drain()
                assert c.sV.inc(ins) == v_tok(i) - 1
                ve.wait_ge(c.sS.h, ev_tok(i, 1))
                if i >= 2:
                    ve.wait_ge(c.sX.h, x_tok(i - 2))
                ins = ve.tensor_scalar_mul(m_rot[i % 2][:], k_rot[i % 2][:], vs[:, i:i + 1])
                assert c.sV.inc(ins) == v_tok(i)

        @b.sync
        def _(sy):
            for i in range(TCH):
                sy.wait_ge(c.sV.h, v_tok(i))
                ins = sy.dma_start(out=dram["cc_in"][i * 128:(i + 1) * 128, :],
                                   in_=m_rot[i % 2][:])
                assert c.sX.inc(ins, 16) == x_tok(i)

    exq.close()
    tok_proj_mm = c.sMM.v
    exw.close()

    if L0A_PHASES < 3:
        c.fence_deps = c.barrier_tokens()
        ex.close()
        return
    m_bf = sb("l0a_mbf", [128, TCH, D], BF16)
    tok_mloc = c.sX.v
    with nc.Block("l0a_cc") as b:
        @b.gpsimd
        def _(ge):
            ge.wait_ge(c.sX.h, tok_mloc)
            ge.wait_ge(c.sMM.h, tok_proj_mm)
            ge.collective_compute(
                "AllReduce", ALU.add, replica_groups=[list(range(NCORE))],
                ins=[dram["cc_in"]], outs=[dram["cc_out"]],
            ).then_inc(c.sCC.h)
            ge.wait_ge(c.sCC.h, 1)
            src = dram["cc_out"].rearrange("(i p) r -> p i r", p=128)
            ins = None
            for i0 in range(0, TCH, 4):
                ins = ge.dma_start(out=m_bf[:, i0:i0 + 4, :], in_=src[:, i0:i0 + 4, :])
                c.sW.inc(ins, 16)
    tok_mbf = c.sW.v

    if L0A_PHASES < 4:
        c.fence_deps = c.barrier_tokens()
        ex.close()
        return
    a_rot = [sb(f"l0a_a{i}", [128, TCH, 512], BF16) for i in range(2)]
    exd = ExitStack()
    psd = [c.psum(exd, f"l0a_psd{i}", [128, 512]) for i in range(2)]
    t1 = [sb(f"l0a_t1_{i}", [128, 512], F32) for i in range(2)]

    sX0, sMM0, sV0 = c.sX.v, c.sMM.v, c.sV.v
    tok_proj_evac = c.sS.v
    a_tok = lambda blk: sX0 + (blk + 1) * 16 * TCH
    d_mm_tok = lambda blk, rs: sMM0 + blk * 8 + rs + 1
    d_v_tok = lambda blk, rs: sV0 + (blk * 8 + rs + 1) * 2

    with nc.Block("l0a_sprime", no_gpsimd_drain=True) as b:
        @b.sync
        def _(sy):
            for blk in range(4):
                if blk >= 2:
                    sy.wait_ge(c.sMM.h, d_mm_tok(blk - 2, 7))
                col0 = blk * 512
                src = dram["a_cat"]
                ins = None
                for jc in range(TCH):
                    ins = sy.dma_start(out=a_rot[blk % 2][:, jc, :],
                                       in_=src[jc * 128:(jc + 1) * 128, col0:col0 + 512])
                    c.sX.inc(ins, 16)
                assert c.sX.v == a_tok(blk)

        @b.tensor
        def _(te):
            te.wait_ge(c.sW.h, tok_mbf)
            te.wait_ge(c.sS.h, tok_proj_evac)
            for blk in range(4):
                te.wait_ge(c.sX.h, a_tok(blk))
                for rs in range(8):
                    if blk * 8 + rs >= 2:
                        prev = blk * 8 + rs - 2
                        te.wait_ge(c.sV.h, d_v_tok(prev // 8, prev % 8))
                    ins = None
                    for jc in range(TCH):
                        ins = te.matmul(psd[rs % 2][:], m_bf[:, jc, rs * 128:(rs + 1) * 128],
                                        a_rot[blk % 2][:, jc, :],
                                        start=(jc == 0), stop=(jc == TCH - 1))
                    assert c.sMM.inc(ins) == d_mm_tok(blk, rs)

        @b.vector
        def _(ve):
            ve.wait_ge(c.sS.h, tok_proj_evac)
            for blk in range(4):
                sel, cs = blk // 2, blk % 2
                for rs in range(8):
                    ct = sel * 8 + rs
                    ve.wait_ge(c.sMM.h, d_mm_tok(blk, rs))
                    ve.tensor_tensor(t1[rs % 2][:], psd[rs % 2][:],
                                     q_tm[:, ct, cs * 512:(cs + 1) * 512], ALU.mult)
                    ins = ve.tensor_add(c.xres[:, ct, cs * 512:(cs + 1) * 512],
                                        c.xres[:, ct, cs * 512:(cs + 1) * 512], t1[rs % 2][:])
                    c.sV.inc(ins, 2)
                    assert c.sV.v == d_v_tok(blk, rs)

    exd.close()
    c.stream_deps = [(c.sV, c.sV.v)]
    c.fence_deps = c.barrier_tokens()
    ex.close()


def emit_l0ffn(c, w, dram, tag):
    nc = c.nc
    emit_ln(c, tag)
    fence = list(c.fence_deps)
    ex = ExitStack()
    sb = lambda name, shape, dt: ex.enter_context(nc.sbuf_tensor(f"{tag}_{name}", shape, dt))
    NE, NL = 8, 12
    FF = 2048
    FC = FF // 128

    gwt_sb = sb("gwt", [128, DCH, NL], BF16)
    eb_sb = sb("eb", [1, NL], BF16)
    lg = sb("lg", [128, TCH, NL], F32)
    eq1 = sb("eq1", [128, TCH, NL], F32)
    eq2 = sb("eq2", [128, TCH, NL], F32)
    m1 = sb("m1", [128, TCH], F32)
    m2 = sb("m2", [128, TCH], F32)
    p1 = sb("p1", [128, TCH], F32)
    p2 = sb("p2", [128, TCH], F32)
    wz = sb("wz", [128, TCH], F32)
    ab = sb("ab", [128, 2, TCH], F32)
    h_sb = sb("h", [128, FC, T], BF16)
    w1rot = [sb(f"w1r{i}", [128, DCH, 128], BF16) for i in range(2)]
    w2_sb = sb("w2", [128, FC, D], BF16)

    exp = ExitStack()
    ps_lg = [c.psum(exp, f"{tag}_pslg{i}", [128, NL]) for i in range(2)]
    ps_h = [c.psum(exp, f"{tag}_psh{i}", [128, 512]) for i in range(2)]
    ps_y = [c.psum(exp, f"{tag}_psy{i}", [128, 512]) for i in range(4)]

    with nc.Block(f"{tag}_smload", no_gpsimd_drain=True) as b:
        @b.gpsimd
        def _(ge):
            _wait(ge, fence)
            c.sW.inc(ge.dma_start(out=gwt_sb[:], in_=w["l0f_gwt"]), 16)
            c.sW.inc(ge.dma_start(out=eb_sb[:], in_=w["l0f_eb"]), 16)
    tok_sm = c.sW.v

    sMM0, sV0 = c.sMM.v, c.sV.v
    lgmm_tok = [sMM0 + i + 1 for i in range(TCH)]
    lgev_tok = [sV0 + i + 1 for i in range(TCH)]
    with nc.Block(f"{tag}_gates", no_gpsimd_drain=True) as b:
        @b.tensor
        def _(te):
            _wait(te, fence)
            te.wait_ge(c.sW.h, tok_sm)
            for i in range(TCH):
                te.wait_ge(c.sS.h, c.ln_evac[i])
                if i >= 2:
                    te.wait_ge(c.sV.h, lgev_tok[i - 2])
                for kc in range(DCH):
                    te.matmul(ps_lg[i % 2][:], c.xlnT[:, kc, i * 128:(i + 1) * 128],
                              gwt_sb[:, kc, :], start=(kc == 0), stop=False)
                ins = te.matmul(ps_lg[i % 2][:], c.ones[0:1, :], eb_sb[:],
                                start=False, stop=True)
                assert c.sMM.inc(ins) == lgmm_tok[i]

        @b.vector
        def _(ve):
            for i in range(TCH):
                ve.wait_ge(c.sMM.h, lgmm_tok[i])
                ins = ve.tensor_copy(out=lg[:, i, :], in_=ps_lg[i % 2][:])
                assert c.sV.inc(ins) == lgev_tok[i]
            ve.drain()
            ve.reduce_max(m1[:, :, None], lg[:], AX.X)
            ve.drain()
            ve.tensor_tensor(eq1[:], lg[:], m1[:, :, None].broadcast_to([128, TCH, NL]), ALU.is_equal)
            ve.scalar_tensor_tensor(eq2[:], eq1[:], -1e30, lg[:], ALU.mult, ALU.add)
            ve.drain()
            ve.reduce_max(m2[:, :, None], eq2[:], AX.X)
            ve.drain()
            ve.tensor_tensor(eq2[:], lg[:], m2[:, :, None].broadcast_to([128, TCH, NL]), ALU.is_equal)
            ins = ve.tensor_sub(p2[:], m2[:], m1[:])
            c.sV.inc(ins)

    tok_diff = c.sV.v
    with nc.Block(f"{tag}_sig", no_gpsimd_drain=True) as b:
        @b.scalar
        def _(se):
            se.wait_ge(c.sV.h, tok_diff)
            ins = se.activation(p1[:], p2[:], AF.Sigmoid, scale=-1.0)
            c.sS.inc(ins)
    tok_p1 = c.sS.v

    with nc.Block(f"{tag}_gmath", no_gpsimd_drain=True) as b:
        @b.vector
        def _(ve):
            ve.wait_ge(c.sS.h, tok_p1)
            ve.tensor_scalar(p2[:], p1[:], -1.0, 1.0, ALU.mult, ALU.add)
            ve.drain()
            ve.tensor_tensor(lg[:], eq1[:], p1[:, :, None].broadcast_to([128, TCH, NL]), ALU.mult)
            ve.tensor_tensor(eq2[:], eq2[:], p2[:, :, None].broadcast_to([128, TCH, NL]), ALU.mult)
            ve.tensor_add(lg[:], lg[:], eq2[:])
            ve.drain()
            ve.reduce_sum(wz[:, :, None], lg[:, :, NE:NL], AX.X)
            ve.drain()
            ve.tensor_tensor(ab[:, 0, :], wz[:], c.rstd[:], ALU.mult)
            ve.drain()
            ve.tensor_tensor(ab[:, 1, :], ab[:, 0, :], c.mu[:], ALU.mult)
            ve.tensor_scalar_add(ab[:, 0, :], ab[:, 0, :], 1.0)
            ve.drain()
            ins = None
            for i in range(TCH):
                ins = ve.tensor_scalar(c.xres[:, i, :], c.xres[:, i, :],
                                       ab[:, 0, i:i + 1], ab[:, 1, i:i + 1],
                                       ALU.mult, ALU.subtract)
            c.sV.inc(ins)
    tok_gates = c.sV.v

    w1t = w["l0f_w1t"]
    w2t = w["l0f_w2t"]
    sW0, sMM1, sS1, sV1 = c.sW.v, c.sMM.v, c.sS.v, c.sV.v
    w1tok = lambda e, fc: sW0 + e * 32 * 16 + (fc + 1) * 16
    w2tok = lambda e: sW0 + e * 32 * 16 + 32 * 16
    hmm = lambda e, g: sMM1 + e * 96 + g + 1
    ymm = lambda e, yi: sMM1 + e * 96 + 64 + yi + 1
    hev = lambda e, g: sS1 + e * 64 + g + 1
    ytok = lambda e, yi: sV1 + e * 32 + yi + 1

    with nc.Block(f"{tag}_experts", no_gpsimd_drain=True) as b:
        @b.gpsimd
        def _(ge):
            for e in range(NE):
                src1 = w1t[e].rearrange("(dc p) f -> p dc f", p=128)
                for fc in range(FC):
                    slot = fc % 2
                    if fc >= 2:
                        ge.wait_ge(c.sMM.h, hmm(e, (fc - 2) * 4 + 3))
                    elif e > 0:
                        ge.wait_ge(c.sMM.h, hmm(e - 1, (fc + 14) * 4 + 3))
                    ins = ge.dma_start(out=w1rot[slot][:],
                                       in_=src1[:, :, fc * 128:(fc + 1) * 128])
                    assert c.sW.inc(ins, 16) == w1tok(e, fc)
                if e > 0:
                    ge.wait_ge(c.sV.h, ytok(e - 1, 31))
                src2 = w2t[e].rearrange("(fc p) d -> p fc d", p=128)
                for fc in range(FC):
                    ins = ge.dma_start(out=w2_sb[:, fc, :], in_=src2[:, fc, :])
                    c.sW.inc(ins, 16)
                assert c.sW.v == w2tok(e)

        @b.tensor
        def _(te):
            for e in range(NE):
                for fc in range(FC):
                    te.wait_ge(c.sW.h, w1tok(e, fc))
                    for tt in range(4):
                        g = fc * 4 + tt
                        if g >= 2:
                            te.wait_ge(c.sS.h, hev(e, g - 2))
                        elif e > 0:
                            te.wait_ge(c.sS.h, hev(e - 1, 62 + g))
                        ins = None
                        for dc in range(DCH):
                            ins = te.matmul(ps_h[g % 2][:],
                                            w1rot[fc % 2][:, dc, :],
                                            c.xlnT[:, dc, tt * 512:(tt + 1) * 512],
                                            start=(dc == 0), stop=(dc == DCH - 1))
                        assert c.sMM.inc(ins) == hmm(e, g)
                te.wait_ge(c.sW.h, w2tok(e))
                te.wait_ge(c.sS.h, hev(e, 63))
                for tcol in range(TCH):
                    for dd in range(2):
                        yi = tcol * 2 + dd
                        if yi >= 4:
                            te.wait_ge(c.sV.h, ytok(e, yi - 4))
                        elif e > 0:
                            te.wait_ge(c.sV.h, ytok(e - 1, 28 + yi))
                        ins = None
                        for fc in range(FC):
                            ins = te.matmul(ps_y[yi % 4][:],
                                            h_sb[:, fc, tcol * 128:(tcol + 1) * 128],
                                            w2_sb[:, fc, dd * 512:(dd + 1) * 512],
                                            start=(fc == 0), stop=(fc == FC - 1))
                        assert c.sMM.inc(ins) == ymm(e, yi)

        @b.scalar
        def _(se):
            for e in range(NE):
                if e > 0:
                    se.wait_ge(c.sMM.h, ymm(e - 1, 31))
                for g in range(64):
                    fc, tt = g // 4, g % 4
                    se.wait_ge(c.sMM.h, hmm(e, g))
                    ins = se.activation(h_sb[:, fc, tt * 512:(tt + 1) * 512],
                                        ps_h[g % 2][:], AF.Silu)
                    assert c.sS.inc(ins) == hev(e, g)

        @b.vector
        def _(ve):
            for e in range(NE):
                for yi in range(32):
                    tcol, dd = yi // 2, yi % 2
                    ve.wait_ge(c.sMM.h, ymm(e, yi))
                    ins = ve.scalar_tensor_tensor(
                        c.xres[:, tcol, dd * 512:(dd + 1) * 512],
                        ps_y[yi % 4][:], lg[:, tcol, e:e + 1],
                        c.xres[:, tcol, dd * 512:(dd + 1) * 512],
                        ALU.mult, ALU.add)
                    assert c.sV.inc(ins) == ytok(e, yi)

    exp.close()
    c.stream_deps = [(c.sV, c.sV.v)]
    c.fence_deps = c.barrier_tokens()
    ex.close()


L0B_PHASES = 9

def emit_l0b(c, w, dram, tag="s2"):
    nc = c.nc
    emit_ln(c, tag)
    fence = list(c.fence_deps)
    ex = ExitStack()
    sb = lambda name, shape, dt: ex.enter_context(nc.sbuf_tensor(f"{tag}_{name}", shape, dt))
    q_fm = sb("qfm", [128, DCH, T], BF16)
    bq_cols = sb("bqc", [128, DCH], F32)
    bkv_row = sb("bkvr", [1, 2048], BF16)
    bout_row = sb("boutr", [1, D], BF16)
    nk2 = sb("nk2", [128, TCH], F32)
    rk = sb("rk", [128, TCH], F32)
    k_rot = [sb(f"k{i}", [128, D], BF16) for i in range(2)]
    v_rot = [sb(f"v{i}", [128, D], BF16) for i in range(2)]
    pr_rot = [sb(f"pr{i}", [128, D], BF16) for i in range(2)]
    sq2 = sb("sq2", [128, T], BF16)
    row_bf = sb("rowbf", [1, T], BF16)
    rq_col = sb("rqc", [128, TCH], F32)
    kvT_col = sb("kvT", [128, DCH], F32)
    bb_sb = sb("bb", [128, D], BF16)
    t2 = [sb(f"t2_{i}", [128, 512], F32) for i in range(2)]

    ex2 = ExitStack()
    wqkv_sb = ex2.enter_context(nc.sbuf_tensor(f"{tag}_wqkv", [128, DCH, 2 * D], BF16))

    with nc.Block(f"{tag}_load", no_gpsimd_drain=True) as b:
        @b.gpsimd
        def _(ge):
            _wait(ge, fence)
            src = w["l0b_wqkv"].rearrange("(dc p) e -> p dc e", p=128)
            for dc in range(DCH):
                c.sW.inc(ge.dma_start(out=wqkv_sb[:, dc, 0:D], in_=src[:, dc, 0:D]), 16)
            c.sW.inc(ge.dma_start(out=bkv_row[:], in_=w["l0b_bqkv"][:, D:3 * D]), 16)
            c.sW.inc(ge.dma_start(out=bout_row[:], in_=w["l0b_bout"]), 16)

        @b.sync
        def _(sy):
            _wait(sy, fence)
            c.sX.inc(sy.dma_start(out=bq_cols[:], in_=w["l0b_bq_cols"]), 16)
    tok_w = c.sW.v
    tok_bq = c.sX.v

    ex_row = ExitStack()
    ps_row = [c.psum(ex_row, f"{tag}_psr{i}", [1, 512]) for i in range(2)]
    exp1 = ExitStack()
    ps_q = [c.psum(exp1, f"{tag}_psq{i}", [128, 512]) for i in range(2)]
    ps_k = [c.psum(exp1, f"{tag}_psk{i}", [128, 512]) for i in range(2)]
    ps_v = [c.psum(exp1, f"{tag}_psv{i}", [128, 512]) for i in range(2)]

    sMM0, sV0q = c.sMM.v, c.sV.v
    qmm = lambda g: sMM0 + g + 1
    qev = lambda g: sV0q + g + 1
    with nc.Block(f"{tag}_p1", no_gpsimd_drain=True) as b:
        @b.tensor
        def _(te):
            _wait(te, fence)
            te.wait_ge(c.sW.h, tok_w)
            te.wait_ge(c.sS.h, c.tok_xlnT[1])
            for ec in range(DCH):
                for tt in range(4):
                    g = ec * 4 + tt
                    if g >= 2:
                        te.wait_ge(c.sV.h, qev(g - 2))
                    ins = None
                    for dc in range(DCH):
                        ins = te.matmul(ps_q[g % 2][:], wqkv_sb[:, dc, ec * 128:(ec + 1) * 128],
                                        c.xlnT[:, dc, tt * 512:(tt + 1) * 512],
                                        start=(dc == 0), stop=(dc == DCH - 1))
                    assert c.sMM.inc(ins) == qmm(g)

        @b.vector
        def _(ve):
            ve.wait_ge(c.sX.h, tok_bq)
            for g in range(32):
                ec, tt = g // 4, g % 4
                ve.wait_ge(c.sMM.h, qmm(g))
                ins = ve.tensor_scalar_add(q_fm[:, ec, tt * 512:(tt + 1) * 512],
                                           ps_q[g % 2][:], bq_cols[:, ec:ec + 1])
                assert c.sV.inc(ins) == qev(g)

    if L0B_PHASES < 2:
        c.fence_deps = c.barrier_tokens()
        exp1.close(); ex_row.close(); ex2.close(); ex.close()
        return
    tok_p1_mm = c.sMM.v
    with nc.Block(f"{tag}_loadB", no_gpsimd_drain=True) as b:
        @b.gpsimd
        def _(ge):
            ge.wait_ge(c.sMM.h, tok_p1_mm)
            src = w["l0b_wqkv"].rearrange("(dc p) e -> p dc e", p=128)
            for dc in range(DCH):
                c.sW.inc(ge.dma_start(out=wqkv_sb[:, dc, :], in_=src[:, dc, D:3 * D]), 16)
    tok_wB = c.sW.v

    sMM0, sS0, sV0 = c.sMM.v, c.sS.v, c.sV.v
    kvmm = lambda i, g: sMM0 + i * 5 + g + 1
    rowmm = lambda i: sMM0 + i * 5 + 5
    sev = lambda i, j: sS0 + i * 6 + j + 1
    vtk = lambda i, j: sV0 + i * 2 + j + 1
    with nc.Block(f"{tag}_p2", no_gpsimd_drain=True) as b:
        @b.tensor
        def _(te):
            te.wait_ge(c.sW.h, tok_wB)
            for i in range(TCH):
                for g in range(4):
                    src_e = g * 512
                    pp = (ps_k if g < 2 else ps_v)[g % 2]
                    if i >= 1:
                        te.wait_ge(c.sS.h, sev(i - 1, g))
                    for dc in range(DCH):
                        te.matmul(pp[:], c.xlnT[:, dc, i * 128:(i + 1) * 128],
                                  wqkv_sb[:, dc, src_e:src_e + 512],
                                  start=(dc == 0), stop=False)
                    ins = te.matmul(pp[:], c.ones[0:1, :],
                                    bkv_row[:, src_e:src_e + 512],
                                    start=False, stop=True)
                    assert c.sMM.inc(ins) == kvmm(i, g)
                te.wait_ge(c.sV.h, vtk(i, 1))
                ins = None
                for half in range(2):
                    ins = te.matmul(ps_row[half][:], c.ones[:, 0:1],
                                    pr_rot[i % 2][:, half * 512:(half + 1) * 512],
                                    start=(i == 0), stop=(i == TCH - 1))
                c.sMM.inc(ins)
                assert c.sMM.v == rowmm(i)

        @b.scalar
        def _(se):
            for i in range(TCH):
                for g in range(4):
                    se.wait_ge(c.sMM.h, kvmm(i, g))
                    if i >= 2:
                        se.wait_ge(c.sV.h, vtk(i - 2, 1))
                    dst = (k_rot if g < 2 else v_rot)[i % 2][:, (g % 2) * 512:(g % 2) * 512 + 512]
                    ins = se.copy(out=dst, in_=(ps_k if g < 2 else ps_v)[g % 2][:])
                    assert c.sS.inc(ins) == sev(i, g)
                ins = se.activation(c.sq_scr[:], k_rot[i % 2][:], AF.Square,
                                    accum_out=nk2[:, i:i + 1])
                assert c.sS.inc(ins) == sev(i, 4)
                se.wait_ge(c.sV.h, vtk(i, 0))
                ins = se.activation(rk[:, i:i + 1], rk[:, i:i + 1], AF.Sqrt)
                assert c.sS.inc(ins) == sev(i, 5)

        @b.vector
        def _(ve):
            for i in range(TCH):
                ve.wait_ge(c.sS.h, sev(i, 4))
                ins = ve.reciprocal(rk[:, i:i + 1], nk2[:, i:i + 1])
                ve.drain()
                assert c.sV.inc(ins) == vtk(i, 0)
                ve.wait_ge(c.sS.h, sev(i, 5))
                if i >= 2:
                    ve.wait_ge(c.sMM.h, rowmm(i - 2))
                ins = ve.scalar_tensor_tensor(pr_rot[i % 2][:], k_rot[i % 2][:],
                                              rk[:, i:i + 1], v_rot[i % 2][:],
                                              ALU.mult, ALU.mult)
                assert c.sV.inc(ins) == vtk(i, 1)

    tok_p2_mm = c.sMM.v
    exp1.close()
    ex2.close()
    if L0B_PHASES < 3:
        c.fence_deps = c.barrier_tokens()
        ex_row.close(); ex.close()
        return

    exn1 = ExitStack()
    ps_nq = [c.psum(exn1, f"{tag}_psnq{i}", [1, 512]) for i in range(4)]
    ps_tr2 = c.psum(exn1, f"{tag}_pstr2", [128, TCH], F32)
    sMM0, sS0 = c.sMM.v, c.sS.v
    sqe = lambda ec: sS0 + ec + 1
    nqmm = lambda ec: sMM0 + ec + 1
    with nc.Block(f"{tag}_qn", no_gpsimd_drain=True) as b:
        @b.scalar
        def _(se):
            for ec in range(DCH):
                if ec >= 1:
                    se.wait_ge(c.sMM.h, nqmm(ec - 1))
                ins = se.activation(sq2[:], q_fm[:, ec, :], AF.Square)
                assert c.sS.inc(ins) == sqe(ec)

        @b.tensor
        def _(te):
            for ec in range(DCH):
                te.wait_ge(c.sS.h, sqe(ec))
                ins = None
                for tt in range(4):
                    ins = te.matmul(ps_nq[tt][:], c.ones[:, 0:1],
                                    sq2[:, tt * 512:(tt + 1) * 512],
                                    start=(ec == 0), stop=(ec == DCH - 1))
                assert c.sMM.inc(ins) == nqmm(ec)

    if L0B_PHASES < 2.7:
        c.fence_deps = c.barrier_tokens()
        exn1.close(); ex_row.close(); ex.close()
        return
    tok_nq = c.sMM.v
    sV0 = c.sV.v
    with nc.Block(f"{tag}_qn2", no_gpsimd_drain=True) as b:
        @b.vector
        def _(ve):
            ve.wait_ge(c.sMM.h, tok_nq)
            ins = None
            for tt in range(4):
                ins = ve.tensor_copy(out=row_bf[:, tt * 512:(tt + 1) * 512], in_=ps_nq[tt][:])
            c.sV.inc(ins)
    tok_nqrow = c.sV.v

    if L0B_PHASES < 2.9:
        c.fence_deps = c.barrier_tokens()
        exn1.close(); ex_row.close(); ex.close()
        return
    sMM0, sV0, sS0 = c.sMM.v, c.sV.v, c.sS.v
    stage = ex.enter_context(nc.sbuf_tensor(f"{tag}_stage", [128, 128], BF16))
    ps_trs = exn1.enter_context(nc.psum_tensor(f"{tag}_pstrs", [128, 128], BF16))
    stg_tok = [sV0 + 1 + i for i in range(TCH)]
    trp_tok = [sMM0 + 1 + i for i in range(TCH)]
    col_tok = [sS0 + 1 + i for i in range(TCH)]
    with nc.Block(f"{tag}_qn3", no_gpsimd_drain=True) as b:
        @b.vector
        def _(ve):
            ve.memset(stage[:], 0.0)
            for i in range(TCH):
                if i >= 1:
                    ve.wait_ge(c.sMM.h, trp_tok[i - 1])
                ins = ve.tensor_copy(out=stage[0:1, :], in_=row_bf[:, i * 128:(i + 1) * 128])
                assert c.sV.inc(ins) == stg_tok[i]

        @b.tensor
        def _(te):
            for i in range(TCH):
                te.wait_ge(c.sV.h, stg_tok[i])
                if i >= 1:
                    te.wait_ge(c.sS.h, col_tok[i - 1])
                ins = te.transpose(ps_trs[:], stage[:], c.ident[:])
                assert c.sMM.inc(ins) == trp_tok[i]

        @b.scalar
        def _(se):
            for i in range(TCH):
                se.wait_ge(c.sMM.h, trp_tok[i])
                ins = se.copy(out=rq_col[:, i:i + 1], in_=ps_trs[:, 0:1])
                assert c.sS.inc(ins) == col_tok[i]
    tok_qn3 = c.sS.v
    sV0, sS0 = c.sV.v, c.sS.v
    with nc.Block(f"{tag}_qn4", no_gpsimd_drain=True) as b:
        @b.vector
        def _(ve):
            ve.wait_ge(c.sS.h, tok_qn3)
            ins = ve.reciprocal(rq_col[:], rq_col[:])
            ve.drain()
            c.sV.inc(ins)

        @b.scalar
        def _(se):
            se.wait_ge(c.sV.h, sV0 + 1)
            ins = se.activation(rq_col[:], rq_col[:], AF.Sqrt)
            c.sS.inc(ins)
    tok_rq = (c.sS, c.sS.v)
    tok_rowbf_free = trp_tok[-1]
    exn1.close()
    if L0B_PHASES < 4:
        c.fence_deps = c.barrier_tokens()
        ex_row.close(); ex.close()
        return

    ex3 = ExitStack()
    wout_sb = ex3.enter_context(nc.sbuf_tensor(f"{tag}_wout", [128, DCH, D], BF16))
    ws_sb = ex3.enter_context(nc.sbuf_tensor(f"{tag}_ws", [128, DCH, D], BF16))
    exn2 = ExitStack()
    ps_trk = c.psum(exn2, f"{tag}_pstrk", [128, DCH], F32)
    ps_bb = [c.psum(exn2, f"{tag}_psbb{i}", [128, 512]) for i in range(2)]

    with nc.Block(f"{tag}_wload2", no_gpsimd_drain=True) as b:
        @b.gpsimd
        def _(ge):
            ge.wait_ge(c.sMM.h, tok_p2_mm)
            src = w["l0b_wout"].rearrange("(dc p) e -> p dc e", p=128)
            for dc in range(DCH):
                c.sW.inc(ge.dma_start(out=wout_sb[:, dc, :], in_=src[:, dc, :]), 16)
    tok_wout = c.sW.v

    sV0, sMM0 = c.sV.v, c.sMM.v
    with nc.Block(f"{tag}_ws", no_gpsimd_drain=True) as b:
        @b.vector
        def _(ve):
            ve.wait_ge(c.sMM.h, tok_p2_mm)
            ve.wait_ge(c.sMM.h, tok_rowbf_free)
            ve.tensor_copy(out=row_bf[:, 0:512], in_=ps_row[0][:])
            ins = ve.tensor_copy(out=row_bf[:, 512:1024], in_=ps_row[1][:])
            assert c.sV.inc(ins) == sV0 + 1

        @b.tensor
        def _(te):
            te.wait_ge(c.sV.h, sV0 + 1)
            ins = None
            for dc in range(DCH):
                ins = te.matmul(ps_trk[:, dc:dc + 1], row_bf[:, dc * 128:(dc + 1) * 128],
                                c.ones[0:1, 0:1], start=True, stop=True)
            assert c.sMM.inc(ins) == sMM0 + 1
            te.wait_ge(c.sW.h, tok_w)
            for half in range(2):
                ins = te.matmul(ps_bb[half][:], c.ones[0:1, :],
                                bout_row[:, half * 512:(half + 1) * 512],
                                start=True, stop=True)
            assert c.sMM.inc(ins) == sMM0 + 2

        @b.vector
        def _(ve):
            ve.wait_ge(c.sMM.h, sMM0 + 1)
            ve.tensor_copy(out=kvT_col[:], in_=ps_trk[:])
            ve.drain()
            ve.wait_ge(c.sW.h, tok_wout)
            for dc in range(DCH):
                ve.tensor_scalar_mul(ws_sb[:, dc, :], wout_sb[:, dc, :], kvT_col[:, dc:dc + 1])
            ve.wait_ge(c.sMM.h, sMM0 + 2)
            ve.tensor_copy(out=bb_sb[:, 0:512], in_=ps_bb[0][:])
            ins = ve.tensor_copy(out=bb_sb[:, 512:1024], in_=ps_bb[1][:])
            assert c.sV.inc(ins) == sV0 + 2
    tok_ws = c.sV.v
    exn2.close()
    ex_row.close()
    if L0B_PHASES < 5:
        c.fence_deps = c.barrier_tokens()
        ex3.close(); ex.close()
        return

    exf = ExitStack()
    ps_f = [c.psum(exf, f"{tag}_psf{i}", [128, 512]) for i in range(2)]
    sMM0, sV0 = c.sMM.v, c.sV.v
    fmm = lambda j: sMM0 + j + 1
    fvt = lambda j: sV0 + 2 * (j + 1)
    with nc.Block(f"{tag}_fin", no_gpsimd_drain=True) as b:
        @b.tensor
        def _(te):
            te.wait_ge(c.sV.h, tok_ws)
            for j in range(32):
                i, dd = j // 2, j % 2
                if j >= 2:
                    te.wait_ge(c.sV.h, fvt(j - 2))
                ins = None
                for dc in range(DCH):
                    ins = te.matmul(ps_f[j % 2][:], q_fm[:, dc, i * 128:(i + 1) * 128],
                                    ws_sb[:, dc, dd * 512:(dd + 1) * 512],
                                    start=(dc == 0), stop=(dc == DCH - 1))
                assert c.sMM.inc(ins) == fmm(j)

        @b.vector
        def _(ve):
            ve.wait_ge(tok_rq[0].h, tok_rq[1])
            for j in range(32):
                i, dd = j // 2, j % 2
                ve.wait_ge(c.sMM.h, fmm(j))
                ve.scalar_tensor_tensor(t2[j % 2][:], ps_f[j % 2][:], rq_col[:, i:i + 1],
                                        bb_sb[:, dd * 512:(dd + 1) * 512], ALU.mult, ALU.add)
                ins = ve.tensor_add(c.xres[:, i, dd * 512:(dd + 1) * 512],
                                    c.xres[:, i, dd * 512:(dd + 1) * 512], t2[j % 2][:])
                c.sV.inc(ins, 2)
                assert c.sV.v == fvt(j)

    exf.close()
    ex3.close()
    c.stream_deps = [(c.sV, c.sV.v)]
    c.fence_deps = c.barrier_tokens()
    ex.close()


def emit_l1a(c, w, dram, tag="s4"):
    nc = c.nc
    emit_ln(c, tag)
    fence = list(c.fence_deps)
    EPS = 1e-3
    ex = ExitStack()
    sb = lambda name, shape, dt: ex.enter_context(nc.sbuf_tensor(f"{tag}_{name}", shape, dt))
    wqkv_sb = sb("wqkv", [128, DCH, 3 * D], BF16)
    wproj_sb = sb("wproj", [128, DCH, D], BF16)
    bqkv_row = sb("bqkvr", [1, 3 * D], BF16)
    bproj_row = sb("bprojr", [1, D], BF16)
    scale_col = sb("scol", [128, 1], F32)
    carry = sb("carry", [1, 2 * D], F32)
    carry_bf = sb("carrybf", [1, 2 * D], BF16)
    ea = sb("ea", [128, D], BF16)
    eb = sb("ebuf", [128, D], BF16)
    q_t = sb("qt", [128, D], BF16)
    k_t = sb("kt", [128, D], BF16)
    v_t = sb("vt", [128, D], BF16)
    kv_t = sb("kvt", [128, D], BF16)
    kc_t = sb("kct", [128, D], BF16)
    kvc_t = sb("kvct", [128, D], BF16)
    y_t = sb("yt", [128, D], BF16)
    ylT = sb("ylT", [128, DCH, 128], BF16)
    den_ch = sb("den", [128, 16], F32)
    rden_ch = sb("rden", [128, 16], F32)

    exp = ExitStack()
    ps = [c.psum(exp, f"{tag}_ps{i}", [128, 512]) for i in range(2)]
    ps_cum = [c.psum(exp, f"{tag}_pscum{i}", [128, 512]) for i in range(2)]
    ps_sum = c.psum(exp, f"{tag}_pssum", [1, 512])
    ps_ytr = c.psum(exp, f"{tag}_psytr", [128, D], BF16)
    ps_pr = [c.psum(exp, f"{tag}_pspr{i}", [128, 512]) for i in range(2)]

    with nc.Block(f"{tag}_load", no_gpsimd_drain=True) as b:
        @b.gpsimd
        def _(ge):
            _wait(ge, fence)
            src = w["l1a_wqkvt"].rearrange("(dc p) e -> p dc e", p=128)
            for dc in range(DCH):
                c.sW.inc(ge.dma_start(out=wqkv_sb[:, dc, :], in_=src[:, dc, :]), 16)
            src = w["l1a_wprojt"].rearrange("(dc p) e -> p dc e", p=128)
            for dc in range(DCH):
                c.sW.inc(ge.dma_start(out=wproj_sb[:, dc, :], in_=src[:, dc, :]), 16)
            c.sW.inc(ge.dma_start(out=bqkv_row[:], in_=w["l1a_bqkv"]), 16)
            c.sW.inc(ge.dma_start(out=bproj_row[:], in_=w["l1a_bproj"]), 16)

        @b.sync
        def _(sy):
            _wait(sy, fence)
            c.sX.inc(sy.dma_start(out=scale_col[:], in_=w["l1a_scale_col"]), 16)
    tok_w = c.sW.v
    tok_sc = c.sX.v

    sMM0, sS0, sV0 = c.sMM.v, c.sS.v, c.sV.v
    NT, NS, NV = 17, 9, 24
    tM = lambda i, j: sMM0 + i * NT + j
    tS = lambda i, j: sS0 + i * NS + j
    tV = lambda i, j: sV0 + 1 + i * NV + j
    cons = {0: ("V", 3), 1: ("V", 4), 2: ("V", 8), 3: ("V", 9), 4: ("S", 3), 5: ("S", 4)}

    with nc.Block(f"{tag}_main", no_gpsimd_drain=True) as b:
        @b.tensor
        def _(te):
            te.wait_ge(c.sW.h, tok_w)
            te.wait_ge(c.sS.h, c.tok_xlnT[1])
            for i in range(TCH):
                for g in range(6):
                    G = i * 6 + g
                    if G >= 2:
                        pi, pg = divmod(G - 2, 6)
                        eng, j = cons[pg]
                        if eng == "V":
                            te.wait_ge(c.sV.h, tV(pi, j))
                        else:
                            te.wait_ge(c.sS.h, tS(pi, j))
                    for dc in range(DCH):
                        te.matmul(ps[g % 2][:], c.xlnT[:, dc, i * 128:(i + 1) * 128],
                                  wqkv_sb[:, dc, g * 512:(g + 1) * 512],
                                  start=(dc == 0), stop=False)
                    ins = te.matmul(ps[g % 2][:], c.ones[0:1, :],
                                    bqkv_row[:, g * 512:(g + 1) * 512],
                                    start=False, stop=True)
                    assert c.sMM.inc(ins) == tM(i, g + 1)
                te.wait_ge(c.sV.h, tV(i, 10))
                te.wait_ge(c.sV.h, tV(i - 1, 16) if i >= 1 else sV0 + 1)
                for sl in range(2):
                    if i >= 1:
                        te.wait_ge(c.sS.h, tS(i - 1, 7 + sl))
                    te.matmul(ps_cum[sl][:], c.triu[:], k_t[:, sl * 512:(sl + 1) * 512],
                              start=True, stop=False)
                    ins = te.matmul(ps_cum[sl][:], c.ones[0:1, :],
                                    carry_bf[:, sl * 512:(sl + 1) * 512],
                                    start=False, stop=True)
                    assert c.sMM.inc(ins) == tM(i, 7 + sl)
                te.wait_ge(c.sV.h, tV(i, 11))
                for sl in range(2):
                    te.wait_ge(c.sS.h, tS(i, 5 + sl))
                    te.matmul(ps_cum[sl][:], c.triu[:], kv_t[:, sl * 512:(sl + 1) * 512],
                              start=True, stop=False)
                    ins = te.matmul(ps_cum[sl][:], c.ones[0:1, :],
                                    carry_bf[:, D + sl * 512:D + (sl + 1) * 512],
                                    start=False, stop=True)
                    assert c.sMM.inc(ins) == tM(i, 9 + sl)
                for j in range(4):
                    if j == 0:
                        if i >= 1:
                            te.wait_ge(c.sV.h, tV(i - 1, 15))
                    else:
                        te.wait_ge(c.sV.h, tV(i, 11 + j))
                    src = (k_t if j < 2 else kv_t)[:, (j % 2) * 512:(j % 2) * 512 + 512]
                    ins = te.matmul(ps_sum[:], c.ones[:, 0:1], src, start=True, stop=True)
                    assert c.sMM.inc(ins) == tM(i, 11 + j)
                te.wait_ge(c.sV.h, tV(i, 22))
                if i >= 1:
                    te.wait_ge(c.sS.h, tS(i - 1, 9))
                ins = None
                for cc_ in range(DCH):
                    ins = te.transpose(ps_ytr[:, cc_ * 128:(cc_ + 1) * 128],
                                       y_t[:, cc_ * 128:(cc_ + 1) * 128], c.ident[:])
                assert c.sMM.inc(ins) == tM(i, 15)
                te.wait_ge(c.sS.h, tS(i, 9))
                for cs in range(2):
                    if i >= 1:
                        te.wait_ge(c.sV.h, tV(i - 1, 23 + cs))
                    for ec in range(DCH):
                        te.matmul(ps_pr[cs][:], ylT[:, ec, :],
                                  wproj_sb[:, ec, cs * 512:(cs + 1) * 512],
                                  start=(ec == 0), stop=False)
                    ins = te.matmul(ps_pr[cs][:], c.ones[0:1, :],
                                    bproj_row[:, cs * 512:(cs + 1) * 512],
                                    start=False, stop=True)
                    assert c.sMM.inc(ins) == tM(i, 16 + cs)

        @b.scalar
        def _(se):
            for i in range(TCH):
                se.wait_ge(c.sV.h, tV(i, 2))
                ins = se.activation(ea[:], ea[:], AF.Exp)
                assert c.sS.inc(ins) == tS(i, 1)
                se.wait_ge(c.sV.h, tV(i, 7))
                ins = se.activation(ea[:], ea[:], AF.Exp)
                assert c.sS.inc(ins) == tS(i, 2)
                for sl in range(2):
                    se.wait_ge(c.sMM.h, tM(i, 5 + sl))
                    if i >= 1:
                        se.wait_ge(c.sV.h, tV(i - 1, 11))
                    ins = se.copy(out=v_t[:, sl * 512:(sl + 1) * 512],
                                  in_=ps[sl][:])
                    assert c.sS.inc(ins) == tS(i, 3 + sl)
                for sl in range(2):
                    se.wait_ge(c.sMM.h, tM(i, 7 + sl))
                    if i >= 1:
                        se.wait_ge(c.sV.h, tV(i - 1, 17))
                    ins = se.copy(out=kc_t[:, sl * 512:(sl + 1) * 512], in_=ps_cum[sl][:])
                    assert c.sS.inc(ins) == tS(i, 5 + sl)
                for sl in range(2):
                    se.wait_ge(c.sMM.h, tM(i, 9 + sl))
                    if i >= 1:
                        se.wait_ge(c.sV.h, tV(i - 1, 21))
                    ins = se.copy(out=kvc_t[:, sl * 512:(sl + 1) * 512], in_=ps_cum[sl][:])
                    assert c.sS.inc(ins) == tS(i, 7 + sl)
                se.wait_ge(c.sMM.h, tM(i, 15))
                if i >= 1:
                    se.wait_ge(c.sMM.h, tM(i - 1, 17))
                ins = se.copy(out=ylT[:],
                              in_=ps_ytr[:].rearrange("p (cc q) -> p cc q", cc=DCH))
                assert c.sS.inc(ins) == tS(i, 9)

        @b.vector
        def _(ve):
            ve.memset(carry[:], 0.0)
            ins = ve.memset(carry_bf[:], 0.0)
            assert c.sV.inc(ins) == sV0 + 1
            ve.wait_ge(c.sX.h, tok_sc)
            for i in range(TCH):
                bank_q = [0, 1]
                for sl in range(2):
                    ve.wait_ge(c.sMM.h, tM(i, 1 + sl))
                    ins = ve.tensor_scalar(ea[:, sl * 512:(sl + 1) * 512],
                                           ps[bank_q[sl]][:], scale_col[:], 0.0,
                                           ALU.mult, ALU.min)
                    assert c.sV.inc(ins) == tV(i, 1 + sl)
                for sl in range(2):
                    ins = ve.tensor_scalar(eb[:, sl * 512:(sl + 1) * 512],
                                           ps[bank_q[sl]][:], scale_col[:], 0.0,
                                           ALU.mult, ALU.max)
                    assert c.sV.inc(ins) == tV(i, 3 + sl)
                ve.wait_ge(c.sS.h, tS(i, 1))
                ins = ve.tensor_add(q_t[:], ea[:], eb[:])
                assert c.sV.inc(ins) == tV(i, 5)
                bank_k = [0, 1]
                for sl in range(2):
                    ve.wait_ge(c.sMM.h, tM(i, 3 + sl))
                    ins = ve.tensor_scalar(ea[:, sl * 512:(sl + 1) * 512],
                                           ps[bank_k[sl]][:], scale_col[:], 0.0,
                                           ALU.mult, ALU.min)
                    assert c.sV.inc(ins) == tV(i, 6 + sl)
                for sl in range(2):
                    ins = ve.tensor_scalar(eb[:, sl * 512:(sl + 1) * 512],
                                           ps[bank_k[sl]][:], scale_col[:], 0.0,
                                           ALU.mult, ALU.max)
                    assert c.sV.inc(ins) == tV(i, 8 + sl)
                ve.wait_ge(c.sS.h, tS(i, 2))
                ins = ve.tensor_add(k_t[:], ea[:], eb[:])
                assert c.sV.inc(ins) == tV(i, 10)
                ve.wait_ge(c.sS.h, tS(i, 4))
                ins = ve.tensor_mul(kv_t[:], k_t[:], v_t[:])
                assert c.sV.inc(ins) == tV(i, 11)
                for j in range(4):
                    ve.wait_ge(c.sMM.h, tM(i, 11 + j))
                    off = (0 if j < 2 else D) + (j % 2) * 512
                    ins = ve.tensor_add(carry[:, off:off + 512], carry[:, off:off + 512],
                                        ps_sum[:])
                    assert c.sV.inc(ins) == tV(i, 12 + j)
                ve.drain()
                ve.wait_ge(c.sMM.h, tM(i, 10))
                ins = ve.tensor_copy(out=carry_bf[:], in_=carry[:])
                assert c.sV.inc(ins) == tV(i, 16)
                ve.wait_ge(c.sS.h, tS(i, 6))
                ins = ve.tensor_mul(ea[:], q_t[:], kc_t[:])
                assert c.sV.inc(ins) == tV(i, 17)
                ve.drain()
                ins = ve.reduce_sum(den_ch[:, :, None],
                                    ea[:].rearrange("p (h d) -> p h d", h=16), AX.X)
                assert c.sV.inc(ins) == tV(i, 18)
                ve.drain()
                ins = ve.tensor_scalar_max(den_ch[:], den_ch[:], EPS)
                assert c.sV.inc(ins) == tV(i, 19)
                ve.drain()
                ins = ve.reciprocal(rden_ch[:], den_ch[:])
                assert c.sV.inc(ins) == tV(i, 20)
                ve.drain()
                ve.wait_ge(c.sS.h, tS(i, 8))
                ins = ve.tensor_mul(ea[:], q_t[:], kvc_t[:])
                assert c.sV.inc(ins) == tV(i, 21)
                if i >= 1:
                    ve.wait_ge(c.sMM.h, tM(i - 1, 15))
                ins = ve.tensor_tensor(y_t[:].rearrange("p (h d) -> p h d", h=16),
                                       ea[:].rearrange("p (h d) -> p h d", h=16),
                                       rden_ch[:, :, None].broadcast_to([128, 16, 64]),
                                       ALU.mult)
                assert c.sV.inc(ins) == tV(i, 22)
                for cs in range(2):
                    ve.wait_ge(c.sMM.h, tM(i, 16 + cs))
                    ins = ve.tensor_add(c.xres[:, i, cs * 512:(cs + 1) * 512],
                                        c.xres[:, i, cs * 512:(cs + 1) * 512],
                                        ps_pr[cs][:])
                    assert c.sV.inc(ins) == tV(i, 23 + cs)

    exp.close()
    c.stream_deps = [(c.sV, c.sV.v)]
    c.fence_deps = c.barrier_tokens()
    ex.close()


def emit_l1ffn(c, w, dram, tag):
    nc = c.nc
    emit_ln(c, tag)
    fence = list(c.fence_deps)
    NE, FF = 8, 4096
    FC = FF // 128
    NSPAN = 2
    ex = ExitStack()
    sb = lambda name, shape, dt: ex.enter_context(nc.sbuf_tensor(f"{tag}_{name}", shape, dt))
    wr_sb = sb("wr", [128, DCH, NE], BF16)
    br_row = sb("br", [1, NE], BF16)
    b1_cols = sb("b1c", [128, NE * FC], F32)
    b2_sb = sb("b2", [8, D], BF16)
    lgr = sb("lgr", [128, TCH, NE], F32)
    Xs = sb("Xs", [128, TCH, NE], F32)
    diff = sb("diff", [128, TCH, NE], F32)
    sq = sb("sq", [128, TCH, NE], F32)
    gates = sb("gates", [128, TCH, NE], F32)
    gates_bf = sb("gatesb", [128, TCH, NE], BF16)
    tau_lo = sb("taulo", [128, TCH], F32)
    tau_m = sb("taum", [128, TCH], F32)
    f_lo = sb("flo", [128, TCH], F32)
    fm = sb("fm", [128, TCH], F32)
    prod = sb("prod", [128, TCH], F32)
    maskb = sb("maskb", [128, TCH], mybir.dt.uint8)
    gT_sb = sb("gT", [8, TCH, 128], BF16)
    h_sb = sb("h", [128, FC, 1024], BF16)
    w1rot = [sb(f"w1r{i}", [128, DCH, 256], BF16) for i in range(3)]
    w2rot = [sb(f"w2r{i}", [128, 2, 512], BF16) for i in range(3)]

    exp_r = ExitStack()
    ps_r = [c.psum(exp_r, f"{tag}_psr{i}", [128, NE]) for i in range(2)]
    ps_gt = c.psum(exp_r, f"{tag}_psgt", [8, 128], BF16)

    with nc.Block(f"{tag}_load", no_gpsimd_drain=True) as b:
        @b.gpsimd
        def _(ge):
            _wait(ge, fence)
            c.sW.inc(ge.dma_start(out=wr_sb[:], in_=w["l1f_wrt"]), 16)
            c.sW.inc(ge.dma_start(out=br_row[:], in_=w["l1f_br"]), 16)
            c.sW.inc(ge.dma_start(out=b2_sb[:], in_=w["l1f_b2"]), 16)

        @b.sync
        def _(sy):
            _wait(sy, fence)
            c.sX.inc(sy.dma_start(out=b1_cols[:], in_=w["l1f_b1_cols"]), 16)
    tok_sm = c.sW.v
    tok_b1 = c.sX.v

    sMM0, sV0, sS0 = c.sMM.v, c.sV.v, c.sS.v
    rmm = lambda i: sMM0 + i + 1
    rev = lambda i: sV0 + i + 1
    with nc.Block(f"{tag}_router", no_gpsimd_drain=True) as b:
        @b.tensor
        def _(te):
            _wait(te, fence)
            te.wait_ge(c.sW.h, tok_sm)
            te.wait_ge(c.sS.h, c.tok_xlnT[1])
            for i in range(TCH):
                if i >= 2:
                    te.wait_ge(c.sV.h, rev(i - 2))
                for dc in range(DCH):
                    te.matmul(ps_r[i % 2][:], c.xlnT[:, dc, i * 128:(i + 1) * 128],
                              wr_sb[:, dc, :], start=(dc == 0), stop=False)
                ins = te.matmul(ps_r[i % 2][:], c.ones[0:1, :], br_row[:],
                                start=False, stop=True)
                assert c.sMM.inc(ins) == rmm(i)

        @b.vector
        def _(ve):
            for i in range(TCH):
                ve.wait_ge(c.sMM.h, rmm(i))
                ins = ve.tensor_copy(out=lgr[:, i, :], in_=ps_r[i % 2][:])
                assert c.sV.inc(ins) == rev(i)
            bcast = lambda t: t[:, :, None].broadcast_to([128, TCH, NE])
            ve.drain()
            ve.tensor_scalar_mul(Xs[:], lgr[:], 0.5)
            ve.drain()
            ve.reduce_max(tau_lo[:, :, None], Xs[:], AX.X)
            ve.drain()
            ve.tensor_scalar_add(tau_lo[:], tau_lo[:], -1.0)
            ve.drain()
            ve.tensor_tensor(diff[:], Xs[:], bcast(tau_lo), ALU.subtract)
            ve.scalar_tensor_tensor(sq[:], diff[:], 0.0, diff[:], ALU.max, ALU.mult)
            ve.drain()
            ve.reduce_sum(f_lo[:, :, None], sq[:], AX.X)
            ve.drain()
            ve.tensor_scalar_add(f_lo[:], f_lo[:], -1.0)
            ve.drain()
            dmc = float(1.0 - (1.0 / NE) ** 0.5)
            for it in range(25):
                dmi = dmc * (0.5 ** (it + 1))
                ve.tensor_scalar_add(tau_m[:], tau_lo[:], dmi)
                ve.drain()
                ve.tensor_tensor(diff[:], Xs[:], bcast(tau_m), ALU.subtract)
                ve.scalar_tensor_tensor(sq[:], diff[:], 0.0, diff[:], ALU.max, ALU.mult)
                ve.drain()
                ve.reduce_sum(fm[:, :, None], sq[:], AX.X)
                ve.drain()
                ve.tensor_scalar_add(fm[:], fm[:], -1.0)
                ve.drain()
                ve.tensor_mul(prod[:], fm[:], f_lo[:])
                ve.drain()
                ve.tensor_scalar(maskb[:], prod[:], 0.0, None, ALU.is_ge)
                ve.drain()
                ve.copy_predicated(tau_lo[:], maskb[:], tau_m[:])
                ve.drain()
            ve.tensor_tensor(diff[:], Xs[:], bcast(tau_m), ALU.subtract)
            ve.scalar_tensor_tensor(sq[:], diff[:], 0.0, diff[:], ALU.max, ALU.mult)
            ve.drain()
            ve.reduce_sum(fm[:, :, None], sq[:], AX.X)
            ve.drain()
            ve.reciprocal(fm[:], fm[:])
            ve.drain()
            ve.tensor_tensor(gates[:], sq[:], bcast(fm), ALU.mult)
            ve.drain()
            ve.tensor_scalar(diff[:], gates[:], 1e-9, None, ALU.is_gt)
            ve.drain()
            ve.tensor_mul(gates[:], gates[:], diff[:])
            ve.drain()
            ins = ve.tensor_copy(out=gates_bf[:], in_=gates[:])
            c.sV.inc(ins)
    tok_gates = c.sV.v

    sMM0, sS0 = c.sMM.v, c.sS.v
    with nc.Block(f"{tag}_gt", no_gpsimd_drain=True) as b:
        @b.tensor
        def _(te):
            te.wait_ge(c.sV.h, tok_gates)
            for i in range(TCH):
                if i >= 1:
                    te.wait_ge(c.sS.h, sS0 + i)
                ins = te.transpose(ps_gt[:], gates_bf[:, i, :], c.ident[:])
                assert c.sMM.inc(ins) == sMM0 + i + 1

        @b.scalar
        def _(se):
            for i in range(TCH):
                se.wait_ge(c.sMM.h, sMM0 + i + 1)
                ins = se.copy(out=gT_sb[:, i, :], in_=ps_gt[:])
                assert c.sS.inc(ins) == sS0 + i + 1
    tok_gT = c.sS.v
    exp_r.close()
    exp = ExitStack()
    ps_hy = [c.psum(exp, f"{tag}_pshy{i}", [128, 512]) for i in range(8)]

    w1 = w["l1f_w1"]
    w2 = w["l1f_w2"]
    sMM0, sS0, sV0, sW0 = c.sMM.v, c.sS.v, c.sV.v, c.sW.v
    NT_e, NS_e, NV_e, NW_e = 144, 64, 16, (16 + 32) * 16
    tMb = lambda spe: sMM0 + spe * NT_e
    hmmt = lambda spe, fc, tt: tMb(spe) + fc * 2 + tt + 1
    yfct = lambda spe, dd, fc: tMb(spe) + 64 + dd * 40 + fc + 1
    ybct = lambda spe, dd, tcol: tMb(spe) + 64 + dd * 40 + 32 + tcol + 1
    hevt = lambda spe, g: sS0 + spe * NS_e + g + 1
    yvt = lambda spe, dd, tcol: sV0 + spe * NV_e + dd * 8 + tcol + 1
    w1t = lambda spe, fc: sW0 + spe * NW_e + (fc // 2 + 1) * 16
    w2t = lambda spe, dd, fc: sW0 + spe * NW_e + 16 * 16 + ((dd * 32 + fc) // 2 + 1) * 16

    with nc.Block(f"{tag}_experts", no_gpsimd_drain=True) as b:
        @b.gpsimd
        def _(ge):
            for spe in range(NSPAN * NE):
                sp, e = divmod(spe, NE)
                src1 = w1[e].rearrange("(dc p) f -> p dc f", p=128)
                for fp in range(FC // 2):
                    fc = 2 * fp
                    if fp >= 3:
                        ge.wait_ge(c.sMM.h, hmmt(spe, (fc - 6) * 2 // 2 * 2 + 2 * 2 - 1)
                                   ) if False else ge.wait_ge(c.sMM.h, hmmt(spe, (fp - 3) * 2 + 1, 1))
                    elif spe > 0:
                        ge.wait_ge(c.sMM.h, hmmt(spe - 1, (fp + 13) * 2 + 1, 1))
                    ins = ge.dma_start(out=w1rot[fp % 3][:],
                                       in_=src1[:, :, fc * 128:(fc + 2) * 128])
                    assert c.sW.inc(ins, 16) == w1t(spe, fc)
                src2 = w2[e].rearrange("(fc p) d -> p fc d", p=128)
                for dd in range(2):
                    for fp in range(FC // 2):
                        fc = 2 * fp
                        k = dd * 16 + fp
                        if k >= 3:
                            pd, pf = divmod(k - 3, 16)
                            ge.wait_ge(c.sMM.h, yfct(spe, pd, 2 * pf + 1))
                        elif spe > 0:
                            pd, pf = divmod(k + 29, 16)
                            ge.wait_ge(c.sMM.h, yfct(spe - 1, pd, 2 * pf + 1))
                        ins = ge.dma_start(out=w2rot[k % 3][:],
                                           in_=src2[:, fc:fc + 2, dd * 512:(dd + 1) * 512])
                        assert c.sW.inc(ins, 16) == w2t(spe, dd, fc)

        @b.tensor
        def _(te):
            for spe in range(NSPAN * NE):
                sp, e = divmod(spe, NE)
                t0 = sp * 8
                if spe == 0:
                    te.wait_ge(c.sS.h, tok_gT)
                    te.wait_ge(c.sW.h, tok_sm)
                elif True:
                    te.wait_ge(c.sV.h, yvt(spe - 1, 1, 7))
                for fc in range(FC):
                    te.wait_ge(c.sW.h, w1t(spe, fc))
                    for tt in range(2):
                        g = fc * 2 + tt
                        if g >= 2:
                            te.wait_ge(c.sS.h, hevt(spe, g - 2))
                        elif spe > 0:
                            te.wait_ge(c.sS.h, hevt(spe - 1, 62 + g))
                        ins = None
                        for dc in range(DCH):
                            ins = te.matmul(ps_hy[g % 2][:],
                                            w1rot[(fc // 2) % 3][:, dc, (fc % 2) * 128:(fc % 2) * 128 + 128],
                                            c.xlnT[:, dc, sp * 1024 + tt * 512:sp * 1024 + (tt + 1) * 512],
                                            start=(dc == 0), stop=(dc == DCH - 1))
                        assert c.sMM.inc(ins) == hmmt(spe, fc, tt)
                te.wait_ge(c.sS.h, hevt(spe, 63))
                for dd in range(2):
                    for fc in range(FC):
                        te.wait_ge(c.sW.h, w2t(spe, dd, fc))
                        if dd == 1 and fc == 0:
                            for tcol in range(8):
                                te.wait_ge(c.sV.h, yvt(spe, 0, tcol))
                        ins = None
                        for tcol in range(8):
                            ins = te.matmul(ps_hy[tcol][:],
                                            h_sb[:, fc, tcol * 128:(tcol + 1) * 128],
                                            w2rot[((dd * 32 + fc) // 2) % 3][:, fc % 2, :],
                                            start=(fc == 0), stop=False)
                        assert c.sMM.inc(ins) == yfct(spe, dd, fc)
                    for tcol in range(8):
                        ins = te.matmul(ps_hy[tcol][:], gT_sb[:, t0 + tcol, :],
                                        b2_sb[:, dd * 512:(dd + 1) * 512],
                                        start=False, stop=True)
                        assert c.sMM.inc(ins) == ybct(spe, dd, tcol)

        @b.scalar
        def _(se):
            se.wait_ge(c.sX.h, tok_b1)
            for spe in range(NSPAN * NE):
                sp, e = divmod(spe, NE)
                if spe > 0:
                    se.wait_ge(c.sMM.h, yfct(spe - 1, 1, 31))
                for g in range(64):
                    fc, tt = g // 2, g % 2
                    se.wait_ge(c.sMM.h, hmmt(spe, fc, tt))
                    ins = se.activation(h_sb[:, fc, tt * 512:(tt + 1) * 512],
                                        ps_hy[g % 2][:], AF.Relu,
                                        bias=b1_cols[:, e * FC + fc:e * FC + fc + 1])
                    assert c.sS.inc(ins) == hevt(spe, g)

        @b.vector
        def _(ve):
            for spe in range(NSPAN * NE):
                sp, e = divmod(spe, NE)
                t0 = sp * 8
                for dd in range(2):
                    for tcol in range(8):
                        ve.wait_ge(c.sMM.h, ybct(spe, dd, tcol))
                        ins = ve.scalar_tensor_tensor(
                            c.xres[:, t0 + tcol, dd * 512:(dd + 1) * 512],
                            ps_hy[tcol][:], gates[:, t0 + tcol, e:e + 1],
                            c.xres[:, t0 + tcol, dd * 512:(dd + 1) * 512],
                            ALU.mult, ALU.add)
                        assert c.sV.inc(ins) == yvt(spe, dd, tcol)

    exp.close()
    c.stream_deps = [(c.sV, c.sV.v)]
    c.fence_deps = c.barrier_tokens()
    ex.close()


def emit_l1b(c, w, dram, tag="s6"):
    nc = c.nc
    emit_ln(c, tag)
    fence = list(c.fence_deps)
    NH = 16
    ex = ExitStack()
    sb = lambda name, shape, dt: ex.enter_context(nc.sbuf_tensor(f"{tag}_{name}", shape, dt))
    K_fm = sb("kfm", [128, DCH, T], BF16)
    V_sb = sb("vsb", [128, TCH, NH, 65], BF16)
    hind = sb("hind", [128, DCH, NH], BF16)
    bq_cols = sb("bqc", [128, DCH], F32)
    bk_cols = sb("bkc", [128, DCH], F32)
    bqs_cols = sb("bqsc", [128, DCH], F32)
    bv_row = sb("bvr", [1, D], BF16)
    bo_row = sb("bor", [1, D], BF16)
    ascale = sb("asc", [128, 1], F32)
    sq1 = sb("sq1", [128, 1024], BF16)
    nk2m = sb("nk2m", [16, 1], F32)
    sk = sb("sk", [16, 1], F32)
    cq2 = sb("cq2", [16, 1], F32)
    cq = sb("cq", [16, 1], F32)
    ch_bf = sb("chbf", [16, 1], BF16)
    chrow = sb("chrow", [1, 16], BF16)
    cneg = sb("cneg", [128, 16], F32)
    rd_row = sb("rdrow", [1, 512], BF16)
    rb = sb("rbb", [64, 512], BF16)

    ex2 = ExitStack()
    wkv_sb = ex2.enter_context(nc.sbuf_tensor(f"{tag}_wkv", [128, DCH, D], BF16))

    with nc.Block(f"{tag}_load", no_gpsimd_drain=True) as b:
        @b.gpsimd
        def _(ge):
            _wait(ge, fence)
            src = w["l1b_wkt"].rearrange("(dc p) e -> p dc e", p=128)
            for dc in range(DCH):
                c.sW.inc(ge.dma_start(out=wkv_sb[:, dc, :], in_=src[:, dc, :]), 16)
            c.sW.inc(ge.dma_start(out=bv_row[:], in_=w["l1b_bv"]), 16)
            c.sW.inc(ge.dma_start(out=bo_row[:], in_=w["l1b_bo"]), 16)
            c.sW.inc(ge.dma_start(out=hind[:], in_=w["l1b_hind"]), 16)

        @b.sync
        def _(sy):
            _wait(sy, fence)
            c.sX.inc(sy.dma_start(out=bq_cols[:], in_=w["l1b_bq_cols"]), 16)
            c.sX.inc(sy.dma_start(out=bk_cols[:], in_=w["l1b_bk_cols"]), 16)
            c.sX.inc(sy.dma_start(out=ascale[:], in_=w["l1b_ascale_col"]), 16)
    tok_w1 = c.sW.v
    tok_cols = c.sX.v

    exp1 = ExitStack()
    ps_kv = [c.psum(exp1, f"{tag}_pskv{i}", [128, 512]) for i in range(4)]
    ps_nk = [c.psum(exp1, f"{tag}_psnk{i}", [16, 512]) for i in range(4)]

    sMM0, sS0, sV0 = c.sMM.v, c.sS.v, c.sV.v
    kmm = lambda g: sMM0 + g + 1
    kev = lambda g: sV0 + 2 + g + 1
    vmm = lambda g: sMM0 + 32 + g + 1
    vev = lambda g: sS0 + g + 1
    sqe = lambda ec, hf: sS0 + 32 + ec * 2 + hf + 1
    nkmm = lambda ec, hf: sMM0 + 64 + ec * 2 + hf + 1

    tok_wv = c.sW.v + 8 * 16
    with nc.Block(f"{tag}_p1", no_gpsimd_drain=True) as b:
        @b.gpsimd
        def _(ge):
            ge.wait_ge(c.sMM.h, kmm(31))
            src = w["l1b_wvt"].rearrange("(dc p) e -> p dc e", p=128)
            for dc in range(DCH):
                c.sW.inc(ge.dma_start(out=wkv_sb[:, dc, :], in_=src[:, dc, :]), 16)
            assert c.sW.v == tok_wv

        @b.tensor
        def _(te):
            _wait(te, fence)
            te.wait_ge(c.sW.h, tok_w1)
            te.wait_ge(c.sS.h, c.tok_xlnT[1])
            for g in range(32):
                ec, tt = g // 4, g % 4
                if g >= 2:
                    te.wait_ge(c.sV.h, kev(g - 2))
                ins = None
                for dc in range(DCH):
                    ins = te.matmul(ps_kv[g % 2][:], wkv_sb[:, dc, ec * 128:(ec + 1) * 128],
                                    c.xlnT[:, dc, tt * 512:(tt + 1) * 512],
                                    start=(dc == 0), stop=(dc == DCH - 1))
                assert c.sMM.inc(ins) == kmm(g)
            te.wait_ge(c.sW.h, tok_wv)
            for g in range(32):
                i, ee = g // 2, g % 2
                if g >= 2:
                    te.wait_ge(c.sS.h, vev(g - 2))
                for dc in range(DCH):
                    te.matmul(ps_kv[2 + g % 2][:], c.xlnT[:, dc, i * 128:(i + 1) * 128],
                              wkv_sb[:, dc, ee * 512:(ee + 1) * 512],
                              start=(dc == 0), stop=False)
                ins = te.matmul(ps_kv[2 + g % 2][:], c.ones[0:1, :],
                                bv_row[:, ee * 512:(ee + 1) * 512], start=False, stop=True)
                assert c.sMM.inc(ins) == vmm(g)
            for ec in range(DCH):
                for hf in range(2):
                    te.wait_ge(c.sS.h, sqe(ec, hf))
                    ins = None
                    for tt in range(2):
                        ins = te.matmul(ps_nk[hf * 2 + tt][:], hind[:, ec, :],
                                        sq1[:, tt * 512:(tt + 1) * 512],
                                        start=(ec == 0), stop=(ec == DCH - 1))
                    assert c.sMM.inc(ins) == nkmm(ec, hf)

        @b.vector
        def _(ve):
            ve.wait_ge(c.sX.h, tok_cols)
            ins = ve.tensor_mul(bqs_cols[:], bq_cols[:], ascale[:].broadcast_to([128, DCH]))
            assert c.sV.inc(ins) == sV0 + 1
            ins = ve.memset(V_sb[:, :, :, 64:65], 1.0)
            assert c.sV.inc(ins) == sV0 + 2
            for g in range(32):
                ec, tt = g // 4, g % 4
                ve.wait_ge(c.sMM.h, kmm(g))
                ins = ve.tensor_scalar_add(K_fm[:, ec, tt * 512:(tt + 1) * 512],
                                           ps_kv[g % 2][:], bk_cols[:, ec:ec + 1])
                assert c.sV.inc(ins) == kev(g)
            ve.wait_ge(c.sMM.h, nkmm(DCH - 1, 1))
            for tt in range(4):
                if tt == 0:
                    ve.reduce_max(nk2m[:], ps_nk[tt][:], AX.X)
                else:
                    ve.reduce_max(cq2[:], ps_nk[tt][:], AX.X)
                    ve.drain()
                    ve.tensor_max(nk2m[:], nk2m[:], cq2[:])
                ve.drain()
            ins = ve.tensor_copy(out=sk[:], in_=nk2m[:])
            c.sV.inc(ins)

        @b.scalar
        def _(se):
            for g in range(32):
                i, ee = g // 2, g % 2
                se.wait_ge(c.sMM.h, vmm(g))
                dst = V_sb[:, i, ee * 8:(ee + 1) * 8, 0:64]
                ins = se.copy(out=dst,
                              in_=ps_kv[2 + g % 2][:].rearrange("p (h d) -> p h d", h=8))
                assert c.sS.inc(ins) == vev(g)
            for ec in range(DCH):
                se.wait_ge(c.sV.h, kev(ec * 4 + 3))
                for hf in range(2):
                    if ec * 2 + hf >= 2:
                        pe, ph = divmod(ec * 2 + hf - 2, 2)
                        se.wait_ge(c.sMM.h, nkmm(pe, ph))
                    ins = se.activation(sq1[:], K_fm[:, ec, hf * 1024:(hf + 1) * 1024],
                                        AF.Square)
                    assert c.sS.inc(ins) == sqe(ec, hf)
    tok_p1v = c.sV.v

    with nc.Block(f"{tag}_sk", no_gpsimd_drain=True) as b:
        @b.scalar
        def _(se):
            se.wait_ge(c.sV.h, tok_p1v)
            se.drain()
            ins = se.activation(sk[:], sk[:], AF.Sqrt)
            c.sS.inc(ins)
    tok_sk = c.sS.v
    exp1.close()
    ex2.close()

    ex3 = ExitStack()
    sb3 = lambda name, shape, dt: ex3.enter_context(nc.sbuf_tensor(f"{tag}_{name}", shape, dt))
    mask_sb = sb3("masks", [128, 4, 512], BF16)
    wo_half = sb3("woh", [128, DCH, 512], BF16)
    Q_ch = sb3("qch", [128, DCH, 512], BF16)
    attn = sb3("attn", [128, DCH, 512], BF16)
    expT = [sb3(f"expT{i}", [128, 512], BF16) for i in range(3)]
    wq_rot = [sb3(f"wqr{i}", [128, 128], BF16) for i in range(2)]

    exp2 = ExitStack()
    ps_s = [c.psum(exp2, f"{tag}_pss{i}", [128, 512]) for i in range(2)]
    ps_a = [c.psum(exp2, f"{tag}_psa{i}", [65, 512]) for i in range(2)]
    ps_o = [c.psum(exp2, f"{tag}_pso{i}", [128, 512]) for i in range(2)]
    ps_n = c.psum(exp2, f"{tag}_psn", [16, 512])
    ps_cn = c.psum(exp2, f"{tag}_pscn", [128, 16])

    with nc.Block(f"{tag}_mload", no_gpsimd_drain=True) as b:
        @b.gpsimd
        def _(ge):
            ge.wait_ge(c.sMM.h, c.sMM.v)
            c.sW.inc(ge.dma_start(out=mask_sb[:], in_=w["l1b_masks"]), 16)
    tok_mask = c.sW.v

    NT_j = 64 + 8 + 1 + NH * 33 + 8
    NS_j = 8 + 1 + NH * 16
    NV_j = 8 + 1 + 1 + 1 + NH * 7 + 8
    NW_j = 64 * 16 + 2 * 8 * 16
    NX_j = 32
    sMM1, sS1, sV1, sW1, sX1 = c.sMM.v, c.sS.v, c.sV.v, c.sW.v, c.sX.v
    qmk = lambda j, ec, dc: sMM1 + j * NT_j + ec * 8 + dc + 1
    qmm = lambda j, ec: sMM1 + j * NT_j + ec * 8 + 8
    nmm = lambda j, ec: sMM1 + j * NT_j + 64 + ec + 1
    cbm = lambda j: sMM1 + j * NT_j + 73
    _spos = lambda kt: 8 * (kt // 4) + kt % 4 + 1
    _apos = lambda kt: 8 * (kt // 4) + 4 + kt % 4 + 1
    smm = lambda j, h, kt: sMM1 + j * NT_j + 73 + h * 33 + _spos(kt)
    amk = lambda j, h, kt: sMM1 + j * NT_j + 73 + h * 33 + _apos(kt)
    bmm = lambda j, h: sMM1 + j * NT_j + 73 + h * 33 + 33
    omm = lambda j, g: sMM1 + j * NT_j + 73 + NH * 33 + g + 1
    sqq = lambda j, ec: sS1 + j * NS_j + ec + 1
    cqs = lambda j: sS1 + j * NS_j + 9
    eev = lambda j, h, kt: sS1 + j * NS_j + 9 + h * 16 + kt + 1
    qev = lambda j, ec: sV1 + j * NV_j + ec + 1
    nqv = lambda j: sV1 + j * NV_j + 9
    chv = lambda j: sV1 + j * NV_j + 10
    cnv = lambda j: sV1 + j * NV_j + 11
    mskv = lambda j, h, oi: sV1 + j * NV_j + 11 + h * 7 + oi + 1
    rcp = lambda j, h: sV1 + j * NV_j + 11 + h * 7 + 5
    rbc = lambda j, h: sV1 + j * NV_j + 11 + h * 7 + 6
    atv = lambda j, h: sV1 + j * NV_j + 11 + h * 7 + 7
    oev = lambda j, g: sV1 + j * NV_j + 11 + NH * 7 + g + 1
    wqt = lambda j, blk: sW1 + j * NW_j + (blk + 1) * 16
    wot = lambda j, dd: sW1 + j * NW_j + 64 * 16 + (dd + 1) * 8 * 16
    xrow = lambda j, n: sX1 + j * NX_j + n * 16

    with nc.Block(f"{tag}_qloop", no_gpsimd_drain=True) as b:
        @b.gpsimd
        def _(ge):
            for j in range(4):
                src = w["l1b_wqt"].rearrange("(dc p) e -> p dc e", p=128)
                for blk in range(64):
                    ec, dc = divmod(blk, 8)
                    if blk >= 2:
                        pe_, pd_ = divmod(blk - 2, 8)
                        ge.wait_ge(c.sMM.h, qmk(j, pe_, pd_))
                    elif j > 0:
                        pe_, pd_ = divmod(blk + 62, 8)
                        ge.wait_ge(c.sMM.h, qmk(j - 1, pe_, pd_))
                    ins = ge.dma_start(out=wq_rot[blk % 2][:],
                                       in_=src[:, dc, ec * 128:(ec + 1) * 128])
                    assert c.sW.inc(ins, 16) == wqt(j, blk)
                srco = w["l1b_wot"].rearrange("(dc p) e -> p dc e", p=128)
                for dd in range(2):
                    if j > 0 or dd > 0:
                        pj, pdd = (j, 0) if dd == 1 else (j - 1, 1)
                        ge.wait_ge(c.sMM.h, omm(pj, pdd * 4 + 3))
                    for dc in range(DCH):
                        c.sW.inc(ge.dma_start(out=wo_half[:, dc, :],
                                              in_=srco[:, dc, dd * 512:(dd + 1) * 512]), 16)
                    assert c.sW.v == wot(j, dd)

        @b.sync
        def _(sy):
            for j in range(4):
                sy.wait_ge(c.sV.h, chv(j))
                ins = sy.dma_start(out=dram["l1b_scr"][:], in_=ch_bf[:])
                assert c.sX.inc(ins, 16) == xrow(j, 1)
                sy.wait_ge(c.sX.h, xrow(j, 1))
                if j > 0:
                    sy.wait_ge(c.sMM.h, cbm(j - 1))
                ins = sy.dma_start(out=chrow[:], in_=dram["l1b_scr"].rearrange("s one -> one s"))
                assert c.sX.inc(ins, 16) == xrow(j, 2)

        @b.tensor
        def _(te):
            for j in range(4):
                if j == 0:
                    te.wait_ge(c.sS.h, tok_sk)
                    te.wait_ge(c.sW.h, tok_mask)
                for ec in range(DCH):
                    if ec >= 2:
                        te.wait_ge(c.sV.h, qev(j, ec - 2))
                    elif j > 0:
                        te.wait_ge(c.sS.h, eev(j - 1, 15, 15))
                    for dc in range(DCH):
                        blk = ec * 8 + dc
                        te.wait_ge(c.sW.h, wqt(j, blk))
                        ins = te.matmul(ps_s[ec % 2][:], wq_rot[blk % 2][:],
                                        c.xlnT[:, dc, j * 512:(j + 1) * 512],
                                        start=(dc == 0), stop=(dc == DCH - 1))
                        assert c.sMM.inc(ins) == qmk(j, ec, dc)
                for ec in range(DCH):
                    te.wait_ge(c.sS.h, sqq(j, ec))
                    if ec == 0 and j > 0:
                        te.wait_ge(c.sV.h, nqv(j - 1))
                    ins = te.matmul(ps_n[:], hind[:, ec, :],
                                    sq1[:, (ec % 2) * 512:(ec % 2) * 512 + 512],
                                    start=(ec == 0), stop=(ec == DCH - 1))
                    assert c.sMM.inc(ins) == nmm(j, ec)
                te.wait_ge(c.sX.h, xrow(j, 2))
                if j > 0:
                    te.wait_ge(c.sV.h, cnv(j - 1))
                ins = te.matmul(ps_cn[:], c.ones[0:1, :], chrow[:], start=True, stop=True)
                assert c.sMM.inc(ins) == cbm(j)
                for h in range(NH):
                    base = (h % 2) * 64
                    hc = h // 2

                    def _scores(kt, h=h, base=base, hc=hc, j=j):
                        if kt >= 2:
                            te.wait_ge(c.sS.h, eev(j, h, kt - 2))
                        elif h > 0:
                            te.wait_ge(c.sS.h, eev(j, h - 1, 14 + kt))
                        else:
                            te.wait_ge(c.sV.h, qev(j, 7))
                        ins = te.matmul(ps_s[kt % 2][:],
                                        K_fm[base:base + 64, hc, kt * 128:(kt + 1) * 128],
                                        Q_ch[base:base + 64, hc, :],
                                        start=True, stop=True)
                        assert c.sMM.inc(ins) == smm(j, h, kt)

                    def _att(kt, h=h, j=j):
                        if kt == 0:
                            if h >= 2:
                                te.wait_ge(c.sV.h, atv(j, h - 2))
                            elif j > 0:
                                te.wait_ge(c.sV.h, atv(j - 1, 14 + h))
                        oi = kt - 4 * j
                        if 0 <= oi < 4:
                            te.wait_ge(c.sV.h, mskv(j, h, oi))
                        else:
                            te.wait_ge(c.sS.h, eev(j, h, kt))
                        ins = te.matmul(ps_a[h % 2][:], V_sb[:, kt, h, :], expT[kt % 3][:],
                                        start=(kt == 0), stop=(kt == TCH - 1))
                        assert c.sMM.inc(ins) == amk(j, h, kt)

                    for gb in range(4):
                        for ii in range(4):
                            _scores(4 * gb + ii)
                        for ii in range(4):
                            _att(4 * gb + ii)
                    te.wait_ge(c.sV.h, rcp(j, h))
                    if h == 0 and j > 0:
                        te.wait_ge(c.sV.h, oev(j - 1, 7))
                    ins = te.matmul(ps_o[0][0:64, :], c.ones[0:1, 0:64], rd_row[:],
                                    start=True, stop=True)
                    assert c.sMM.inc(ins) == bmm(j, h)
                for g in range(8):
                    dd, tcol = g // 4, g % 4
                    te.wait_ge(c.sW.h, wot(j, dd))
                    if g >= 2:
                        te.wait_ge(c.sV.h, oev(j, g - 2))
                    else:
                        te.wait_ge(c.sV.h, atv(j, 15))
                    for dc in range(DCH):
                        te.matmul(ps_o[g % 2][:], attn[:, dc, tcol * 128:(tcol + 1) * 128],
                                  wo_half[:, dc, :],
                                  start=(dc == 0), stop=False)
                    ins = te.matmul(ps_o[g % 2][:], c.ones[0:1, :],
                                    bo_row[:, dd * 512:(dd + 1) * 512], start=False, stop=True)
                    assert c.sMM.inc(ins) == omm(j, g)

        @b.scalar
        def _(se):
            for j in range(4):
                for ec in range(DCH):
                    se.wait_ge(c.sV.h, qev(j, ec))
                    if ec >= 2:
                        se.wait_ge(c.sMM.h, nmm(j, ec - 2))
                    elif j > 0:
                        se.wait_ge(c.sMM.h, nmm(j - 1, ec + 6))
                    else:
                        se.wait_ge(c.sMM.h, nkmm(DCH - 1, 1))
                    ins = se.activation(sq1[:, (ec % 2) * 512:(ec % 2) * 512 + 512],
                                        Q_ch[:, ec, :], AF.Square)
                    assert c.sS.inc(ins) == sqq(j, ec)
                se.wait_ge(c.sV.h, nqv(j))
                se.drain()
                ins = se.activation(cq[:], cq2[:], AF.Sqrt)
                assert c.sS.inc(ins) == cqs(j)
                for h in range(NH):
                    for kt in range(TCH):
                        se.wait_ge(c.sMM.h, smm(j, h, kt))
                        if h == 0 and kt < 2:
                            se.wait_ge(c.sV.h, cnv(j))
                        if kt >= 3:
                            se.wait_ge(c.sMM.h, amk(j, h, kt - 3))
                        elif h > 0:
                            se.wait_ge(c.sMM.h, amk(j, h - 1, 13 + kt))
                        elif j > 0:
                            se.wait_ge(c.sMM.h, amk(j - 1, 15, 13 + kt))
                        ins = se.activation(expT[kt % 3][:], ps_s[kt % 2][:], AF.Exp,
                                            bias=cneg[:, h:h + 1])
                        assert c.sS.inc(ins) == eev(j, h, kt)

        @b.vector
        def _(ve):
            for j in range(4):
                for ec in range(DCH):
                    ve.wait_ge(c.sMM.h, qmm(j, ec))
                    if j > 0:
                        ve.wait_ge(c.sMM.h, smm(j - 1, 2 * ec + 1, 15))
                    ins = ve.tensor_scalar(Q_ch[:, ec, :], ps_s[ec % 2][:],
                                           ascale[:], bqs_cols[:, ec:ec + 1],
                                           ALU.mult, ALU.add)
                    assert c.sV.inc(ins) == qev(j, ec)
                ve.wait_ge(c.sMM.h, nmm(j, 7))
                ins = ve.reduce_max(cq2[:], ps_n[:], AX.X)
                assert c.sV.inc(ins) == nqv(j)
                ve.drain()
                ve.wait_ge(c.sS.h, cqs(j))
                if j > 0:
                    ve.wait_ge(c.sX.h, xrow(j - 1, 1))
                ins = ve.tensor_mul(ch_bf[:], cq[:], sk[:])
                assert c.sV.inc(ins) == chv(j)
                ve.wait_ge(c.sMM.h, cbm(j))
                if j > 0:
                    ve.wait_ge(c.sS.h, eev(j - 1, 15, 15))
                ins = ve.tensor_scalar_mul(cneg[:], ps_cn[:], -1.0)
                assert c.sV.inc(ins) == cnv(j)
                for h in range(NH):
                    for oi in range(4):
                        kt = 4 * j + oi
                        ve.wait_ge(c.sS.h, eev(j, h, kt))
                        ins = ve.tensor_mul(expT[kt % 3][:], expT[kt % 3][:],
                                            mask_sb[:, oi, :])
                        assert c.sV.inc(ins) == mskv(j, h, oi)
                    ve.wait_ge(c.sMM.h, amk(j, h, 15))
                    if h > 0:
                        ve.wait_ge(c.sMM.h, bmm(j, h - 1))
                    elif j > 0:
                        ve.wait_ge(c.sMM.h, bmm(j - 1, 15))
                    with nc.allow_low_precision("denom recip feeds a bf16 matmul operand"):
                        ins = ve.reciprocal(rd_row[:], ps_a[h % 2][64:65, :])
                    assert c.sV.inc(ins) == rcp(j, h)
                    ve.drain()
                    ve.wait_ge(c.sMM.h, bmm(j, h))
                    ins = ve.tensor_copy(out=rb[:], in_=ps_o[0][0:64, :])
                    assert c.sV.inc(ins) == rbc(j, h)
                    if h == 0 and j > 0:
                        ve.wait_ge(c.sMM.h, omm(j - 1, 7))
                    ins = ve.tensor_tensor(attn[(h % 2) * 64:(h % 2) * 64 + 64, h // 2, :],
                                           ps_a[h % 2][0:64, :], rb[:], ALU.mult)
                    assert c.sV.inc(ins) == atv(j, h)
                for g in range(8):
                    dd, tcol = g // 4, g % 4
                    ve.wait_ge(c.sMM.h, omm(j, g))
                    ins = ve.tensor_add(c.xres[:, j * 4 + tcol, dd * 512:(dd + 1) * 512],
                                        c.xres[:, j * 4 + tcol, dd * 512:(dd + 1) * 512],
                                        ps_o[g % 2][:])
                    assert c.sV.inc(ins) == oev(j, g)

    exp2.close()
    ex3.close()
    c.stream_deps = [(c.sV, c.sV.v)]
    c.fence_deps = c.barrier_tokens()
    ex.close()


SUB_WEIGHTS = {
    0: ["l0a_wq", "l0a_wk", "l0a_wv"],
    1: ["l0f_gwt", "l0f_eb", "l0f_w1t", "l0f_w2t"],
    3: ["l0f_gwt", "l0f_eb", "l0f_w1t", "l0f_w2t"],
    2: ["l0b_wqkv", "l0b_bqkv", "l0b_wout", "l0b_bout", "l0b_bq_cols"],
    4: ["l1a_wqkvt", "l1a_bqkv", "l1a_wprojt", "l1a_bproj", "l1a_scale_col"],
    5: ["l1f_wrt", "l1f_br", "l1f_b1_cols", "l1f_b2", "l1f_w1", "l1f_w2"],
    7: ["l1f_wrt", "l1f_br", "l1f_b1_cols", "l1f_b2", "l1f_w1", "l1f_w2"],
    6: ["l1b_wqt", "l1b_wkt", "l1b_wvt", "l1b_wot", "l1b_bq_cols", "l1b_bk_cols",
        "l1b_bv", "l1b_bo", "l1b_ascale_col", "l1b_masks", "l1b_hind"],
}
SUB_CONSTS = {
    0: ["a_cat"],
}
EMITTERS = {}


def _emit_sub0(c, w, dram):
    emit_l0a(c, w, dram)


EMITTERS[0] = _emit_sub0
EMITTERS[1] = lambda c, w, dram: emit_l0ffn(c, w, dram, "s1")
EMITTERS[3] = lambda c, w, dram: emit_l0ffn(c, w, dram, "s3")
EMITTERS[2] = lambda c, w, dram: emit_l0b(c, w, dram, "s2")
EMITTERS[4] = lambda c, w, dram: emit_l1a(c, w, dram, "s4")
EMITTERS[5] = lambda c, w, dram: emit_l1ffn(c, w, dram, "s5")
EMITTERS[7] = lambda c, w, dram: emit_l1ffn(c, w, dram, "s7")
EMITTERS[6] = lambda c, w, dram: emit_l1b(c, w, dram, "s6")

_WSHAPES = {
    "l0a_wq": ([D, D], F32), "l0a_wk": ([D, D], F32), "l0a_wv": ([D, D], F32),
    "l0f_gwt": ([128, DCH, 12], F32), "l0f_eb": ([1, 12], F32),
    "l0f_w1t": ([8, D, 2048], F32), "l0f_w2t": ([8, 2048, D], F32),
    "l0b_wqkv": ([D, 3 * D], F32), "l0b_bqkv": ([1, 3 * D], F32),
    "l0b_wout": ([D, D], F32), "l0b_bout": ([1, D], F32),
    "l0b_bq_cols": ([128, DCH], F32),
    "l1a_wqkvt": ([D, 3 * D], F32), "l1a_bqkv": ([1, 3 * D], F32),
    "l1a_wprojt": ([D, D], F32), "l1a_bproj": ([1, D], F32),
    "l1a_scale_col": ([128, 1], F32),
    "l1f_wrt": ([128, DCH, 8], F32), "l1f_br": ([1, 8], F32),
    "l1f_b1_cols": ([128, 8 * 32], F32), "l1f_b2": ([8, D], F32),
    "l1f_w1": ([8, D, 4096], F32), "l1f_w2": ([8, 4096, D], F32),
    "l1b_wqt": ([D, D], F32), "l1b_wkt": ([D, D], F32),
    "l1b_wvt": ([D, D], F32), "l1b_wot": ([D, D], F32),
    "l1b_bq_cols": ([128, DCH], F32), "l1b_bk_cols": ([128, DCH], F32),
    "l1b_bv": ([1, D], F32), "l1b_bo": ([1, D], F32),
    "l1b_ascale_col": ([128, 1], F32),
    "l1b_masks": ([128, 4, 512], BF16), "l1b_hind": ([128, DCH, 16], BF16),
}
_CSHAPES = {
    "ident": ([128, 128], BF16),
    "a_cat": ([T, T], BF16),
    "triu": ([128, 128], BF16),
}


def _build_launch(subs):
    nc = bass.Bass("TRN2", target_bir_lowering=False)
    x_in = nc.dram_tensor("x_in", [T, D], F32, kind="ExternalInput").ap()
    out = nc.dram_tensor("out", [T, D], F32, kind="ExternalOutput").ap()

    w = {}
    for s in subs:
        for name in SUB_WEIGHTS[s]:
            if name not in w:
                shape, dt = _WSHAPES[name]
                w[name] = nc.dram_tensor("d_" + name, shape, dt, kind="ExternalInput").ap()
    need_consts = {"ident", "triu"}
    for s in subs:
        need_consts.update(SUB_CONSTS.get(s, []))
    for name in sorted(need_consts):
        shape, dt = _CSHAPES[name]
        w[name] = nc.dram_tensor("c_" + name, shape, dt, kind="ExternalInput").ap()

    dram = {"a_cat": w.get("a_cat")}
    if 6 in subs:
        dram["l1b_scr"] = nc.dram_tensor("l1b_scr", [16, 1], BF16).ap()
    if 0 in subs:
        dram["cc_in"] = nc.dram_tensor("cc_in", [T, D], F32).ap()
        dram["cc_out"] = nc.dram_tensor("cc_out", [T, D], F32, addr_space="Shared").ap()

    with ExitStack() as ex:
        c = Ctx(nc, ex)
        with nc.Block("init", no_gpsimd_drain=True) as b:
            @b.sync
            def _(sy):
                src = x_in.rearrange("(i p) d -> p i d", p=128)
                for i0 in range(0, TCH, 4):
                    ins = sy.dma_start(out=c.xres[:, i0:i0 + 4, :], in_=src[:, i0:i0 + 4, :])
                    c.sX.inc(ins, 16)
                c.sX.inc(sy.dma_start(out=c.ident[:], in_=w["ident"]), 16)
                c.sX.inc(sy.dma_start(out=c.triu[:], in_=w["triu"]), 16)

            @b.gpsimd
            def _(ge):
                ins = ge.memset(c.ones[:], 1.0)
                c.sG.inc(ins)
        tok_init = c.sX.v
        c.stream_deps = [(c.sX, tok_init)]

        for s in subs:
            EMITTERS[s](c, w, dram)

        tok_store = c.sX.v + 16 * 4
        with nc.Block("fin") as b:
            @b.sync
            def _(sy):
                _wait(sy, c.stream_deps)
                dst = out.rearrange("(i p) d -> p i d", p=128)
                for i0 in range(0, TCH, 4):
                    ins = sy.dma_start(out=dst[:, i0:i0 + 4, :], in_=c.xres[:, i0:i0 + 4, :])
                    c.sX.inc(ins, 16)
                assert c.sX.v == tok_store

            @b.gpsimd
            def _(ge):
                ge.wait_ge(c.sX.h, tok_store)
    return nc


_NC_CACHE = {}


def _get_launch(subs):
    key = tuple(subs)
    if key not in _NC_CACHE:
        _NC_CACHE[key] = _build_launch(list(subs))
    return _NC_CACHE[key]


def kernel(**inputs):
    consts = _host_consts()
    prep = _prep_weights(inputs)
    x = np.ascontiguousarray(np.asarray(inputs["L0_attn_a_x"], np.float32))
    stream = [np.ascontiguousarray(x[b]) for b in range(NCORE)]

    for subs in GROUPS:
        nc = _get_launch(subs)
        names = set()
        for s in subs:
            names.update(SUB_WEIGHTS[s])
        cnames = {"ident", "triu"}
        for s in subs:
            cnames.update(SUB_CONSTS.get(s, []))
        in_maps = []
        for b in range(NCORE):
            m = {"x_in": stream[b]}
            for n in names:
                m["d_" + n] = prep[n]
            for n in cnames:
                m["c_" + n] = consts[n]
            in_maps.append(m)
        res = run_bass_kernel_spmd(nc, in_maps, core_ids=list(range(NCORE)))
        stream = [np.asarray(res.results[b]["out"]) for b in range(NCORE)]

    return np.stack(stream).astype(np.float32)


# version 06 (final)
# speedup vs baseline: 1.0406x; 1.0101x over previous
import numpy as np
import ml_dtypes
from contextlib import ExitStack

import concourse.bass as bass
import concourse.mybir as mybir
from concourse.bass_utils import run_bass_kernel_spmd

F32 = mybir.dt.float32
BF16 = mybir.dt.bfloat16
AF = mybir.ActivationFunctionType
ALU = mybir.AluOpType
AX = mybir.AxisListType
BF = ml_dtypes.bfloat16

T, D = 2048, 1024
TCH = T // 128
DCH = D // 128
NCORE = 8
GAMMA = 0.96875

GROUPS = [[0, 1, 2, 3, 4, 5, 6, 7]]


def _host_consts():
    out = {}
    out["ident"] = np.eye(128, dtype=BF)
    A = np.zeros((T, T), np.float64)
    A[0, 1] = 1.0
    for t in range(1, T):
        j = np.arange(1, t + 1)
        A[t, j] = GAMMA ** (t - j)
    A_cat = np.concatenate([A[0::2].T, A[1::2].T], axis=1)
    out["a_cat"] = np.ascontiguousarray(A_cat.astype(BF))
    out["triu"] = np.ascontiguousarray(np.triu(np.ones((128, 128))).astype(BF))
    return out


def _prep_weights(inp):
    f = lambda a: np.ascontiguousarray(np.asarray(a, np.float32))
    tr = lambda a: np.ascontiguousarray(np.asarray(a, np.float32).T)
    p = {}
    p["l0a_wq"] = f(inp["L0_attn_a_Wq"])
    p["l0a_wk"] = f(inp["L0_attn_a_Wk"])
    p["l0a_wv"] = f(inp["L0_attn_a_Wv"])
    gw = np.asarray(inp["L0_ffn_gate_w"], np.float32)
    p["l0f_gwt"] = np.ascontiguousarray(gw.T.reshape(DCH, 128, 12).transpose(1, 0, 2))
    p["l0f_eb"] = np.ascontiguousarray(np.asarray(inp["L0_ffn_expert_bias"], np.float32)[None, :])
    p["l0f_w1t"] = np.ascontiguousarray(np.asarray(inp["L0_ffn_w1"], np.float32).transpose(0, 2, 1))
    p["l0f_w2t"] = np.ascontiguousarray(np.asarray(inp["L0_ffn_w2"], np.float32).transpose(0, 2, 1))
    p["l0b_wqkv"] = f(inp["L0_attn_b_Wqkv"])
    bqkv = np.asarray(inp["L0_attn_b_bqkv"], np.float32)
    p["l0b_bqkv"] = np.ascontiguousarray(bqkv[None, :])
    p["l0b_wout"] = f(inp["L0_attn_b_Wout"])
    p["l0b_bout"] = np.ascontiguousarray(np.asarray(inp["L0_attn_b_bout"], np.float32)[None, :])
    p["l0b_bq_cols"] = np.ascontiguousarray(bqkv[:D].reshape(DCH, 128).T)
    p["l1a_wqkvt"] = tr(inp["L1_attn_a_Wqkv"])
    p["l1a_bqkv"] = np.ascontiguousarray(np.asarray(inp["L1_attn_a_bqkv"], np.float32)[None, :])
    p["l1a_wprojt"] = tr(inp["L1_attn_a_Wproj"])
    p["l1a_bproj"] = np.ascontiguousarray(np.asarray(inp["L1_attn_a_bproj"], np.float32)[None, :])
    p["l1a_scale_col"] = np.ascontiguousarray(
        np.broadcast_to(np.asarray(inp["L1_attn_a_scale"], np.float32).reshape(1, 1), (128, 1)))
    wr = np.asarray(inp["L1_ffn_Wr"], np.float32)
    p["l1f_wrt"] = np.ascontiguousarray(wr.reshape(DCH, 128, 8).transpose(1, 0, 2))
    p["l1f_br"] = np.ascontiguousarray(np.asarray(inp["L1_ffn_br"], np.float32)[None, :])
    b1 = np.asarray(inp["L1_ffn_b1"], np.float32)
    p["l1f_b1_cols"] = np.ascontiguousarray(
        b1.reshape(8, 32, 128).transpose(2, 0, 1).reshape(128, 8 * 32))
    p["l1f_b2"] = np.ascontiguousarray(np.asarray(inp["L1_ffn_b2"], np.float32))
    p["l1f_w1"] = np.ascontiguousarray(np.asarray(inp["L1_ffn_W1"], np.float32))
    p["l1f_w2"] = np.ascontiguousarray(np.asarray(inp["L1_ffn_W2"], np.float32))
    p["l1b_wqt"] = tr(inp["L1_attn_b_Wq"])
    p["l1b_wkt"] = tr(inp["L1_attn_b_Wk"])
    p["l1b_wvt"] = tr(inp["L1_attn_b_Wv"])
    p["l1b_wot"] = tr(inp["L1_attn_b_Wo"])
    p["l1b_bq_cols"] = np.ascontiguousarray(
        np.asarray(inp["L1_attn_b_bq"], np.float32).reshape(DCH, 128).T)
    p["l1b_bk_cols"] = np.ascontiguousarray(
        np.asarray(inp["L1_attn_b_bk"], np.float32).reshape(DCH, 128).T)
    p["l1b_bv"] = np.ascontiguousarray(np.asarray(inp["L1_attn_b_bv"], np.float32)[None, :])
    p["l1b_bo"] = np.ascontiguousarray(np.asarray(inp["L1_attn_b_bo"], np.float32)[None, :])
    p["l1b_ascale_col"] = np.ascontiguousarray(
        np.broadcast_to(np.asarray(inp["L1_attn_b_attn_scale"], np.float32).reshape(1, 1), (128, 1)))
    pgrid = np.arange(128)[:, None]
    cgrid = np.arange(512)[None, :]
    m = np.zeros((128, 4, 512), np.float32)
    for o in range(4):
        ktv = 128 * o + pgrid
        keep = ~((ktv > cgrid) & ((ktv // 64) == (cgrid // 64)))
        m[:, o, :] = keep.astype(np.float32)
    p["l1b_masks"] = m.astype(BF)
    hi = np.zeros((128, DCH, 16), np.float32)
    for dc in range(DCH):
        for pp in range(128):
            hi[pp, dc, dc * 2 + pp // 64] = 1.0
    p["l1b_hind"] = hi.astype(BF)
    return p


class Sem:
    def __init__(self, h):
        self.h = h
        self.v = 0

    def inc(self, inst, n=1):
        inst.then_inc(self.h, n)
        self.v += n
        return self.v


class Ctx:

    def __init__(self, nc, ex):
        self.nc = nc
        self.ex = ex
        self.sb = lambda name, shape, dt: ex.enter_context(nc.sbuf_tensor(name, shape, dt))
        self.psum = lambda ex_, name, shape, dt=F32: ex_.enter_context(nc.psum_tensor(name, shape, dt))
        mksem = lambda name: Sem(ex.enter_context(nc.semaphore(name)))
        self.sW = mksem("sW")
        self.sX = mksem("sX")
        self.sMM = mksem("sMM")
        self.sV = mksem("sV")
        self.sS = mksem("sS")
        self.sG = mksem("sG")
        self.sCC = mksem("sCC")

        self.xres = self.sb("xres", [128, TCH, D], F32)
        self.xlnT = self.sb("xlnT", [128, DCH, T], BF16)
        self.ones = self.sb("ones", [128, 128], BF16)
        self.ident = self.sb("ident", [128, 128], BF16)
        self.triu = self.sb("triu", [128, 128], BF16)
        self.mu = self.sb("mu", [128, TCH], F32)
        self.rstd = self.sb("rstd", [128, TCH], F32)
        self.acc = self.sb("acc", [128, TCH], F32)
        self.s1 = self.sb("s1", [128, TCH], F32)
        self.var = self.sb("var", [128, TCH], F32)
        self.sq_scr = self.sb("sq_scr", [128, D], BF16)
        self.xln_rot = [self.sb(f"xln_rot{i}", [128, D], BF16) for i in range(2)]

        self.stream_deps = []
        self.fence_deps = []
        self.ln_evac = []

    def barrier_tokens(self):
        return [(s, s.v) for s in (self.sMM, self.sV, self.sS, self.sX, self.sW, self.sG)]


def _wait(eng, deps):
    for (sem, val) in deps:
        if val > 0:
            eng.wait_ge(sem.h, val)


def emit_ln(c, tag):
    nc = c.nc
    deps = list(c.stream_deps)
    tok_acc = c.sS.v + 1

    chunk_deps = getattr(c, "stream_chunk_deps", None)
    c.stream_chunk_deps = None
    with nc.Block(f"ln{tag}_stats", no_gpsimd_drain=True) as b:
        @b.scalar
        def _(se):
            ins = None
            if chunk_deps is None:
                _wait(se, deps)
            for i in range(TCH):
                if chunk_deps is not None:
                    _wait(se, [chunk_deps[i]])
                ins = se.activation(c.sq_scr[:], c.xres[:, i, :], AF.Square,
                                    accum_out=c.acc[:, i:i + 1])
            assert c.sS.inc(ins) == tok_acc

        @b.vector
        def _(ve):
            if chunk_deps is None:
                _wait(ve, deps)
            for i in range(TCH):
                if chunk_deps is not None:
                    _wait(ve, [chunk_deps[i]])
                ve.reduce_sum(c.s1[:, i:i + 1], c.xres[:, i, :], AX.X)
            ve.drain()
            ve.wait_ge(c.sS.h, tok_acc)
            ve.tensor_scalar_mul(c.mu[:], c.s1[:], 1.0 / D)
            ve.drain()
            ve.scalar_tensor_tensor(c.var[:], c.mu[:], -1.0, c.mu[:], ALU.mult, ALU.mult)
            ve.drain()
            ve.scalar_tensor_tensor(c.var[:], c.acc[:], 1.0 / D, c.var[:], ALU.mult, ALU.add)
            ve.drain()
            ve.tensor_scalar_add(c.var[:], c.var[:], 1e-5)
            ve.drain()
            ins = ve.reciprocal(c.var[:], c.var[:])
            c.sV.inc(ins)

    tok_var = c.sV.v
    with nc.Block(f"ln{tag}_rstd", no_gpsimd_drain=True) as b:
        @b.scalar
        def _(se):
            se.wait_ge(c.sV.h, tok_var)
            se.drain()
            ins = se.activation(c.rstd[:], c.var[:], AF.Sqrt)
            c.sS.inc(ins)

    tok_rstd = c.sS.v
    c.tok_stats = (c.sS, tok_rstd)

    ex = ExitStack()
    ps_tr = [c.psum(ex, f"ps_tr_{tag}_{i}", [128, D], BF16) for i in range(2)]
    sV0, sMM0, sS0 = c.sV.v, c.sMM.v, c.sS.v
    xln_tok = [sV0 + i + 1 for i in range(TCH)]
    tr_tok = [sMM0 + i + 1 for i in range(TCH)]
    evac_tok = [sS0 + i + 1 for i in range(TCH)]

    with nc.Block(f"ln{tag}_tr", no_gpsimd_drain=True) as b:
        @b.vector
        def _(ve):
            ve.wait_ge(c.sS.h, tok_rstd)
            for i in range(TCH):
                if i >= 2:
                    ve.wait_ge(c.sMM.h, tr_tok[i - 2])
                ins = ve.tensor_scalar(c.xln_rot[i % 2][:], c.xres[:, i, :],
                                       c.mu[:, i:i + 1], c.rstd[:, i:i + 1],
                                       ALU.subtract, ALU.mult)
                assert c.sV.inc(ins) == xln_tok[i]

        @b.tensor
        def _(te):
            for i in range(TCH):
                te.wait_ge(c.sV.h, xln_tok[i])
                if i >= 2:
                    te.wait_ge(c.sS.h, evac_tok[i - 2])
                ins = None
                for cc_ in range(DCH):
                    ins = te.transpose(ps_tr[i % 2][:, cc_ * 128:(cc_ + 1) * 128],
                                       c.xln_rot[i % 2][:, cc_ * 128:(cc_ + 1) * 128],
                                       c.ident[:])
                assert c.sMM.inc(ins) == tr_tok[i]

        @b.scalar
        def _(se):
            for i in range(TCH):
                se.wait_ge(c.sMM.h, tr_tok[i])
                src = ps_tr[i % 2][:].rearrange("p (c q) -> p c q", c=DCH)
                ins = se.copy(out=c.xlnT[:, :, i * 128:(i + 1) * 128], in_=src)
                assert c.sS.inc(ins) == evac_tok[i]

    ex.close()
    c.ln_evac = evac_tok
    c.tok_xlnT = (c.sS, c.sS.v)


L0A_PHASES = 4

def emit_l0a(c, w, dram):
    nc = c.nc
    emit_ln(c, "s0")
    if L0A_PHASES < 1:
        return
    fence = list(c.fence_deps)
    ex = ExitStack()
    sb = lambda name, shape, dt: ex.enter_context(nc.sbuf_tensor(name, shape, dt))
    q_tm = sb("l0a_q", [128, TCH, D], BF16)
    exw = ExitStack()
    sbw = lambda name, shape, dt: exw.enter_context(nc.sbuf_tensor(name, shape, dt))
    wq_sb = sbw("l0a_wq", [128, DCH, D], BF16)
    wk_sb = sbw("l0a_wk", [128, DCH, D], BF16)
    wv_sb = sbw("l0a_wv", [128, DCH, D], BF16)
    k_rot = [sbw(f"l0a_k{i}", [128, D], BF16) for i in range(2)]
    v_rot = [sbw(f"l0a_v{i}", [128, D], BF16) for i in range(2)]
    m_rot = [sbw(f"l0a_m{i}", [128, D], F32) for i in range(2)]
    vs = sbw("l0a_vs", [128, TCH], F32)

    wtok = {}
    with nc.Block("l0a_wload", no_gpsimd_drain=True) as b:
        @b.gpsimd
        def _(ge):
            _wait(ge, fence)
            for name, dst in (("l0a_wk", wk_sb), ("l0a_wv", wv_sb), ("l0a_wq", wq_sb)):
                src = w[name].rearrange("(kc p) e -> p kc e", p=128)
                for kc in range(DCH):
                    ins = ge.dma_start(out=dst[:, kc, :], in_=src[:, kc, :])
                    c.sW.inc(ins, 16)
                wtok[name] = c.sW.v

    if L0A_PHASES < 2:
        exw.close(); ex.close()
        return
    exq = ExitStack()
    ps = [c.psum(exq, f"l0a_ps{g}", [128, 512]) for g in range(6)]
    sMM0, sS0, sV0, sX0 = c.sMM.v, c.sS.v, c.sV.v, c.sX.v
    mm_tok = lambda i, g: sMM0 + i * 6 + g + 1
    ev_tok = lambda i, g: sS0 + i * 6 + g + 1
    v_tok = lambda i: sV0 + 2 * i + 2
    x_tok = lambda i: sX0 + (i + 1) * 16

    tok_xlnT_all = c.tok_xlnT[1]
    with nc.Block("l0a_proj", no_gpsimd_drain=True) as b:
        @b.tensor
        def _(te):
            te.wait_ge(c.sW.h, wtok["l0a_wq"])
            te.wait_ge(c.sS.h, tok_xlnT_all)
            for i in range(TCH):
                te.wait_ge(c.sS.h, c.ln_evac[i])
                for g, (wsb, e0) in enumerate(((wk_sb, 0), (wk_sb, 512), (wv_sb, 0),
                                               (wv_sb, 512), (wq_sb, 0), (wq_sb, 512))):
                    if i >= 1:
                        te.wait_ge(c.sS.h, ev_tok(i - 1, g))
                    ins = None
                    for kc in range(DCH):
                        ins = te.matmul(ps[g][:], c.xlnT[:, kc, i * 128:(i + 1) * 128],
                                        wsb[:, kc, e0:e0 + 512],
                                        start=(kc == 0), stop=(kc == DCH - 1))
                    assert c.sMM.inc(ins) == mm_tok(i, g)

        @b.scalar
        def _(se):
            for i in range(TCH):
                for g in range(6):
                    se.wait_ge(c.sMM.h, mm_tok(i, g))
                    if g < 2:
                        if i >= 2:
                            se.wait_ge(c.sV.h, v_tok(i - 2))
                        dst = k_rot[i % 2][:, g * 512:(g + 1) * 512]
                    elif g < 4:
                        if i >= 2:
                            se.wait_ge(c.sV.h, v_tok(i - 2))
                        dst = v_rot[i % 2][:, (g - 2) * 512:(g - 1) * 512]
                    else:
                        dst = q_tm[:, i, (g - 4) * 512:(g - 3) * 512]
                    ins = se.copy(out=dst, in_=ps[g][:])
                    assert c.sS.inc(ins) == ev_tok(i, g)

        @b.vector
        def _(ve):
            for i in range(TCH):
                ve.wait_ge(c.sS.h, ev_tok(i, 3))
                ins = ve.reduce_sum(vs[:, i:i + 1], v_rot[i % 2][:], AX.X)
                ve.drain()
                assert c.sV.inc(ins) == v_tok(i) - 1
                ve.wait_ge(c.sS.h, ev_tok(i, 1))
                if i >= 2:
                    ve.wait_ge(c.sX.h, x_tok(i - 2))
                ins = ve.tensor_scalar_mul(m_rot[i % 2][:], k_rot[i % 2][:], vs[:, i:i + 1])
                assert c.sV.inc(ins) == v_tok(i)

        @b.sync
        def _(sy):
            for i in range(TCH):
                sy.wait_ge(c.sV.h, v_tok(i))
                ins = sy.dma_start(out=dram["cc_in"][i * 128:(i + 1) * 128, :],
                                   in_=m_rot[i % 2][:])
                assert c.sX.inc(ins, 16) == x_tok(i)

    exq.close()
    tok_proj_mm = c.sMM.v
    exw.close()

    if L0A_PHASES < 3:
        c.fence_deps = c.barrier_tokens()
        ex.close()
        return
    m_bf = sb("l0a_mbf", [128, TCH, D], BF16)
    tok_mloc = c.sX.v
    with nc.Block("l0a_cc") as b:
        @b.gpsimd
        def _(ge):
            ge.wait_ge(c.sX.h, tok_mloc)
            ge.wait_ge(c.sMM.h, tok_proj_mm)
            ge.collective_compute(
                "AllReduce", ALU.add, replica_groups=[list(range(NCORE))],
                ins=[dram["cc_in"]], outs=[dram["cc_out"]],
            ).then_inc(c.sCC.h)
            ge.wait_ge(c.sCC.h, 1)
            src = dram["cc_out"].rearrange("(i p) r -> p i r", p=128)
            ins = None
            for i0 in range(0, TCH, 4):
                ins = ge.dma_start(out=m_bf[:, i0:i0 + 4, :], in_=src[:, i0:i0 + 4, :])
                c.sW.inc(ins, 16)
    tok_mbf = c.sW.v

    if L0A_PHASES < 4:
        c.fence_deps = c.barrier_tokens()
        ex.close()
        return
    a_rot = [sb(f"l0a_a{i}", [128, TCH, 512], BF16) for i in range(2)]
    exd = ExitStack()
    psd = [c.psum(exd, f"l0a_psd{i}", [128, 512]) for i in range(2)]
    t1 = [sb(f"l0a_t1_{i}", [128, 512], F32) for i in range(2)]

    sX0, sMM0, sV0 = c.sX.v, c.sMM.v, c.sV.v
    tok_proj_evac = c.sS.v
    a_tok = lambda blk: sX0 + (blk + 1) * 16 * TCH
    d_mm_tok = lambda blk, rs: sMM0 + blk * 8 + rs + 1
    d_v_tok = lambda blk, rs: sV0 + (blk * 8 + rs + 1) * 2

    with nc.Block("l0a_sprime", no_gpsimd_drain=True) as b:
        @b.sync
        def _(sy):
            for blk in range(4):
                if blk >= 2:
                    sy.wait_ge(c.sMM.h, d_mm_tok(blk - 2, 7))
                col0 = blk * 512
                src = dram["a_cat"]
                ins = None
                for jc in range(TCH):
                    ins = sy.dma_start(out=a_rot[blk % 2][:, jc, :],
                                       in_=src[jc * 128:(jc + 1) * 128, col0:col0 + 512])
                    c.sX.inc(ins, 16)
                assert c.sX.v == a_tok(blk)

        @b.tensor
        def _(te):
            te.wait_ge(c.sW.h, tok_mbf)
            te.wait_ge(c.sS.h, tok_proj_evac)
            for blk in range(4):
                te.wait_ge(c.sX.h, a_tok(blk))
                for rs in range(8):
                    if blk * 8 + rs >= 2:
                        prev = blk * 8 + rs - 2
                        te.wait_ge(c.sV.h, d_v_tok(prev // 8, prev % 8))
                    ins = None
                    for jc in range(TCH):
                        ins = te.matmul(psd[rs % 2][:], m_bf[:, jc, rs * 128:(rs + 1) * 128],
                                        a_rot[blk % 2][:, jc, :],
                                        start=(jc == 0), stop=(jc == TCH - 1))
                    assert c.sMM.inc(ins) == d_mm_tok(blk, rs)

        @b.vector
        def _(ve):
            ve.wait_ge(c.sS.h, tok_proj_evac)
            for blk in range(4):
                sel, cs = blk // 2, blk % 2
                for rs in range(8):
                    ct = sel * 8 + rs
                    ve.wait_ge(c.sMM.h, d_mm_tok(blk, rs))
                    ve.tensor_tensor(t1[rs % 2][:], psd[rs % 2][:],
                                     q_tm[:, ct, cs * 512:(cs + 1) * 512], ALU.mult)
                    ins = ve.tensor_add(c.xres[:, ct, cs * 512:(cs + 1) * 512],
                                        c.xres[:, ct, cs * 512:(cs + 1) * 512], t1[rs % 2][:])
                    c.sV.inc(ins, 2)
                    assert c.sV.v == d_v_tok(blk, rs)

    exd.close()
    c.stream_deps = [(c.sV, c.sV.v)]
    c.fence_deps = c.barrier_tokens()
    ex.close()


def emit_l0ffn(c, w, dram, tag):
    nc = c.nc
    emit_ln(c, tag)
    fence = list(c.fence_deps)
    ex = ExitStack()
    sb = lambda name, shape, dt: ex.enter_context(nc.sbuf_tensor(f"{tag}_{name}", shape, dt))
    NE, NL = 8, 12
    FF = 2048
    FC = FF // 128

    gwt_sb = sb("gwt", [128, DCH, NL], BF16)
    eb_sb = sb("eb", [1, NL], BF16)
    lg = sb("lg", [128, TCH, NL], F32)
    eq1 = sb("eq1", [128, TCH, NL], F32)
    eq2 = sb("eq2", [128, TCH, NL], F32)
    m1 = sb("m1", [128, TCH], F32)
    m2 = sb("m2", [128, TCH], F32)
    p1 = sb("p1", [128, TCH], F32)
    p2 = sb("p2", [128, TCH], F32)
    wz = sb("wz", [128, TCH], F32)
    ab = sb("ab", [128, 2, TCH], F32)
    h_sb = sb("h", [128, FC, T], BF16)
    w1rot = [sb(f"w1r{i}", [128, DCH, 128], BF16) for i in range(2)]
    w2_sb = sb("w2", [128, FC, D], BF16)

    exp = ExitStack()
    ps_lg = [c.psum(exp, f"{tag}_pslg{i}", [128, NL]) for i in range(2)]
    ps_h = [c.psum(exp, f"{tag}_psh{i}", [128, 512]) for i in range(2)]
    ps_y = [c.psum(exp, f"{tag}_psy{i}", [128, 512]) for i in range(4)]

    with nc.Block(f"{tag}_smload", no_gpsimd_drain=True) as b:
        @b.gpsimd
        def _(ge):
            _wait(ge, fence)
            c.sW.inc(ge.dma_start(out=gwt_sb[:], in_=w["l0f_gwt"]), 16)
            c.sW.inc(ge.dma_start(out=eb_sb[:], in_=w["l0f_eb"]), 16)
    tok_sm = c.sW.v

    sMM0, sV0 = c.sMM.v, c.sV.v
    lgmm_tok = [sMM0 + i + 1 for i in range(TCH)]
    lgev_tok = [sV0 + i + 1 for i in range(TCH)]
    with nc.Block(f"{tag}_gates", no_gpsimd_drain=True) as b:
        @b.tensor
        def _(te):
            _wait(te, fence)
            te.wait_ge(c.sW.h, tok_sm)
            for i in range(TCH):
                te.wait_ge(c.sS.h, c.ln_evac[i])
                if i >= 2:
                    te.wait_ge(c.sV.h, lgev_tok[i - 2])
                for kc in range(DCH):
                    te.matmul(ps_lg[i % 2][:], c.xlnT[:, kc, i * 128:(i + 1) * 128],
                              gwt_sb[:, kc, :], start=(kc == 0), stop=False)
                ins = te.matmul(ps_lg[i % 2][:], c.ones[0:1, :], eb_sb[:],
                                start=False, stop=True)
                assert c.sMM.inc(ins) == lgmm_tok[i]

        @b.vector
        def _(ve):
            for i in range(TCH):
                ve.wait_ge(c.sMM.h, lgmm_tok[i])
                ins = ve.tensor_copy(out=lg[:, i, :], in_=ps_lg[i % 2][:])
                assert c.sV.inc(ins) == lgev_tok[i]
            ve.drain()
            ve.reduce_max(m1[:, :, None], lg[:], AX.X)
            ve.drain()
            ve.tensor_tensor(eq1[:], lg[:], m1[:, :, None].broadcast_to([128, TCH, NL]), ALU.is_equal)
            ve.scalar_tensor_tensor(eq2[:], eq1[:], -1e30, lg[:], ALU.mult, ALU.add)
            ve.drain()
            ve.reduce_max(m2[:, :, None], eq2[:], AX.X)
            ve.drain()
            ve.tensor_tensor(eq2[:], lg[:], m2[:, :, None].broadcast_to([128, TCH, NL]), ALU.is_equal)
            ins = ve.tensor_sub(p2[:], m2[:], m1[:])
            c.sV.inc(ins)

    tok_diff = c.sV.v
    with nc.Block(f"{tag}_sig", no_gpsimd_drain=True) as b:
        @b.scalar
        def _(se):
            se.wait_ge(c.sV.h, tok_diff)
            ins = se.activation(p1[:], p2[:], AF.Sigmoid, scale=-1.0)
            c.sS.inc(ins)
    tok_p1 = c.sS.v

    with nc.Block(f"{tag}_gmath", no_gpsimd_drain=True) as b:
        @b.vector
        def _(ve):
            ve.wait_ge(c.sS.h, tok_p1)
            ve.tensor_scalar(p2[:], p1[:], -1.0, 1.0, ALU.mult, ALU.add)
            ve.drain()
            ve.tensor_tensor(lg[:], eq1[:], p1[:, :, None].broadcast_to([128, TCH, NL]), ALU.mult)
            ve.tensor_tensor(eq2[:], eq2[:], p2[:, :, None].broadcast_to([128, TCH, NL]), ALU.mult)
            ve.tensor_add(lg[:], lg[:], eq2[:])
            ve.drain()
            ve.reduce_sum(wz[:, :, None], lg[:, :, NE:NL], AX.X)
            ve.drain()
            ve.tensor_tensor(ab[:, 0, :], wz[:], c.rstd[:], ALU.mult)
            ve.drain()
            ve.tensor_tensor(ab[:, 1, :], ab[:, 0, :], c.mu[:], ALU.mult)
            ve.tensor_scalar_add(ab[:, 0, :], ab[:, 0, :], 1.0)
            ve.drain()
            ins = None
            for i in range(TCH):
                ins = ve.tensor_scalar(c.xres[:, i, :], c.xres[:, i, :],
                                       ab[:, 0, i:i + 1], ab[:, 1, i:i + 1],
                                       ALU.mult, ALU.subtract)
            c.sV.inc(ins)
    tok_gates = c.sV.v

    w1t = w["l0f_w1t"]
    w2t = w["l0f_w2t"]
    sW0, sMM1, sS1, sV1 = c.sW.v, c.sMM.v, c.sS.v, c.sV.v
    w1tok = lambda e, fc: sW0 + e * 32 * 16 + (fc + 1) * 16
    w2tok = lambda e: sW0 + e * 32 * 16 + 32 * 16
    hmm = lambda e, g: sMM1 + e * 96 + g + 1
    ymm = lambda e, yi: sMM1 + e * 96 + 64 + yi + 1
    hev = lambda e, g: sS1 + e * 64 + g + 1
    ytok = lambda e, yi: sV1 + e * 32 + yi + 1

    with nc.Block(f"{tag}_experts", no_gpsimd_drain=True) as b:
        @b.gpsimd
        def _(ge):
            for e in range(NE):
                src1 = w1t[e].rearrange("(dc p) f -> p dc f", p=128)
                for fc in range(FC):
                    slot = fc % 2
                    if fc >= 2:
                        ge.wait_ge(c.sMM.h, hmm(e, (fc - 2) * 4 + 3))
                    elif e > 0:
                        ge.wait_ge(c.sMM.h, hmm(e - 1, (fc + 14) * 4 + 3))
                    ins = ge.dma_start(out=w1rot[slot][:],
                                       in_=src1[:, :, fc * 128:(fc + 1) * 128])
                    assert c.sW.inc(ins, 16) == w1tok(e, fc)
                if e > 0:
                    ge.wait_ge(c.sV.h, ytok(e - 1, 31))
                src2 = w2t[e].rearrange("(fc p) d -> p fc d", p=128)
                for fc in range(FC):
                    ins = ge.dma_start(out=w2_sb[:, fc, :], in_=src2[:, fc, :])
                    c.sW.inc(ins, 16)
                assert c.sW.v == w2tok(e)

        @b.tensor
        def _(te):
            for e in range(NE):
                for fc in range(FC):
                    te.wait_ge(c.sW.h, w1tok(e, fc))
                    for tt in range(4):
                        g = fc * 4 + tt
                        if g >= 2:
                            te.wait_ge(c.sS.h, hev(e, g - 2))
                        elif e > 0:
                            te.wait_ge(c.sS.h, hev(e - 1, 62 + g))
                        ins = None
                        for dc in range(DCH):
                            ins = te.matmul(ps_h[g % 2][:],
                                            w1rot[fc % 2][:, dc, :],
                                            c.xlnT[:, dc, tt * 512:(tt + 1) * 512],
                                            start=(dc == 0), stop=(dc == DCH - 1))
                        assert c.sMM.inc(ins) == hmm(e, g)
                te.wait_ge(c.sW.h, w2tok(e))
                te.wait_ge(c.sS.h, hev(e, 63))
                for tcol in range(TCH):
                    for dd in range(2):
                        yi = tcol * 2 + dd
                        if yi >= 4:
                            te.wait_ge(c.sV.h, ytok(e, yi - 4))
                        elif e > 0:
                            te.wait_ge(c.sV.h, ytok(e - 1, 28 + yi))
                        ins = None
                        for fc in range(FC):
                            ins = te.matmul(ps_y[yi % 4][:],
                                            h_sb[:, fc, tcol * 128:(tcol + 1) * 128],
                                            w2_sb[:, fc, dd * 512:(dd + 1) * 512],
                                            start=(fc == 0), stop=(fc == FC - 1))
                        assert c.sMM.inc(ins) == ymm(e, yi)

        @b.scalar
        def _(se):
            for e in range(NE):
                if e > 0:
                    se.wait_ge(c.sMM.h, ymm(e - 1, 31))
                for g in range(64):
                    fc, tt = g // 4, g % 4
                    se.wait_ge(c.sMM.h, hmm(e, g))
                    ins = se.activation(h_sb[:, fc, tt * 512:(tt + 1) * 512],
                                        ps_h[g % 2][:], AF.Silu)
                    assert c.sS.inc(ins) == hev(e, g)

        @b.vector
        def _(ve):
            for e in range(NE):
                for yi in range(32):
                    tcol, dd = yi // 2, yi % 2
                    ve.wait_ge(c.sMM.h, ymm(e, yi))
                    ins = ve.scalar_tensor_tensor(
                        c.xres[:, tcol, dd * 512:(dd + 1) * 512],
                        ps_y[yi % 4][:], lg[:, tcol, e:e + 1],
                        c.xres[:, tcol, dd * 512:(dd + 1) * 512],
                        ALU.mult, ALU.add)
                    assert c.sV.inc(ins) == ytok(e, yi)

    exp.close()
    c.stream_deps = [(c.sV, c.sV.v)]
    c.fence_deps = c.barrier_tokens()
    ex.close()


L0B_PHASES = 9

def emit_l0b(c, w, dram, tag="s2"):
    nc = c.nc
    emit_ln(c, tag)
    fence = list(c.fence_deps)
    ex = ExitStack()
    sb = lambda name, shape, dt: ex.enter_context(nc.sbuf_tensor(f"{tag}_{name}", shape, dt))
    q_fm = sb("qfm", [128, DCH, T], BF16)
    bq_cols = sb("bqc", [128, DCH], F32)
    bkv_row = sb("bkvr", [1, 2048], BF16)
    bout_row = sb("boutr", [1, D], BF16)
    nk2 = sb("nk2", [128, TCH], F32)
    rk = sb("rk", [128, TCH], F32)
    k_rot = [sb(f"k{i}", [128, D], BF16) for i in range(2)]
    v_rot = [sb(f"v{i}", [128, D], BF16) for i in range(2)]
    pr_rot = [sb(f"pr{i}", [128, D], BF16) for i in range(2)]
    sq2 = sb("sq2", [128, T], BF16)
    row_bf = sb("rowbf", [1, T], BF16)
    rq_col = sb("rqc", [128, TCH], F32)
    kvT_col = sb("kvT", [128, DCH], F32)
    bb_sb = sb("bb", [128, D], BF16)
    t2 = [sb(f"t2_{i}", [128, 512], F32) for i in range(2)]

    ex2 = ExitStack()
    wqkv_sb = ex2.enter_context(nc.sbuf_tensor(f"{tag}_wqkv", [128, DCH, 2 * D], BF16))

    with nc.Block(f"{tag}_load", no_gpsimd_drain=True) as b:
        @b.gpsimd
        def _(ge):
            _wait(ge, fence)
            src = w["l0b_wqkv"].rearrange("(dc p) e -> p dc e", p=128)
            for dc in range(DCH):
                c.sW.inc(ge.dma_start(out=wqkv_sb[:, dc, 0:D], in_=src[:, dc, 0:D]), 16)
            c.sW.inc(ge.dma_start(out=bkv_row[:], in_=w["l0b_bqkv"][:, D:3 * D]), 16)
            c.sW.inc(ge.dma_start(out=bout_row[:], in_=w["l0b_bout"]), 16)

        @b.sync
        def _(sy):
            _wait(sy, fence)
            c.sX.inc(sy.dma_start(out=bq_cols[:], in_=w["l0b_bq_cols"]), 16)
    tok_w = c.sW.v
    tok_bq = c.sX.v

    ex_row = ExitStack()
    ps_row = [c.psum(ex_row, f"{tag}_psr{i}", [1, 512]) for i in range(2)]
    exp1 = ExitStack()
    ps_q = [c.psum(exp1, f"{tag}_psq{i}", [128, 512]) for i in range(2)]
    ps_k = [c.psum(exp1, f"{tag}_psk{i}", [128, 512]) for i in range(2)]
    ps_v = [c.psum(exp1, f"{tag}_psv{i}", [128, 512]) for i in range(2)]

    sMM0, sV0q = c.sMM.v, c.sV.v
    qmm = lambda g: sMM0 + g + 1
    qev = lambda g: sV0q + g + 1
    with nc.Block(f"{tag}_p1", no_gpsimd_drain=True) as b:
        @b.tensor
        def _(te):
            _wait(te, fence)
            te.wait_ge(c.sW.h, tok_w)
            te.wait_ge(c.sS.h, c.tok_xlnT[1])
            for ec in range(DCH):
                for tt in range(4):
                    g = ec * 4 + tt
                    if g >= 2:
                        te.wait_ge(c.sV.h, qev(g - 2))
                    ins = None
                    for dc in range(DCH):
                        ins = te.matmul(ps_q[g % 2][:], wqkv_sb[:, dc, ec * 128:(ec + 1) * 128],
                                        c.xlnT[:, dc, tt * 512:(tt + 1) * 512],
                                        start=(dc == 0), stop=(dc == DCH - 1))
                    assert c.sMM.inc(ins) == qmm(g)

        @b.vector
        def _(ve):
            ve.wait_ge(c.sX.h, tok_bq)
            for g in range(32):
                ec, tt = g // 4, g % 4
                ve.wait_ge(c.sMM.h, qmm(g))
                ins = ve.tensor_scalar_add(q_fm[:, ec, tt * 512:(tt + 1) * 512],
                                           ps_q[g % 2][:], bq_cols[:, ec:ec + 1])
                assert c.sV.inc(ins) == qev(g)

    if L0B_PHASES < 2:
        c.fence_deps = c.barrier_tokens()
        exp1.close(); ex_row.close(); ex2.close(); ex.close()
        return
    tok_p1_mm = c.sMM.v
    with nc.Block(f"{tag}_loadB", no_gpsimd_drain=True) as b:
        @b.gpsimd
        def _(ge):
            ge.wait_ge(c.sMM.h, tok_p1_mm)
            src = w["l0b_wqkv"].rearrange("(dc p) e -> p dc e", p=128)
            for dc in range(DCH):
                c.sW.inc(ge.dma_start(out=wqkv_sb[:, dc, :], in_=src[:, dc, D:3 * D]), 16)
    tok_wB = c.sW.v

    sMM0, sS0, sV0 = c.sMM.v, c.sS.v, c.sV.v
    kvmm = lambda i, g: sMM0 + i * 5 + g + 1
    rowmm = lambda i: sMM0 + i * 5 + 5
    sev = lambda i, j: sS0 + i * 6 + j + 1
    vtk = lambda i, j: sV0 + i * 2 + j + 1
    with nc.Block(f"{tag}_p2", no_gpsimd_drain=True) as b:
        @b.tensor
        def _(te):
            te.wait_ge(c.sW.h, tok_wB)
            for i in range(TCH):
                for g in range(4):
                    src_e = g * 512
                    pp = (ps_k if g < 2 else ps_v)[g % 2]
                    if i >= 1:
                        te.wait_ge(c.sS.h, sev(i - 1, g))
                    for dc in range(DCH):
                        te.matmul(pp[:], c.xlnT[:, dc, i * 128:(i + 1) * 128],
                                  wqkv_sb[:, dc, src_e:src_e + 512],
                                  start=(dc == 0), stop=False)
                    ins = te.matmul(pp[:], c.ones[0:1, :],
                                    bkv_row[:, src_e:src_e + 512],
                                    start=False, stop=True)
                    assert c.sMM.inc(ins) == kvmm(i, g)
                te.wait_ge(c.sV.h, vtk(i, 1))
                ins = None
                for half in range(2):
                    ins = te.matmul(ps_row[half][:], c.ones[:, 0:1],
                                    pr_rot[i % 2][:, half * 512:(half + 1) * 512],
                                    start=(i == 0), stop=(i == TCH - 1))
                c.sMM.inc(ins)
                assert c.sMM.v == rowmm(i)

        @b.scalar
        def _(se):
            for i in range(TCH):
                for g in range(4):
                    se.wait_ge(c.sMM.h, kvmm(i, g))
                    if i >= 2:
                        se.wait_ge(c.sV.h, vtk(i - 2, 1))
                    dst = (k_rot if g < 2 else v_rot)[i % 2][:, (g % 2) * 512:(g % 2) * 512 + 512]
                    ins = se.copy(out=dst, in_=(ps_k if g < 2 else ps_v)[g % 2][:])
                    assert c.sS.inc(ins) == sev(i, g)
                ins = se.activation(c.sq_scr[:], k_rot[i % 2][:], AF.Square,
                                    accum_out=nk2[:, i:i + 1])
                assert c.sS.inc(ins) == sev(i, 4)
                se.wait_ge(c.sV.h, vtk(i, 0))
                ins = se.activation(rk[:, i:i + 1], rk[:, i:i + 1], AF.Sqrt)
                assert c.sS.inc(ins) == sev(i, 5)

        @b.vector
        def _(ve):
            for i in range(TCH):
                ve.wait_ge(c.sS.h, sev(i, 4))
                ins = ve.reciprocal(rk[:, i:i + 1], nk2[:, i:i + 1])
                ve.drain()
                assert c.sV.inc(ins) == vtk(i, 0)
                ve.wait_ge(c.sS.h, sev(i, 5))
                if i >= 2:
                    ve.wait_ge(c.sMM.h, rowmm(i - 2))
                ins = ve.scalar_tensor_tensor(pr_rot[i % 2][:], k_rot[i % 2][:],
                                              rk[:, i:i + 1], v_rot[i % 2][:],
                                              ALU.mult, ALU.mult)
                assert c.sV.inc(ins) == vtk(i, 1)

    tok_p2_mm = c.sMM.v
    exp1.close()
    ex2.close()
    if L0B_PHASES < 3:
        c.fence_deps = c.barrier_tokens()
        ex_row.close(); ex.close()
        return

    exn1 = ExitStack()
    ps_nq = [c.psum(exn1, f"{tag}_psnq{i}", [1, 512]) for i in range(4)]
    ps_tr2 = c.psum(exn1, f"{tag}_pstr2", [128, TCH], F32)
    sMM0, sS0 = c.sMM.v, c.sS.v
    sqe = lambda ec: sS0 + ec + 1
    nqmm = lambda ec: sMM0 + ec + 1
    with nc.Block(f"{tag}_qn", no_gpsimd_drain=True) as b:
        @b.scalar
        def _(se):
            for ec in range(DCH):
                if ec >= 1:
                    se.wait_ge(c.sMM.h, nqmm(ec - 1))
                ins = se.activation(sq2[:], q_fm[:, ec, :], AF.Square)
                assert c.sS.inc(ins) == sqe(ec)

        @b.tensor
        def _(te):
            for ec in range(DCH):
                te.wait_ge(c.sS.h, sqe(ec))
                ins = None
                for tt in range(4):
                    ins = te.matmul(ps_nq[tt][:], c.ones[:, 0:1],
                                    sq2[:, tt * 512:(tt + 1) * 512],
                                    start=(ec == 0), stop=(ec == DCH - 1))
                assert c.sMM.inc(ins) == nqmm(ec)

    if L0B_PHASES < 2.7:
        c.fence_deps = c.barrier_tokens()
        exn1.close(); ex_row.close(); ex.close()
        return
    tok_nq = c.sMM.v
    sV0 = c.sV.v
    with nc.Block(f"{tag}_qn2", no_gpsimd_drain=True) as b:
        @b.vector
        def _(ve):
            ve.wait_ge(c.sMM.h, tok_nq)
            ins = None
            for tt in range(4):
                ins = ve.tensor_copy(out=row_bf[:, tt * 512:(tt + 1) * 512], in_=ps_nq[tt][:])
            c.sV.inc(ins)
    tok_nqrow = c.sV.v

    if L0B_PHASES < 2.9:
        c.fence_deps = c.barrier_tokens()
        exn1.close(); ex_row.close(); ex.close()
        return
    sMM0, sV0, sS0 = c.sMM.v, c.sV.v, c.sS.v
    stage = ex.enter_context(nc.sbuf_tensor(f"{tag}_stage", [128, 128], BF16))
    ps_trs = exn1.enter_context(nc.psum_tensor(f"{tag}_pstrs", [128, 128], BF16))
    stg_tok = [sV0 + 1 + i for i in range(TCH)]
    trp_tok = [sMM0 + 1 + i for i in range(TCH)]
    col_tok = [sS0 + 1 + i for i in range(TCH)]
    with nc.Block(f"{tag}_qn3", no_gpsimd_drain=True) as b:
        @b.vector
        def _(ve):
            ve.memset(stage[:], 0.0)
            for i in range(TCH):
                if i >= 1:
                    ve.wait_ge(c.sMM.h, trp_tok[i - 1])
                ins = ve.tensor_copy(out=stage[0:1, :], in_=row_bf[:, i * 128:(i + 1) * 128])
                assert c.sV.inc(ins) == stg_tok[i]

        @b.tensor
        def _(te):
            for i in range(TCH):
                te.wait_ge(c.sV.h, stg_tok[i])
                if i >= 1:
                    te.wait_ge(c.sS.h, col_tok[i - 1])
                ins = te.transpose(ps_trs[:], stage[:], c.ident[:])
                assert c.sMM.inc(ins) == trp_tok[i]

        @b.scalar
        def _(se):
            for i in range(TCH):
                se.wait_ge(c.sMM.h, trp_tok[i])
                ins = se.copy(out=rq_col[:, i:i + 1], in_=ps_trs[:, 0:1])
                assert c.sS.inc(ins) == col_tok[i]
    tok_qn3 = c.sS.v
    sV0, sS0 = c.sV.v, c.sS.v
    with nc.Block(f"{tag}_qn4", no_gpsimd_drain=True) as b:
        @b.vector
        def _(ve):
            ve.wait_ge(c.sS.h, tok_qn3)
            ins = ve.reciprocal(rq_col[:], rq_col[:])
            ve.drain()
            c.sV.inc(ins)

        @b.scalar
        def _(se):
            se.wait_ge(c.sV.h, sV0 + 1)
            ins = se.activation(rq_col[:], rq_col[:], AF.Sqrt)
            c.sS.inc(ins)
    tok_rq = (c.sS, c.sS.v)
    tok_rowbf_free = trp_tok[-1]
    exn1.close()
    if L0B_PHASES < 4:
        c.fence_deps = c.barrier_tokens()
        ex_row.close(); ex.close()
        return

    ex3 = ExitStack()
    wout_sb = ex3.enter_context(nc.sbuf_tensor(f"{tag}_wout", [128, DCH, D], BF16))
    ws_sb = ex3.enter_context(nc.sbuf_tensor(f"{tag}_ws", [128, DCH, D], BF16))
    exn2 = ExitStack()
    ps_trk = c.psum(exn2, f"{tag}_pstrk", [128, DCH], F32)
    ps_bb = [c.psum(exn2, f"{tag}_psbb{i}", [128, 512]) for i in range(2)]

    with nc.Block(f"{tag}_wload2", no_gpsimd_drain=True) as b:
        @b.gpsimd
        def _(ge):
            ge.wait_ge(c.sMM.h, tok_p2_mm)
            src = w["l0b_wout"].rearrange("(dc p) e -> p dc e", p=128)
            for dc in range(DCH):
                c.sW.inc(ge.dma_start(out=wout_sb[:, dc, :], in_=src[:, dc, :]), 16)
    tok_wout = c.sW.v

    sV0, sMM0 = c.sV.v, c.sMM.v
    with nc.Block(f"{tag}_ws", no_gpsimd_drain=True) as b:
        @b.vector
        def _(ve):
            ve.wait_ge(c.sMM.h, tok_p2_mm)
            ve.wait_ge(c.sMM.h, tok_rowbf_free)
            ve.tensor_copy(out=row_bf[:, 0:512], in_=ps_row[0][:])
            ins = ve.tensor_copy(out=row_bf[:, 512:1024], in_=ps_row[1][:])
            assert c.sV.inc(ins) == sV0 + 1

        @b.tensor
        def _(te):
            te.wait_ge(c.sV.h, sV0 + 1)
            ins = None
            for dc in range(DCH):
                ins = te.matmul(ps_trk[:, dc:dc + 1], row_bf[:, dc * 128:(dc + 1) * 128],
                                c.ones[0:1, 0:1], start=True, stop=True)
            assert c.sMM.inc(ins) == sMM0 + 1
            te.wait_ge(c.sW.h, tok_w)
            for half in range(2):
                ins = te.matmul(ps_bb[half][:], c.ones[0:1, :],
                                bout_row[:, half * 512:(half + 1) * 512],
                                start=True, stop=True)
            assert c.sMM.inc(ins) == sMM0 + 2

        @b.vector
        def _(ve):
            ve.wait_ge(c.sMM.h, sMM0 + 1)
            ve.tensor_copy(out=kvT_col[:], in_=ps_trk[:])
            ve.drain()
            ve.wait_ge(c.sW.h, tok_wout)
            for dc in range(DCH):
                ve.tensor_scalar_mul(ws_sb[:, dc, :], wout_sb[:, dc, :], kvT_col[:, dc:dc + 1])
            ve.wait_ge(c.sMM.h, sMM0 + 2)
            ve.tensor_copy(out=bb_sb[:, 0:512], in_=ps_bb[0][:])
            ins = ve.tensor_copy(out=bb_sb[:, 512:1024], in_=ps_bb[1][:])
            assert c.sV.inc(ins) == sV0 + 2
    tok_ws = c.sV.v
    exn2.close()
    ex_row.close()
    if L0B_PHASES < 5:
        c.fence_deps = c.barrier_tokens()
        ex3.close(); ex.close()
        return

    exf = ExitStack()
    ps_f = [c.psum(exf, f"{tag}_psf{i}", [128, 512]) for i in range(2)]
    sMM0, sV0 = c.sMM.v, c.sV.v
    fmm = lambda j: sMM0 + j + 1
    fvt = lambda j: sV0 + 2 * (j + 1)
    with nc.Block(f"{tag}_fin", no_gpsimd_drain=True) as b:
        @b.tensor
        def _(te):
            te.wait_ge(c.sV.h, tok_ws)
            for j in range(32):
                i, dd = j // 2, j % 2
                if j >= 2:
                    te.wait_ge(c.sV.h, fvt(j - 2))
                ins = None
                for dc in range(DCH):
                    ins = te.matmul(ps_f[j % 2][:], q_fm[:, dc, i * 128:(i + 1) * 128],
                                    ws_sb[:, dc, dd * 512:(dd + 1) * 512],
                                    start=(dc == 0), stop=(dc == DCH - 1))
                assert c.sMM.inc(ins) == fmm(j)

        @b.vector
        def _(ve):
            ve.wait_ge(tok_rq[0].h, tok_rq[1])
            for j in range(32):
                i, dd = j // 2, j % 2
                ve.wait_ge(c.sMM.h, fmm(j))
                ve.scalar_tensor_tensor(t2[j % 2][:], ps_f[j % 2][:], rq_col[:, i:i + 1],
                                        bb_sb[:, dd * 512:(dd + 1) * 512], ALU.mult, ALU.add)
                ins = ve.tensor_add(c.xres[:, i, dd * 512:(dd + 1) * 512],
                                    c.xres[:, i, dd * 512:(dd + 1) * 512], t2[j % 2][:])
                c.sV.inc(ins, 2)
                assert c.sV.v == fvt(j)

    exf.close()
    ex3.close()
    c.stream_deps = [(c.sV, c.sV.v)]
    c.fence_deps = c.barrier_tokens()
    ex.close()


def emit_l1a(c, w, dram, tag="s4"):
    nc = c.nc
    emit_ln(c, tag)
    fence = list(c.fence_deps)
    EPS = 1e-3
    ex = ExitStack()
    sb = lambda name, shape, dt: ex.enter_context(nc.sbuf_tensor(f"{tag}_{name}", shape, dt))
    wqkv_sb = sb("wqkv", [128, DCH, 3 * D], BF16)
    wproj_sb = sb("wproj", [128, DCH, D], BF16)
    bqkv_row = sb("bqkvr", [1, 3 * D], BF16)
    bproj_row = sb("bprojr", [1, D], BF16)
    scale_col = sb("scol", [128, 1], F32)
    carry = sb("carry", [1, 2 * D], F32)
    carry_bf = sb("carrybf", [1, 2 * D], BF16)
    ea = sb("ea", [128, D], BF16)
    eb = sb("ebuf", [128, D], BF16)
    q_t = sb("qt", [128, D], BF16)
    k_t = sb("kt", [128, D], BF16)
    v_t = sb("vt", [128, D], BF16)
    kv_t = sb("kvt", [128, D], BF16)
    kc_t = sb("kct", [128, D], BF16)
    kvc_t = sb("kvct", [128, D], BF16)
    y_t = sb("yt", [128, D], BF16)
    ylT = sb("ylT", [128, DCH, 128], BF16)
    den_ch = sb("den", [128, 16], F32)
    rden_ch = sb("rden", [128, 16], F32)

    exp = ExitStack()
    ps = [c.psum(exp, f"{tag}_ps{i}", [128, 512]) for i in range(2)]
    ps_cum = [c.psum(exp, f"{tag}_pscum{i}", [128, 512]) for i in range(2)]
    ps_sum = c.psum(exp, f"{tag}_pssum", [1, 512])
    ps_ytr = c.psum(exp, f"{tag}_psytr", [128, D], BF16)
    ps_pr = [c.psum(exp, f"{tag}_pspr{i}", [128, 512]) for i in range(2)]

    with nc.Block(f"{tag}_load", no_gpsimd_drain=True) as b:
        @b.gpsimd
        def _(ge):
            _wait(ge, fence)
            src = w["l1a_wqkvt"].rearrange("(dc p) e -> p dc e", p=128)
            for dc in range(DCH):
                c.sW.inc(ge.dma_start(out=wqkv_sb[:, dc, :], in_=src[:, dc, :]), 16)
            src = w["l1a_wprojt"].rearrange("(dc p) e -> p dc e", p=128)
            for dc in range(DCH):
                c.sW.inc(ge.dma_start(out=wproj_sb[:, dc, :], in_=src[:, dc, :]), 16)
            c.sW.inc(ge.dma_start(out=bqkv_row[:], in_=w["l1a_bqkv"]), 16)
            c.sW.inc(ge.dma_start(out=bproj_row[:], in_=w["l1a_bproj"]), 16)

        @b.sync
        def _(sy):
            _wait(sy, fence)
            c.sX.inc(sy.dma_start(out=scale_col[:], in_=w["l1a_scale_col"]), 16)
    tok_w = c.sW.v
    tok_sc = c.sX.v

    sMM0, sS0, sV0 = c.sMM.v, c.sS.v, c.sV.v
    NT, NS, NV = 17, 9, 24
    tM = lambda i, j: sMM0 + i * NT + j
    tS = lambda i, j: sS0 + i * NS + j
    tV = lambda i, j: sV0 + 1 + i * NV + j
    cons = {0: ("V", 3), 1: ("V", 4), 2: ("V", 8), 3: ("V", 9), 4: ("S", 3), 5: ("S", 4)}

    with nc.Block(f"{tag}_main", no_gpsimd_drain=True) as b:
        @b.tensor
        def _(te):
            te.wait_ge(c.sW.h, tok_w)
            te.wait_ge(c.sS.h, c.tok_xlnT[1])
            for i in range(TCH):
                for g in range(6):
                    G = i * 6 + g
                    if G >= 2:
                        pi, pg = divmod(G - 2, 6)
                        eng, j = cons[pg]
                        if eng == "V":
                            te.wait_ge(c.sV.h, tV(pi, j))
                        else:
                            te.wait_ge(c.sS.h, tS(pi, j))
                    for dc in range(DCH):
                        te.matmul(ps[g % 2][:], c.xlnT[:, dc, i * 128:(i + 1) * 128],
                                  wqkv_sb[:, dc, g * 512:(g + 1) * 512],
                                  start=(dc == 0), stop=False)
                    ins = te.matmul(ps[g % 2][:], c.ones[0:1, :],
                                    bqkv_row[:, g * 512:(g + 1) * 512],
                                    start=False, stop=True)
                    assert c.sMM.inc(ins) == tM(i, g + 1)
                te.wait_ge(c.sV.h, tV(i, 10))
                te.wait_ge(c.sV.h, tV(i - 1, 16) if i >= 1 else sV0 + 1)
                for sl in range(2):
                    if i >= 1:
                        te.wait_ge(c.sS.h, tS(i - 1, 7 + sl))
                    te.matmul(ps_cum[sl][:], c.triu[:], k_t[:, sl * 512:(sl + 1) * 512],
                              start=True, stop=False)
                    ins = te.matmul(ps_cum[sl][:], c.ones[0:1, :],
                                    carry_bf[:, sl * 512:(sl + 1) * 512],
                                    start=False, stop=True)
                    assert c.sMM.inc(ins) == tM(i, 7 + sl)
                te.wait_ge(c.sV.h, tV(i, 11))
                for sl in range(2):
                    te.wait_ge(c.sS.h, tS(i, 5 + sl))
                    te.matmul(ps_cum[sl][:], c.triu[:], kv_t[:, sl * 512:(sl + 1) * 512],
                              start=True, stop=False)
                    ins = te.matmul(ps_cum[sl][:], c.ones[0:1, :],
                                    carry_bf[:, D + sl * 512:D + (sl + 1) * 512],
                                    start=False, stop=True)
                    assert c.sMM.inc(ins) == tM(i, 9 + sl)
                for j in range(4):
                    if j == 0:
                        if i >= 1:
                            te.wait_ge(c.sV.h, tV(i - 1, 15))
                    else:
                        te.wait_ge(c.sV.h, tV(i, 11 + j))
                    src = (k_t if j < 2 else kv_t)[:, (j % 2) * 512:(j % 2) * 512 + 512]
                    ins = te.matmul(ps_sum[:], c.ones[:, 0:1], src, start=True, stop=True)
                    assert c.sMM.inc(ins) == tM(i, 11 + j)
                te.wait_ge(c.sV.h, tV(i, 22))
                if i >= 1:
                    te.wait_ge(c.sS.h, tS(i - 1, 9))
                ins = None
                for cc_ in range(DCH):
                    ins = te.transpose(ps_ytr[:, cc_ * 128:(cc_ + 1) * 128],
                                       y_t[:, cc_ * 128:(cc_ + 1) * 128], c.ident[:])
                assert c.sMM.inc(ins) == tM(i, 15)
                te.wait_ge(c.sS.h, tS(i, 9))
                for cs in range(2):
                    if i >= 1:
                        te.wait_ge(c.sV.h, tV(i - 1, 23 + cs))
                    for ec in range(DCH):
                        te.matmul(ps_pr[cs][:], ylT[:, ec, :],
                                  wproj_sb[:, ec, cs * 512:(cs + 1) * 512],
                                  start=(ec == 0), stop=False)
                    ins = te.matmul(ps_pr[cs][:], c.ones[0:1, :],
                                    bproj_row[:, cs * 512:(cs + 1) * 512],
                                    start=False, stop=True)
                    assert c.sMM.inc(ins) == tM(i, 16 + cs)

        @b.scalar
        def _(se):
            for i in range(TCH):
                se.wait_ge(c.sV.h, tV(i, 2))
                ins = se.activation(ea[:], ea[:], AF.Exp)
                assert c.sS.inc(ins) == tS(i, 1)
                se.wait_ge(c.sV.h, tV(i, 7))
                ins = se.activation(ea[:], ea[:], AF.Exp)
                assert c.sS.inc(ins) == tS(i, 2)
                for sl in range(2):
                    se.wait_ge(c.sMM.h, tM(i, 5 + sl))
                    if i >= 1:
                        se.wait_ge(c.sV.h, tV(i - 1, 11))
                    ins = se.copy(out=v_t[:, sl * 512:(sl + 1) * 512],
                                  in_=ps[sl][:])
                    assert c.sS.inc(ins) == tS(i, 3 + sl)
                for sl in range(2):
                    se.wait_ge(c.sMM.h, tM(i, 7 + sl))
                    if i >= 1:
                        se.wait_ge(c.sV.h, tV(i - 1, 17))
                    ins = se.copy(out=kc_t[:, sl * 512:(sl + 1) * 512], in_=ps_cum[sl][:])
                    assert c.sS.inc(ins) == tS(i, 5 + sl)
                for sl in range(2):
                    se.wait_ge(c.sMM.h, tM(i, 9 + sl))
                    if i >= 1:
                        se.wait_ge(c.sV.h, tV(i - 1, 21))
                    ins = se.copy(out=kvc_t[:, sl * 512:(sl + 1) * 512], in_=ps_cum[sl][:])
                    assert c.sS.inc(ins) == tS(i, 7 + sl)
                se.wait_ge(c.sMM.h, tM(i, 15))
                if i >= 1:
                    se.wait_ge(c.sMM.h, tM(i - 1, 17))
                ins = se.copy(out=ylT[:],
                              in_=ps_ytr[:].rearrange("p (cc q) -> p cc q", cc=DCH))
                assert c.sS.inc(ins) == tS(i, 9)

        @b.vector
        def _(ve):
            ve.memset(carry[:], 0.0)
            ins = ve.memset(carry_bf[:], 0.0)
            assert c.sV.inc(ins) == sV0 + 1
            ve.wait_ge(c.sX.h, tok_sc)
            for i in range(TCH):
                bank_q = [0, 1]
                for sl in range(2):
                    ve.wait_ge(c.sMM.h, tM(i, 1 + sl))
                    ins = ve.tensor_scalar(ea[:, sl * 512:(sl + 1) * 512],
                                           ps[bank_q[sl]][:], scale_col[:], 0.0,
                                           ALU.mult, ALU.min)
                    assert c.sV.inc(ins) == tV(i, 1 + sl)
                for sl in range(2):
                    ins = ve.tensor_scalar(eb[:, sl * 512:(sl + 1) * 512],
                                           ps[bank_q[sl]][:], scale_col[:], 0.0,
                                           ALU.mult, ALU.max)
                    assert c.sV.inc(ins) == tV(i, 3 + sl)
                ve.wait_ge(c.sS.h, tS(i, 1))
                ins = ve.tensor_add(q_t[:], ea[:], eb[:])
                assert c.sV.inc(ins) == tV(i, 5)
                bank_k = [0, 1]
                for sl in range(2):
                    ve.wait_ge(c.sMM.h, tM(i, 3 + sl))
                    ins = ve.tensor_scalar(ea[:, sl * 512:(sl + 1) * 512],
                                           ps[bank_k[sl]][:], scale_col[:], 0.0,
                                           ALU.mult, ALU.min)
                    assert c.sV.inc(ins) == tV(i, 6 + sl)
                for sl in range(2):
                    ins = ve.tensor_scalar(eb[:, sl * 512:(sl + 1) * 512],
                                           ps[bank_k[sl]][:], scale_col[:], 0.0,
                                           ALU.mult, ALU.max)
                    assert c.sV.inc(ins) == tV(i, 8 + sl)
                ve.wait_ge(c.sS.h, tS(i, 2))
                ins = ve.tensor_add(k_t[:], ea[:], eb[:])
                assert c.sV.inc(ins) == tV(i, 10)
                ve.wait_ge(c.sS.h, tS(i, 4))
                ins = ve.tensor_mul(kv_t[:], k_t[:], v_t[:])
                assert c.sV.inc(ins) == tV(i, 11)
                for j in range(4):
                    ve.wait_ge(c.sMM.h, tM(i, 11 + j))
                    off = (0 if j < 2 else D) + (j % 2) * 512
                    ins = ve.tensor_add(carry[:, off:off + 512], carry[:, off:off + 512],
                                        ps_sum[:])
                    assert c.sV.inc(ins) == tV(i, 12 + j)
                ve.drain()
                ve.wait_ge(c.sMM.h, tM(i, 10))
                ins = ve.tensor_copy(out=carry_bf[:], in_=carry[:])
                assert c.sV.inc(ins) == tV(i, 16)
                ve.wait_ge(c.sS.h, tS(i, 6))
                ins = ve.tensor_mul(ea[:], q_t[:], kc_t[:])
                assert c.sV.inc(ins) == tV(i, 17)
                ve.drain()
                ins = ve.reduce_sum(den_ch[:, :, None],
                                    ea[:].rearrange("p (h d) -> p h d", h=16), AX.X)
                assert c.sV.inc(ins) == tV(i, 18)
                ve.drain()
                ins = ve.tensor_scalar_max(den_ch[:], den_ch[:], EPS)
                assert c.sV.inc(ins) == tV(i, 19)
                ve.drain()
                ins = ve.reciprocal(rden_ch[:], den_ch[:])
                assert c.sV.inc(ins) == tV(i, 20)
                ve.drain()
                ve.wait_ge(c.sS.h, tS(i, 8))
                ins = ve.tensor_mul(ea[:], q_t[:], kvc_t[:])
                assert c.sV.inc(ins) == tV(i, 21)
                if i >= 1:
                    ve.wait_ge(c.sMM.h, tM(i - 1, 15))
                ins = ve.tensor_tensor(y_t[:].rearrange("p (h d) -> p h d", h=16),
                                       ea[:].rearrange("p (h d) -> p h d", h=16),
                                       rden_ch[:, :, None].broadcast_to([128, 16, 64]),
                                       ALU.mult)
                assert c.sV.inc(ins) == tV(i, 22)
                for cs in range(2):
                    ve.wait_ge(c.sMM.h, tM(i, 16 + cs))
                    ins = ve.tensor_add(c.xres[:, i, cs * 512:(cs + 1) * 512],
                                        c.xres[:, i, cs * 512:(cs + 1) * 512],
                                        ps_pr[cs][:])
                    assert c.sV.inc(ins) == tV(i, 23 + cs)

    exp.close()
    c.stream_deps = [(c.sV, c.sV.v)]
    c.fence_deps = c.barrier_tokens()
    ex.close()


def emit_l1ffn(c, w, dram, tag):
    nc = c.nc
    emit_ln(c, tag)
    fence = list(c.fence_deps)
    NE, FF = 8, 4096
    FC = FF // 128
    NSPAN = 2
    ex = ExitStack()
    sb = lambda name, shape, dt: ex.enter_context(nc.sbuf_tensor(f"{tag}_{name}", shape, dt))
    wr_sb = sb("wr", [128, DCH, NE], BF16)
    br_row = sb("br", [1, NE], BF16)
    b1_cols = sb("b1c", [128, NE * FC], F32)
    b2_sb = sb("b2", [8, D], BF16)
    lgr = sb("lgr", [128, TCH, NE], F32)
    Xs = sb("Xs", [128, TCH, NE], F32)
    diff = sb("diff", [128, TCH, NE], F32)
    sq = sb("sq", [128, TCH, NE], F32)
    gates = sb("gates", [128, TCH, NE], F32)
    gates_bf = sb("gatesb", [128, TCH, NE], BF16)
    tau_lo = sb("taulo", [128, TCH], F32)
    tau_m = sb("taum", [128, TCH], F32)
    f_lo = sb("flo", [128, TCH], F32)
    fm = sb("fm", [128, TCH], F32)
    prod = sb("prod", [128, TCH], F32)
    maskb = sb("maskb", [128, TCH], mybir.dt.uint8)
    gT_sb = sb("gT", [8, TCH, 128], BF16)
    h_sb = sb("h", [128, FC, 1024], BF16)
    w1rot = [sb(f"w1r{i}", [128, DCH, 256], BF16) for i in range(3)]
    w2rot = [sb(f"w2r{i}", [128, 2, 512], BF16) for i in range(3)]

    exp_r = ExitStack()
    ps_r = [c.psum(exp_r, f"{tag}_psr{i}", [128, NE]) for i in range(2)]
    ps_gt = c.psum(exp_r, f"{tag}_psgt", [8, 128], BF16)

    with nc.Block(f"{tag}_load", no_gpsimd_drain=True) as b:
        @b.gpsimd
        def _(ge):
            _wait(ge, fence)
            c.sW.inc(ge.dma_start(out=wr_sb[:], in_=w["l1f_wrt"]), 16)
            c.sW.inc(ge.dma_start(out=br_row[:], in_=w["l1f_br"]), 16)
            c.sW.inc(ge.dma_start(out=b2_sb[:], in_=w["l1f_b2"]), 16)

        @b.sync
        def _(sy):
            _wait(sy, fence)
            c.sX.inc(sy.dma_start(out=b1_cols[:], in_=w["l1f_b1_cols"]), 16)
    tok_sm = c.sW.v
    tok_b1 = c.sX.v

    sMM0, sV0, sS0 = c.sMM.v, c.sV.v, c.sS.v
    rmm = lambda i: sMM0 + i + 1
    rev = lambda i: sV0 + i + 1
    with nc.Block(f"{tag}_router", no_gpsimd_drain=True) as b:
        @b.tensor
        def _(te):
            _wait(te, fence)
            te.wait_ge(c.sW.h, tok_sm)
            te.wait_ge(c.sS.h, c.tok_xlnT[1])
            for i in range(TCH):
                if i >= 2:
                    te.wait_ge(c.sV.h, rev(i - 2))
                for dc in range(DCH):
                    te.matmul(ps_r[i % 2][:], c.xlnT[:, dc, i * 128:(i + 1) * 128],
                              wr_sb[:, dc, :], start=(dc == 0), stop=False)
                ins = te.matmul(ps_r[i % 2][:], c.ones[0:1, :], br_row[:],
                                start=False, stop=True)
                assert c.sMM.inc(ins) == rmm(i)

        @b.vector
        def _(ve):
            for i in range(TCH):
                ve.wait_ge(c.sMM.h, rmm(i))
                ins = ve.tensor_copy(out=lgr[:, i, :], in_=ps_r[i % 2][:])
                assert c.sV.inc(ins) == rev(i)
            bcast = lambda t: t[:, :, None].broadcast_to([128, TCH, NE])
            ve.drain()
            ve.tensor_scalar_mul(Xs[:], lgr[:], 0.5)
            ve.drain()
            ve.reduce_max(tau_lo[:, :, None], Xs[:], AX.X)
            ve.drain()
            ve.tensor_scalar_add(tau_lo[:], tau_lo[:], -1.0)
            ve.drain()
            ve.tensor_tensor(diff[:], Xs[:], bcast(tau_lo), ALU.subtract)
            ve.scalar_tensor_tensor(sq[:], diff[:], 0.0, diff[:], ALU.max, ALU.mult)
            ve.drain()
            ve.reduce_sum(f_lo[:, :, None], sq[:], AX.X)
            ve.drain()
            ve.tensor_scalar_add(f_lo[:], f_lo[:], -1.0)
            ve.drain()
            dmc = float(1.0 - (1.0 / NE) ** 0.5)
            for it in range(25):
                dmi = dmc * (0.5 ** (it + 1))
                ve.tensor_scalar_add(tau_m[:], tau_lo[:], dmi)
                ve.drain()
                ve.tensor_tensor(diff[:], Xs[:], bcast(tau_m), ALU.subtract)
                ve.scalar_tensor_tensor(sq[:], diff[:], 0.0, diff[:], ALU.max, ALU.mult)
                ve.drain()
                ve.reduce_sum(fm[:, :, None], sq[:], AX.X)
                ve.drain()
                ve.tensor_scalar_add(fm[:], fm[:], -1.0)
                ve.drain()
                ve.tensor_mul(prod[:], fm[:], f_lo[:])
                ve.drain()
                ve.tensor_scalar(maskb[:], prod[:], 0.0, None, ALU.is_ge)
                ve.drain()
                ve.copy_predicated(tau_lo[:], maskb[:], tau_m[:])
                ve.drain()
            ve.tensor_tensor(diff[:], Xs[:], bcast(tau_m), ALU.subtract)
            ve.scalar_tensor_tensor(sq[:], diff[:], 0.0, diff[:], ALU.max, ALU.mult)
            ve.drain()
            ve.reduce_sum(fm[:, :, None], sq[:], AX.X)
            ve.drain()
            ve.reciprocal(fm[:], fm[:])
            ve.drain()
            ve.tensor_tensor(gates[:], sq[:], bcast(fm), ALU.mult)
            ve.drain()
            ve.tensor_scalar(diff[:], gates[:], 1e-9, None, ALU.is_gt)
            ve.drain()
            ve.tensor_mul(gates[:], gates[:], diff[:])
            ve.drain()
            ins = ve.tensor_copy(out=gates_bf[:], in_=gates[:])
            c.sV.inc(ins)
    tok_gates = c.sV.v

    sMM0, sS0 = c.sMM.v, c.sS.v
    with nc.Block(f"{tag}_gt", no_gpsimd_drain=True) as b:
        @b.tensor
        def _(te):
            te.wait_ge(c.sV.h, tok_gates)
            for i in range(TCH):
                if i >= 1:
                    te.wait_ge(c.sS.h, sS0 + i)
                ins = te.transpose(ps_gt[:], gates_bf[:, i, :], c.ident[:])
                assert c.sMM.inc(ins) == sMM0 + i + 1

        @b.scalar
        def _(se):
            for i in range(TCH):
                se.wait_ge(c.sMM.h, sMM0 + i + 1)
                ins = se.copy(out=gT_sb[:, i, :], in_=ps_gt[:])
                assert c.sS.inc(ins) == sS0 + i + 1
    tok_gT = c.sS.v
    exp_r.close()
    exp = ExitStack()
    ps_hy = [c.psum(exp, f"{tag}_pshy{i}", [128, 512]) for i in range(8)]

    w1 = w["l1f_w1"]
    w2 = w["l1f_w2"]
    sMM0, sS0, sV0, sW0 = c.sMM.v, c.sS.v, c.sV.v, c.sW.v
    NT_e, NS_e, NV_e, NW_e = 144, 64, 16, (16 + 32) * 16
    tMb = lambda spe: sMM0 + spe * NT_e
    hmmt = lambda spe, fc, tt: tMb(spe) + fc * 2 + tt + 1
    yfct = lambda spe, dd, fc: tMb(spe) + 64 + dd * 40 + fc + 1
    ybct = lambda spe, dd, tcol: tMb(spe) + 64 + dd * 40 + 32 + tcol + 1
    hevt = lambda spe, g: sS0 + spe * NS_e + g + 1
    yvt = lambda spe, dd, tcol: sV0 + spe * NV_e + dd * 8 + tcol + 1
    w1t = lambda spe, fc: sW0 + spe * NW_e + (fc // 2 + 1) * 16
    w2t = lambda spe, dd, fc: sW0 + spe * NW_e + 16 * 16 + ((dd * 32 + fc) // 2 + 1) * 16

    with nc.Block(f"{tag}_experts", no_gpsimd_drain=True) as b:
        @b.gpsimd
        def _(ge):
            for spe in range(NSPAN * NE):
                sp, e = divmod(spe, NE)
                src1 = w1[e].rearrange("(dc p) f -> p dc f", p=128)
                for fp in range(FC // 2):
                    fc = 2 * fp
                    if fp >= 3:
                        ge.wait_ge(c.sMM.h, hmmt(spe, (fc - 6) * 2 // 2 * 2 + 2 * 2 - 1)
                                   ) if False else ge.wait_ge(c.sMM.h, hmmt(spe, (fp - 3) * 2 + 1, 1))
                    elif spe > 0:
                        ge.wait_ge(c.sMM.h, hmmt(spe - 1, (fp + 13) * 2 + 1, 1))
                    ins = ge.dma_start(out=w1rot[fp % 3][:],
                                       in_=src1[:, :, fc * 128:(fc + 2) * 128])
                    assert c.sW.inc(ins, 16) == w1t(spe, fc)
                src2 = w2[e].rearrange("(fc p) d -> p fc d", p=128)
                for dd in range(2):
                    for fp in range(FC // 2):
                        fc = 2 * fp
                        k = dd * 16 + fp
                        if k >= 3:
                            pd, pf = divmod(k - 3, 16)
                            ge.wait_ge(c.sMM.h, yfct(spe, pd, 2 * pf + 1))
                        elif spe > 0:
                            pd, pf = divmod(k + 29, 16)
                            ge.wait_ge(c.sMM.h, yfct(spe - 1, pd, 2 * pf + 1))
                        ins = ge.dma_start(out=w2rot[k % 3][:],
                                           in_=src2[:, fc:fc + 2, dd * 512:(dd + 1) * 512])
                        assert c.sW.inc(ins, 16) == w2t(spe, dd, fc)

        @b.tensor
        def _(te):
            for spe in range(NSPAN * NE):
                sp, e = divmod(spe, NE)
                t0 = sp * 8
                if spe == 0:
                    te.wait_ge(c.sS.h, tok_gT)
                    te.wait_ge(c.sW.h, tok_sm)
                elif True:
                    te.wait_ge(c.sV.h, yvt(spe - 1, 1, 7))
                for fc in range(FC):
                    te.wait_ge(c.sW.h, w1t(spe, fc))
                    for tt in range(2):
                        g = fc * 2 + tt
                        if g >= 2:
                            te.wait_ge(c.sS.h, hevt(spe, g - 2))
                        elif spe > 0:
                            te.wait_ge(c.sS.h, hevt(spe - 1, 62 + g))
                        ins = None
                        for dc in range(DCH):
                            ins = te.matmul(ps_hy[g % 2][:],
                                            w1rot[(fc // 2) % 3][:, dc, (fc % 2) * 128:(fc % 2) * 128 + 128],
                                            c.xlnT[:, dc, sp * 1024 + tt * 512:sp * 1024 + (tt + 1) * 512],
                                            start=(dc == 0), stop=(dc == DCH - 1))
                        assert c.sMM.inc(ins) == hmmt(spe, fc, tt)
                te.wait_ge(c.sS.h, hevt(spe, 63))
                for dd in range(2):
                    for fc in range(FC):
                        te.wait_ge(c.sW.h, w2t(spe, dd, fc))
                        if dd == 1 and fc == 0:
                            for tcol in range(8):
                                te.wait_ge(c.sV.h, yvt(spe, 0, tcol))
                        ins = None
                        for tcol in range(8):
                            ins = te.matmul(ps_hy[tcol][:],
                                            h_sb[:, fc, tcol * 128:(tcol + 1) * 128],
                                            w2rot[((dd * 32 + fc) // 2) % 3][:, fc % 2, :],
                                            start=(fc == 0), stop=False)
                        assert c.sMM.inc(ins) == yfct(spe, dd, fc)
                    for tcol in range(8):
                        ins = te.matmul(ps_hy[tcol][:], gT_sb[:, t0 + tcol, :],
                                        b2_sb[:, dd * 512:(dd + 1) * 512],
                                        start=False, stop=True)
                        assert c.sMM.inc(ins) == ybct(spe, dd, tcol)

        @b.scalar
        def _(se):
            se.wait_ge(c.sX.h, tok_b1)
            for spe in range(NSPAN * NE):
                sp, e = divmod(spe, NE)
                if spe > 0:
                    se.wait_ge(c.sMM.h, yfct(spe - 1, 1, 31))
                for g in range(64):
                    fc, tt = g // 2, g % 2
                    se.wait_ge(c.sMM.h, hmmt(spe, fc, tt))
                    ins = se.activation(h_sb[:, fc, tt * 512:(tt + 1) * 512],
                                        ps_hy[g % 2][:], AF.Relu,
                                        bias=b1_cols[:, e * FC + fc:e * FC + fc + 1])
                    assert c.sS.inc(ins) == hevt(spe, g)

        @b.vector
        def _(ve):
            for spe in range(NSPAN * NE):
                sp, e = divmod(spe, NE)
                t0 = sp * 8
                for dd in range(2):
                    for tcol in range(8):
                        ve.wait_ge(c.sMM.h, ybct(spe, dd, tcol))
                        ins = ve.scalar_tensor_tensor(
                            c.xres[:, t0 + tcol, dd * 512:(dd + 1) * 512],
                            ps_hy[tcol][:], gates[:, t0 + tcol, e:e + 1],
                            c.xres[:, t0 + tcol, dd * 512:(dd + 1) * 512],
                            ALU.mult, ALU.add)
                        assert c.sV.inc(ins) == yvt(spe, dd, tcol)

    exp.close()
    c.store_chunks = [(0, 8, (c.sV, yvt(NE - 1, 1, 7))),
                      (8, 16, (c.sV, yvt(NSPAN * NE - 1, 1, 7)))]
    c.stream_deps = [(c.sV, c.sV.v)]
    c.fence_deps = c.barrier_tokens()
    ex.close()


def emit_l1b(c, w, dram, tag="s6"):
    nc = c.nc
    emit_ln(c, tag)
    fence = list(c.fence_deps)
    NH = 16
    ex = ExitStack()
    sb = lambda name, shape, dt: ex.enter_context(nc.sbuf_tensor(f"{tag}_{name}", shape, dt))
    K_fm = sb("kfm", [128, DCH, T], BF16)
    V_sb = sb("vsb", [128, TCH, NH, 65], BF16)
    hind = sb("hind", [128, DCH, NH], BF16)
    bq_cols = sb("bqc", [128, DCH], F32)
    bk_cols = sb("bkc", [128, DCH], F32)
    bqs_cols = sb("bqsc", [128, DCH], F32)
    bv_row = sb("bvr", [1, D], BF16)
    bo_row = sb("bor", [1, D], BF16)
    ascale = sb("asc", [128, 1], F32)
    sq1 = sb("sq1", [128, 1024], BF16)
    nk2m = sb("nk2m", [16, 1], F32)
    sk = sb("sk", [16, 1], F32)
    cq2 = sb("cq2", [16, 1], F32)
    cq = sb("cq", [16, 1], F32)
    ch_bf = sb("chbf", [16, 1], BF16)
    chrow = sb("chrow", [1, 16], BF16)
    cneg = sb("cneg", [128, 16], F32)
    rd_row = sb("rdrow", [1, 512], BF16)
    rb = sb("rbb", [64, 512], BF16)

    ex2 = ExitStack()
    wkv_sb = ex2.enter_context(nc.sbuf_tensor(f"{tag}_wkv", [128, DCH, D], BF16))

    with nc.Block(f"{tag}_load", no_gpsimd_drain=True) as b:
        @b.gpsimd
        def _(ge):
            _wait(ge, fence)
            src = w["l1b_wkt"].rearrange("(dc p) e -> p dc e", p=128)
            for dc in range(DCH):
                c.sW.inc(ge.dma_start(out=wkv_sb[:, dc, :], in_=src[:, dc, :]), 16)
            c.sW.inc(ge.dma_start(out=bv_row[:], in_=w["l1b_bv"]), 16)
            c.sW.inc(ge.dma_start(out=bo_row[:], in_=w["l1b_bo"]), 16)
            c.sW.inc(ge.dma_start(out=hind[:], in_=w["l1b_hind"]), 16)

        @b.sync
        def _(sy):
            _wait(sy, fence)
            c.sX.inc(sy.dma_start(out=bq_cols[:], in_=w["l1b_bq_cols"]), 16)
            c.sX.inc(sy.dma_start(out=bk_cols[:], in_=w["l1b_bk_cols"]), 16)
            c.sX.inc(sy.dma_start(out=ascale[:], in_=w["l1b_ascale_col"]), 16)
    tok_w1 = c.sW.v
    tok_cols = c.sX.v

    exp1 = ExitStack()
    ps_kv = [c.psum(exp1, f"{tag}_pskv{i}", [128, 512]) for i in range(4)]
    ps_nk = [c.psum(exp1, f"{tag}_psnk{i}", [16, 512]) for i in range(4)]

    sMM0, sS0, sV0 = c.sMM.v, c.sS.v, c.sV.v
    kmm = lambda g: sMM0 + g + 1
    kev = lambda g: sV0 + 2 + g + 1
    vmm = lambda g: sMM0 + 32 + g + 1
    vev = lambda g: sS0 + g + 1
    sqe = lambda ec, hf: sS0 + 32 + ec * 2 + hf + 1
    nkmm = lambda ec, hf: sMM0 + 64 + ec * 2 + hf + 1

    tok_wv = c.sW.v + 8 * 16
    with nc.Block(f"{tag}_p1", no_gpsimd_drain=True) as b:
        @b.gpsimd
        def _(ge):
            ge.wait_ge(c.sMM.h, kmm(31))
            src = w["l1b_wvt"].rearrange("(dc p) e -> p dc e", p=128)
            for dc in range(DCH):
                c.sW.inc(ge.dma_start(out=wkv_sb[:, dc, :], in_=src[:, dc, :]), 16)
            assert c.sW.v == tok_wv

        @b.tensor
        def _(te):
            _wait(te, fence)
            te.wait_ge(c.sW.h, tok_w1)
            te.wait_ge(c.sS.h, c.tok_xlnT[1])
            for g in range(32):
                ec, tt = g // 4, g % 4
                if g >= 2:
                    te.wait_ge(c.sV.h, kev(g - 2))
                ins = None
                for dc in range(DCH):
                    ins = te.matmul(ps_kv[g % 2][:], wkv_sb[:, dc, ec * 128:(ec + 1) * 128],
                                    c.xlnT[:, dc, tt * 512:(tt + 1) * 512],
                                    start=(dc == 0), stop=(dc == DCH - 1))
                assert c.sMM.inc(ins) == kmm(g)
            te.wait_ge(c.sW.h, tok_wv)
            for g in range(32):
                i, ee = g // 2, g % 2
                if g >= 2:
                    te.wait_ge(c.sS.h, vev(g - 2))
                for dc in range(DCH):
                    te.matmul(ps_kv[2 + g % 2][:], c.xlnT[:, dc, i * 128:(i + 1) * 128],
                              wkv_sb[:, dc, ee * 512:(ee + 1) * 512],
                              start=(dc == 0), stop=False)
                ins = te.matmul(ps_kv[2 + g % 2][:], c.ones[0:1, :],
                                bv_row[:, ee * 512:(ee + 1) * 512], start=False, stop=True)
                assert c.sMM.inc(ins) == vmm(g)
            for ec in range(DCH):
                for hf in range(2):
                    te.wait_ge(c.sS.h, sqe(ec, hf))
                    ins = None
                    for tt in range(2):
                        ins = te.matmul(ps_nk[hf * 2 + tt][:], hind[:, ec, :],
                                        sq1[:, tt * 512:(tt + 1) * 512],
                                        start=(ec == 0), stop=(ec == DCH - 1))
                    assert c.sMM.inc(ins) == nkmm(ec, hf)

        @b.vector
        def _(ve):
            ve.wait_ge(c.sX.h, tok_cols)
            ins = ve.tensor_mul(bqs_cols[:], bq_cols[:], ascale[:].broadcast_to([128, DCH]))
            assert c.sV.inc(ins) == sV0 + 1
            ins = ve.memset(V_sb[:, :, :, 64:65], 1.0)
            assert c.sV.inc(ins) == sV0 + 2
            for g in range(32):
                ec, tt = g // 4, g % 4
                ve.wait_ge(c.sMM.h, kmm(g))
                ins = ve.tensor_scalar_add(K_fm[:, ec, tt * 512:(tt + 1) * 512],
                                           ps_kv[g % 2][:], bk_cols[:, ec:ec + 1])
                assert c.sV.inc(ins) == kev(g)
            ve.wait_ge(c.sMM.h, nkmm(DCH - 1, 1))
            for tt in range(4):
                if tt == 0:
                    ve.reduce_max(nk2m[:], ps_nk[tt][:], AX.X)
                else:
                    ve.reduce_max(cq2[:], ps_nk[tt][:], AX.X)
                    ve.drain()
                    ve.tensor_max(nk2m[:], nk2m[:], cq2[:])
                ve.drain()
            ins = ve.tensor_copy(out=sk[:], in_=nk2m[:])
            c.sV.inc(ins)

        @b.scalar
        def _(se):
            for g in range(32):
                i, ee = g // 2, g % 2
                se.wait_ge(c.sMM.h, vmm(g))
                dst = V_sb[:, i, ee * 8:(ee + 1) * 8, 0:64]
                ins = se.copy(out=dst,
                              in_=ps_kv[2 + g % 2][:].rearrange("p (h d) -> p h d", h=8))
                assert c.sS.inc(ins) == vev(g)
            for ec in range(DCH):
                se.wait_ge(c.sV.h, kev(ec * 4 + 3))
                for hf in range(2):
                    if ec * 2 + hf >= 2:
                        pe, ph = divmod(ec * 2 + hf - 2, 2)
                        se.wait_ge(c.sMM.h, nkmm(pe, ph))
                    ins = se.activation(sq1[:], K_fm[:, ec, hf * 1024:(hf + 1) * 1024],
                                        AF.Square)
                    assert c.sS.inc(ins) == sqe(ec, hf)
    tok_p1v = c.sV.v

    with nc.Block(f"{tag}_sk", no_gpsimd_drain=True) as b:
        @b.scalar
        def _(se):
            se.wait_ge(c.sV.h, tok_p1v)
            se.drain()
            ins = se.activation(sk[:], sk[:], AF.Sqrt)
            c.sS.inc(ins)
    tok_sk = c.sS.v
    exp1.close()
    ex2.close()

    ex3 = ExitStack()
    sb3 = lambda name, shape, dt: ex3.enter_context(nc.sbuf_tensor(f"{tag}_{name}", shape, dt))
    mask_sb = sb3("masks", [128, 4, 512], BF16)
    wo_half = sb3("woh", [128, DCH, 512], BF16)
    Q_ch = sb3("qch", [128, DCH, 512], BF16)
    attn = sb3("attn", [128, DCH, 512], BF16)
    expT = [sb3(f"expT{i}", [128, 512], BF16) for i in range(2)]
    wq_rot = [sb3(f"wqr{i}", [128, 128], BF16) for i in range(3)]

    exp2 = ExitStack()
    ps_s = [c.psum(exp2, f"{tag}_pss{i}", [128, 512]) for i in range(2)]
    ps_a = [c.psum(exp2, f"{tag}_psa{i}", [65, 512]) for i in range(2)]
    ps_o = [c.psum(exp2, f"{tag}_pso{i}", [128, 512]) for i in range(2)]
    ps_n = c.psum(exp2, f"{tag}_psn", [16, 512])
    ps_cn = c.psum(exp2, f"{tag}_pscn", [128, 16])

    with nc.Block(f"{tag}_mload", no_gpsimd_drain=True) as b:
        @b.gpsimd
        def _(ge):
            ge.wait_ge(c.sMM.h, c.sMM.v)
            c.sW.inc(ge.dma_start(out=mask_sb[:], in_=w["l1b_masks"]), 16)
    tok_mask = c.sW.v

    NT_j = 64 + 8 + 1 + NH * 33 + 8
    NS_j = 8 + 1 + NH * 16
    NV_j = 8 + 1 + 1 + 1 + NH * 7 + 8
    NW_j = 64 * 16 + 2 * 8 * 16
    NX_j = 32
    sMM1, sS1, sV1, sW1, sX1 = c.sMM.v, c.sS.v, c.sV.v, c.sW.v, c.sX.v
    qmk = lambda j, ec, dc: sMM1 + j * NT_j + ec * 8 + dc + 1
    qmm = lambda j, ec: sMM1 + j * NT_j + ec * 8 + 8
    nmm = lambda j, ec: sMM1 + j * NT_j + 64 + ec + 1
    cbm = lambda j: sMM1 + j * NT_j + 73
    _spos = lambda kt: 1 if kt == 0 else 2 * kt
    _apos = lambda kt: 3 + 2 * kt if kt <= 13 else 17 + kt
    smm = lambda j, h, kt: sMM1 + j * NT_j + 73 + h * 33 + _spos(kt)
    amk = lambda j, h, kt: sMM1 + j * NT_j + 73 + h * 33 + _apos(kt)
    bmm = lambda j, h: sMM1 + j * NT_j + 73 + h * 33 + 33
    omm = lambda j, g: sMM1 + j * NT_j + 73 + NH * 33 + g + 1
    sqq = lambda j, ec: sS1 + j * NS_j + ec + 1
    cqs = lambda j: sS1 + j * NS_j + 9
    eev = lambda j, h, kt: sS1 + j * NS_j + 9 + h * 16 + kt + 1
    qev = lambda j, ec: sV1 + j * NV_j + ec + 1
    nqv = lambda j: sV1 + j * NV_j + 9
    chv = lambda j: sV1 + j * NV_j + 10
    cnv = lambda j: sV1 + j * NV_j + 11
    mskv = lambda j, h, oi: sV1 + j * NV_j + 11 + h * 7 + oi + 1
    rcp = lambda j, h: sV1 + j * NV_j + 11 + h * 7 + 5
    rbc = lambda j, h: sV1 + j * NV_j + 11 + h * 7 + 6
    atv = lambda j, h: sV1 + j * NV_j + 11 + h * 7 + 7
    oev = lambda j, g: sV1 + j * NV_j + 11 + NH * 7 + g + 1
    wqt = lambda j, blk: sW1 + j * NW_j + (blk + 1) * 16
    wot = lambda j, dd: sW1 + j * NW_j + 64 * 16 + (dd + 1) * 8 * 16
    xrow = lambda j, n: sX1 + j * NX_j + n * 16

    with nc.Block(f"{tag}_qloop", no_gpsimd_drain=True) as b:
        @b.gpsimd
        def _(ge):
            for j in range(4):
                src = w["l1b_wqt"].rearrange("(dc p) e -> p dc e", p=128)
                for blk in range(64):
                    ec, dc = divmod(blk, 8)
                    if blk >= 3:
                        pe_, pd_ = divmod(blk - 3, 8)
                        ge.wait_ge(c.sMM.h, qmk(j, pe_, pd_))
                    elif j > 0:
                        pe_, pd_ = divmod(blk + 61, 8)
                        ge.wait_ge(c.sMM.h, qmk(j - 1, pe_, pd_))
                    ins = ge.dma_start(out=wq_rot[blk % 3][:],
                                       in_=src[:, dc, ec * 128:(ec + 1) * 128])
                    assert c.sW.inc(ins, 16) == wqt(j, blk)
                srco = w["l1b_wot"].rearrange("(dc p) e -> p dc e", p=128)
                for dd in range(2):
                    if j > 0 or dd > 0:
                        pj, pdd = (j, 0) if dd == 1 else (j - 1, 1)
                        ge.wait_ge(c.sMM.h, omm(pj, pdd * 4 + 3))
                    for dc in range(DCH):
                        c.sW.inc(ge.dma_start(out=wo_half[:, dc, :],
                                              in_=srco[:, dc, dd * 512:(dd + 1) * 512]), 16)
                    assert c.sW.v == wot(j, dd)

        @b.sync
        def _(sy):
            for j in range(4):
                sy.wait_ge(c.sV.h, chv(j))
                ins = sy.dma_start(out=dram["l1b_scr"][:], in_=ch_bf[:])
                assert c.sX.inc(ins, 16) == xrow(j, 1)
                sy.wait_ge(c.sX.h, xrow(j, 1))
                if j > 0:
                    sy.wait_ge(c.sMM.h, cbm(j - 1))
                ins = sy.dma_start(out=chrow[:], in_=dram["l1b_scr"].rearrange("s one -> one s"))
                assert c.sX.inc(ins, 16) == xrow(j, 2)

        @b.tensor
        def _(te):
            for j in range(4):
                if j == 0:
                    te.wait_ge(c.sS.h, tok_sk)
                    te.wait_ge(c.sW.h, tok_mask)
                for ec in range(DCH):
                    if ec >= 2:
                        te.wait_ge(c.sV.h, qev(j, ec - 2))
                    elif j > 0:
                        te.wait_ge(c.sS.h, eev(j - 1, 15, 15))
                    for dc in range(DCH):
                        blk = ec * 8 + dc
                        te.wait_ge(c.sW.h, wqt(j, blk))
                        ins = te.matmul(ps_s[ec % 2][:], wq_rot[blk % 3][:],
                                        c.xlnT[:, dc, j * 512:(j + 1) * 512],
                                        start=(dc == 0), stop=(dc == DCH - 1))
                        assert c.sMM.inc(ins) == qmk(j, ec, dc)
                for ec in range(DCH):
                    te.wait_ge(c.sS.h, sqq(j, ec))
                    if ec == 0 and j > 0:
                        te.wait_ge(c.sV.h, nqv(j - 1))
                    ins = te.matmul(ps_n[:], hind[:, ec, :],
                                    sq1[:, (ec % 2) * 512:(ec % 2) * 512 + 512],
                                    start=(ec == 0), stop=(ec == DCH - 1))
                    assert c.sMM.inc(ins) == nmm(j, ec)
                te.wait_ge(c.sX.h, xrow(j, 2))
                if j > 0:
                    te.wait_ge(c.sV.h, cnv(j - 1))
                ins = te.matmul(ps_cn[:], c.ones[0:1, :], chrow[:], start=True, stop=True)
                assert c.sMM.inc(ins) == cbm(j)
                for h in range(NH):
                    base = (h % 2) * 64
                    hc = h // 2

                    def _scores(kt, h=h, base=base, hc=hc, j=j):
                        if kt >= 2:
                            te.wait_ge(c.sS.h, eev(j, h, kt - 2))
                        elif h > 0:
                            te.wait_ge(c.sS.h, eev(j, h - 1, 14 + kt))
                        else:
                            te.wait_ge(c.sV.h, qev(j, 7))
                        ins = te.matmul(ps_s[kt % 2][:],
                                        K_fm[base:base + 64, hc, kt * 128:(kt + 1) * 128],
                                        Q_ch[base:base + 64, hc, :],
                                        start=True, stop=True)
                        assert c.sMM.inc(ins) == smm(j, h, kt)

                    def _att(kt, h=h, j=j):
                        if kt == 0:
                            if h >= 2:
                                te.wait_ge(c.sV.h, atv(j, h - 2))
                            elif j > 0:
                                te.wait_ge(c.sV.h, atv(j - 1, 14 + h))
                        oi = kt - 4 * j
                        if 0 <= oi < 4:
                            te.wait_ge(c.sV.h, mskv(j, h, oi))
                        else:
                            te.wait_ge(c.sS.h, eev(j, h, kt))
                        ins = te.matmul(ps_a[h % 2][:], V_sb[:, kt, h, :], expT[kt % 2][:],
                                        start=(kt == 0), stop=(kt == TCH - 1))
                        assert c.sMM.inc(ins) == amk(j, h, kt)

                    _scores(0)
                    _scores(1)
                    for kt in range(14):
                        _att(kt)
                        _scores(kt + 2)
                    _att(14)
                    _att(15)
                    te.wait_ge(c.sV.h, rcp(j, h))
                    if h == 0 and j > 0:
                        te.wait_ge(c.sV.h, oev(j - 1, 7))
                    ins = te.matmul(ps_o[0][0:64, :], c.ones[0:1, 0:64], rd_row[:],
                                    start=True, stop=True)
                    assert c.sMM.inc(ins) == bmm(j, h)
                for g in range(8):
                    dd, tcol = g // 4, g % 4
                    te.wait_ge(c.sW.h, wot(j, dd))
                    if g >= 2:
                        te.wait_ge(c.sV.h, oev(j, g - 2))
                    else:
                        te.wait_ge(c.sV.h, atv(j, 15))
                    for dc in range(DCH):
                        te.matmul(ps_o[g % 2][:], attn[:, dc, tcol * 128:(tcol + 1) * 128],
                                  wo_half[:, dc, :],
                                  start=(dc == 0), stop=False)
                    ins = te.matmul(ps_o[g % 2][:], c.ones[0:1, :],
                                    bo_row[:, dd * 512:(dd + 1) * 512], start=False, stop=True)
                    assert c.sMM.inc(ins) == omm(j, g)

        @b.scalar
        def _(se):
            for j in range(4):
                for ec in range(DCH):
                    se.wait_ge(c.sV.h, qev(j, ec))
                    if ec >= 2:
                        se.wait_ge(c.sMM.h, nmm(j, ec - 2))
                    elif j > 0:
                        se.wait_ge(c.sMM.h, nmm(j - 1, ec + 6))
                    else:
                        se.wait_ge(c.sMM.h, nkmm(DCH - 1, 1))
                    ins = se.activation(sq1[:, (ec % 2) * 512:(ec % 2) * 512 + 512],
                                        Q_ch[:, ec, :], AF.Square)
                    assert c.sS.inc(ins) == sqq(j, ec)
                se.wait_ge(c.sV.h, nqv(j))
                se.drain()
                ins = se.activation(cq[:], cq2[:], AF.Sqrt)
                assert c.sS.inc(ins) == cqs(j)
                for h in range(NH):
                    for kt in range(TCH):
                        se.wait_ge(c.sMM.h, smm(j, h, kt))
                        if h == 0 and kt < 2:
                            se.wait_ge(c.sV.h, cnv(j))
                        if kt >= 2:
                            se.wait_ge(c.sMM.h, amk(j, h, kt - 2))
                        elif h > 0:
                            se.wait_ge(c.sMM.h, amk(j, h - 1, 14 + kt))
                        elif j > 0:
                            se.wait_ge(c.sMM.h, amk(j - 1, 15, 14 + kt))
                        ins = se.activation(expT[kt % 2][:], ps_s[kt % 2][:], AF.Exp,
                                            bias=cneg[:, h:h + 1])
                        assert c.sS.inc(ins) == eev(j, h, kt)

        @b.vector
        def _(ve):
            for j in range(4):
                for ec in range(DCH):
                    ve.wait_ge(c.sMM.h, qmm(j, ec))
                    if j > 0:
                        ve.wait_ge(c.sMM.h, smm(j - 1, 2 * ec + 1, 15))
                    ins = ve.tensor_scalar(Q_ch[:, ec, :], ps_s[ec % 2][:],
                                           ascale[:], bqs_cols[:, ec:ec + 1],
                                           ALU.mult, ALU.add)
                    assert c.sV.inc(ins) == qev(j, ec)
                ve.wait_ge(c.sMM.h, nmm(j, 7))
                ins = ve.reduce_max(cq2[:], ps_n[:], AX.X)
                assert c.sV.inc(ins) == nqv(j)
                ve.drain()
                ve.wait_ge(c.sS.h, cqs(j))
                if j > 0:
                    ve.wait_ge(c.sX.h, xrow(j - 1, 1))
                ins = ve.tensor_mul(ch_bf[:], cq[:], sk[:])
                assert c.sV.inc(ins) == chv(j)
                ve.wait_ge(c.sMM.h, cbm(j))
                if j > 0:
                    ve.wait_ge(c.sS.h, eev(j - 1, 15, 15))
                ins = ve.tensor_scalar_mul(cneg[:], ps_cn[:], -1.0)
                assert c.sV.inc(ins) == cnv(j)
                for h in range(NH):
                    for oi in range(4):
                        kt = 4 * j + oi
                        ve.wait_ge(c.sS.h, eev(j, h, kt))
                        ins = ve.tensor_mul(expT[kt % 2][:], expT[kt % 2][:],
                                            mask_sb[:, oi, :])
                        assert c.sV.inc(ins) == mskv(j, h, oi)
                    ve.wait_ge(c.sMM.h, amk(j, h, 15))
                    if h > 0:
                        ve.wait_ge(c.sMM.h, bmm(j, h - 1))
                    elif j > 0:
                        ve.wait_ge(c.sMM.h, bmm(j - 1, 15))
                    with nc.allow_low_precision("denom recip feeds a bf16 matmul operand"):
                        ins = ve.reciprocal(rd_row[:], ps_a[h % 2][64:65, :])
                    assert c.sV.inc(ins) == rcp(j, h)
                    ve.drain()
                    ve.wait_ge(c.sMM.h, bmm(j, h))
                    ins = ve.tensor_copy(out=rb[:], in_=ps_o[0][0:64, :])
                    assert c.sV.inc(ins) == rbc(j, h)
                    if h == 0 and j > 0:
                        ve.wait_ge(c.sMM.h, omm(j - 1, 7))
                    ins = ve.tensor_tensor(attn[(h % 2) * 64:(h % 2) * 64 + 64, h // 2, :],
                                           ps_a[h % 2][0:64, :], rb[:], ALU.mult)
                    assert c.sV.inc(ins) == atv(j, h)
                for g in range(8):
                    dd, tcol = g // 4, g % 4
                    ve.wait_ge(c.sMM.h, omm(j, g))
                    ins = ve.tensor_add(c.xres[:, j * 4 + tcol, dd * 512:(dd + 1) * 512],
                                        c.xres[:, j * 4 + tcol, dd * 512:(dd + 1) * 512],
                                        ps_o[g % 2][:])
                    assert c.sV.inc(ins) == oev(j, g)

    exp2.close()
    ex3.close()
    c.store_chunks = None
    c.stream_deps = [(c.sV, c.sV.v)]
    c.fence_deps = c.barrier_tokens()
    ex.close()


SUB_WEIGHTS = {
    0: ["l0a_wq", "l0a_wk", "l0a_wv"],
    1: ["l0f_gwt", "l0f_eb", "l0f_w1t", "l0f_w2t"],
    3: ["l0f_gwt", "l0f_eb", "l0f_w1t", "l0f_w2t"],
    2: ["l0b_wqkv", "l0b_bqkv", "l0b_wout", "l0b_bout", "l0b_bq_cols"],
    4: ["l1a_wqkvt", "l1a_bqkv", "l1a_wprojt", "l1a_bproj", "l1a_scale_col"],
    5: ["l1f_wrt", "l1f_br", "l1f_b1_cols", "l1f_b2", "l1f_w1", "l1f_w2"],
    7: ["l1f_wrt", "l1f_br", "l1f_b1_cols", "l1f_b2", "l1f_w1", "l1f_w2"],
    6: ["l1b_wqt", "l1b_wkt", "l1b_wvt", "l1b_wot", "l1b_bq_cols", "l1b_bk_cols",
        "l1b_bv", "l1b_bo", "l1b_ascale_col", "l1b_masks", "l1b_hind"],
}
SUB_CONSTS = {
    0: ["a_cat"],
}
EMITTERS = {}


def _emit_sub0(c, w, dram):
    emit_l0a(c, w, dram)


EMITTERS[0] = _emit_sub0
EMITTERS[1] = lambda c, w, dram: emit_l0ffn(c, w, dram, "s1")
EMITTERS[3] = lambda c, w, dram: emit_l0ffn(c, w, dram, "s3")
EMITTERS[2] = lambda c, w, dram: emit_l0b(c, w, dram, "s2")
EMITTERS[4] = lambda c, w, dram: emit_l1a(c, w, dram, "s4")
EMITTERS[5] = lambda c, w, dram: emit_l1ffn(c, w, dram, "s5")
EMITTERS[7] = lambda c, w, dram: emit_l1ffn(c, w, dram, "s7")
EMITTERS[6] = lambda c, w, dram: emit_l1b(c, w, dram, "s6")

_WSHAPES = {
    "l0a_wq": ([D, D], F32), "l0a_wk": ([D, D], F32), "l0a_wv": ([D, D], F32),
    "l0f_gwt": ([128, DCH, 12], F32), "l0f_eb": ([1, 12], F32),
    "l0f_w1t": ([8, D, 2048], F32), "l0f_w2t": ([8, 2048, D], F32),
    "l0b_wqkv": ([D, 3 * D], F32), "l0b_bqkv": ([1, 3 * D], F32),
    "l0b_wout": ([D, D], F32), "l0b_bout": ([1, D], F32),
    "l0b_bq_cols": ([128, DCH], F32),
    "l1a_wqkvt": ([D, 3 * D], F32), "l1a_bqkv": ([1, 3 * D], F32),
    "l1a_wprojt": ([D, D], F32), "l1a_bproj": ([1, D], F32),
    "l1a_scale_col": ([128, 1], F32),
    "l1f_wrt": ([128, DCH, 8], F32), "l1f_br": ([1, 8], F32),
    "l1f_b1_cols": ([128, 8 * 32], F32), "l1f_b2": ([8, D], F32),
    "l1f_w1": ([8, D, 4096], F32), "l1f_w2": ([8, 4096, D], F32),
    "l1b_wqt": ([D, D], F32), "l1b_wkt": ([D, D], F32),
    "l1b_wvt": ([D, D], F32), "l1b_wot": ([D, D], F32),
    "l1b_bq_cols": ([128, DCH], F32), "l1b_bk_cols": ([128, DCH], F32),
    "l1b_bv": ([1, D], F32), "l1b_bo": ([1, D], F32),
    "l1b_ascale_col": ([128, 1], F32),
    "l1b_masks": ([128, 4, 512], BF16), "l1b_hind": ([128, DCH, 16], BF16),
}
_CSHAPES = {
    "ident": ([128, 128], BF16),
    "a_cat": ([T, T], BF16),
    "triu": ([128, 128], BF16),
}


def _build_launch(subs):
    nc = bass.Bass("TRN2", target_bir_lowering=False)
    x_in = nc.dram_tensor("x_in", [T, D], F32, kind="ExternalInput").ap()
    out = nc.dram_tensor("out", [T, D], F32, kind="ExternalOutput").ap()

    w = {}
    for s in subs:
        for name in SUB_WEIGHTS[s]:
            if name not in w:
                shape, dt = _WSHAPES[name]
                w[name] = nc.dram_tensor("d_" + name, shape, dt, kind="ExternalInput").ap()
    need_consts = {"ident", "triu"}
    for s in subs:
        need_consts.update(SUB_CONSTS.get(s, []))
    for name in sorted(need_consts):
        shape, dt = _CSHAPES[name]
        w[name] = nc.dram_tensor("c_" + name, shape, dt, kind="ExternalInput").ap()

    dram = {"a_cat": w.get("a_cat")}
    if 6 in subs:
        dram["l1b_scr"] = nc.dram_tensor("l1b_scr", [16, 1], BF16).ap()
    if 0 in subs:
        dram["cc_in"] = nc.dram_tensor("cc_in", [T, D], F32).ap()
        dram["cc_out"] = nc.dram_tensor("cc_out", [T, D], F32, addr_space="Shared").ap()

    with ExitStack() as ex:
        c = Ctx(nc, ex)
        with nc.Block("init", no_gpsimd_drain=True) as b:
            @b.sync
            def _(sy):
                c.sX.inc(sy.dma_start(out=c.ident[:], in_=w["ident"]), 16)
                c.sX.inc(sy.dma_start(out=c.triu[:], in_=w["triu"]), 16)
                src = x_in.rearrange("(i p) d -> p i d", p=128)
                for i0 in range(0, TCH, 4):
                    ins = sy.dma_start(out=c.xres[:, i0:i0 + 4, :], in_=src[:, i0:i0 + 4, :])
                    c.sX.inc(ins, 16)

            @b.gpsimd
            def _(ge):
                ins = ge.memset(c.ones[:], 1.0)
                c.sG.inc(ins)
        tok_init = c.sX.v
        c.stream_deps = [(c.sX, tok_init)]
        c.stream_chunk_deps = [(c.sX, 32 + 16 * (i // 4 + 1)) for i in range(TCH)]

        for s in subs:
            EMITTERS[s](c, w, dram)

        sc = getattr(c, "store_chunks", None)
        tok_store = c.sX.v + (16 * len(sc) if sc else 16 * 4)
        with nc.Block("fin") as b:
            @b.sync
            def _(sy):
                dst = out.rearrange("(i p) d -> p i d", p=128)
                if sc:
                    for (a, bnd, dep) in sc:
                        _wait(sy, [dep])
                        ins = sy.dma_start(out=dst[:, a:bnd, :], in_=c.xres[:, a:bnd, :])
                        c.sX.inc(ins, 16)
                else:
                    _wait(sy, c.stream_deps)
                    for i0 in range(0, TCH, 4):
                        ins = sy.dma_start(out=dst[:, i0:i0 + 4, :], in_=c.xres[:, i0:i0 + 4, :])
                        c.sX.inc(ins, 16)
                assert c.sX.v == tok_store

            @b.gpsimd
            def _(ge):
                ge.wait_ge(c.sX.h, tok_store)
    return nc


_NC_CACHE = {}


def _get_launch(subs):
    key = tuple(subs)
    if key not in _NC_CACHE:
        _NC_CACHE[key] = _build_launch(list(subs))
    return _NC_CACHE[key]


def kernel(**inputs):
    consts = _host_consts()
    prep = _prep_weights(inputs)
    x = np.ascontiguousarray(np.asarray(inputs["L0_attn_a_x"], np.float32))
    stream = [np.ascontiguousarray(x[b]) for b in range(NCORE)]

    for subs in GROUPS:
        nc = _get_launch(subs)
        names = set()
        for s in subs:
            names.update(SUB_WEIGHTS[s])
        cnames = {"ident", "triu"}
        for s in subs:
            cnames.update(SUB_CONSTS.get(s, []))
        in_maps = []
        for b in range(NCORE):
            m = {"x_in": stream[b]}
            for n in names:
                m["d_" + n] = prep[n]
            for n in cnames:
                m["c_" + n] = consts[n]
            in_maps.append(m)
        res = run_bass_kernel_spmd(nc, in_maps, core_ids=list(range(NCORE)))
        stream = [np.asarray(res.results[b]["out"]) for b in range(NCORE)]

    return np.stack(stream).astype(np.float32)
